# version 13
# baseline (speedup 1.0000x reference)
import contextlib
import numpy as np

import concourse.bass as bass
import concourse.mybir as mybir
from concourse.bass_utils import run_bass_kernel_spmd

F32 = mybir.dt.float32
BF = mybir.dt.bfloat16
AF = mybir.ActivationFunctionType
ALU = mybir.AluOpType
AX = mybir.AxisListType

NCORES = 8
T = 2048
D = 1024
KC = D // 128
NT = T // 128


def ap_off(t_ap, extra_off, ap):
    return bass.AP(t_ap.tensor, t_ap.offset + extra_off, ap)


class Sem:
    def __init__(self, h):
        self.h = h
        self.n = 0

    def inc(self, inst, k=1):
        inst.then_inc(self.h, k)
        self.n += k
        return self.n


class Ctx:
    def __init__(self, nc, es):
        self.nc = nc
        self.es = es
        E = es.enter_context
        self.A = E(nc.sbuf_tensor("arena_a", [128, KC * T], BF))
        self.B = E(nc.sbuf_tensor("arena_b", [128, KC * T], F32))
        self.C = E(nc.sbuf_tensor("arena_c", [128, 8192], F32))
        self.W = E(nc.sbuf_tensor("arena_w", [128, 8192], F32))
        self.Wb = E(nc.sbuf_tensor("arena_wb", [128, 8192], BF))
        self.XS = E(nc.sbuf_tensor("arena_xs", [128, 4096], F32))
        self.S = E(nc.sbuf_tensor("arena_s", [128, 2048], F32))
        self.ident = E(nc.sbuf_tensor("ident", [128, 128], F32))
        self.ones_row = E(nc.sbuf_tensor("ones_row", [1, 128], F32))
        self.R = E(nc.sbuf_tensor("row_scratch", [1, 1152], F32))
        self.ceps = E(nc.sbuf_tensor("c_eps6", [128, 1], F32))
        self.pb = [E(nc.psum_tensor(f"pb{i}", [128, 512], F32)) for i in range(8)]
        self._sems = {}

    def sem(self, role):
        DMA_ROLES = {"in_ld", "out_ldx", "out_ldw", "out_ldw0", "out_ldw1",
                     "ffn1_ld", "fp_ld", "fx_ldw", "fx_ldw0", "fx_ldw1",
                     "fx_ldw2", "fx_ldw3", "wc_l2",
                     "ab_ld", "ab_ld0", "ab_ld1", "aa_pld", "aa_ldw",
                     "aa_vld", "da_pld", "da_nwl", "da_ldw", "rk_ld"}
        if role in ("f_dve", "f_act", "out_st"):
            key = role
        else:
            fam = role.split("_")[0]
            if role in DMA_ROLES:
                fam += "D"
            if not hasattr(self, "_fam_slots"):
                self._fam_slots = {}
                self._fam_counts = {}
            if role not in self._fam_slots:
                idx = self._fam_counts.get(fam, 0)
                self._fam_counts[fam] = idx + 1
                pfx = "dpool" if fam.endswith("D") else "pool"
                self._fam_slots[role] = f"{pfx}{idx}"
            key = self._fam_slots[role]
        if key not in self._sems:
            self._sems[key] = Sem(self.es.enter_context(self.nc.semaphore(key)))
        s = self._sems[key]
        return s

    def fence(self, eng, role, inst=None):
        s = self.sem(role)
        if inst is None:
            inst = eng.engine_nop() if hasattr(eng, "engine_nop") else eng.nop()
        s.inc(inst)
        eng.wait_ge(s.h, s.n)

    def a_fm(self, k, t0=0, n=T):
        return self.A[:, k * T + t0: k * T + t0 + n]

    def b_fm(self, k, t0=0, n=T):
        return self.B[:, k * T + t0: k * T + t0 + n]

    def a_tok(self, i):
        return self.A[:, i * 1024:(i + 1) * 1024]

    def b_tok(self, i):
        return self.B[:, i * 1024:(i + 1) * 1024]


def emit_prelude(ctx):
    nc = ctx.nc
    with nc.Block() as blk:
        @blk.gpsimd
        def _(g):
            g.iota(ctx.C[:, 0:128], [[-1, 128]], channel_multiplier=1,
                   allow_small_or_imprecise_dtypes=True)
            g.memset(ctx.S[:, 0:1], 1.0)
            g.memset(ctx.S[:, 1:2], 0.0)
            g.memset(ctx.ones_row[:, :], 1.0)
            g.memset(ctx.ceps[:, :], 1e-6)
    with nc.Block() as blk:
        @blk.vector
        def _(v):
            v.tensor_scalar(ctx.ident[:, :], ctx.C[:, 0:128], 0.0, None,
                            ALU.is_equal)



def emit_in_ln_tp(ctx, x_in, b, ln=True):
    nc = ctx.nc
    ld = ctx.sem("in_ld")
    with nc.Block() as blk:
        @blk.sync
        def _(sync):
            dst = bass.AP(ctx.B, 0, [[KC * T, 128], [1024, NT], [1, 1024]])
            src = ap_off(x_in, b * T * D,
                         [[1024, 128], [128 * 1024, NT], [1, 1024]])
            ld.inc(sync.dma_start(out=dst, in_=src), 16)
            sync.wait_ge(ld.h, ld.n)

    if ln:
        sq = ctx.sem("ln_sq")
        var = ctx.sem("ln_var")
        rst = ctx.sem("ln_rst")
        with nc.Block() as blk:
            @blk.scalar
            def _(s):
                for i in range(NT):
                    sq.inc(s.activation(ctx.XS[:, 0:1024], ctx.b_tok(i), AF.Square,
                                        accum_out=ctx.S[:, 32 + i:33 + i]))
                s.wait_ge(var.h, var.n + 1)
                inst = s.activation(ctx.S[:, 80:80 + NT], ctx.S[:, 64:64 + NT],
                                    AF.Ln)
                ctx.fence(s, "f_act", inst)
                rst.inc(s.activation(ctx.S[:, 96:96 + NT], ctx.S[:, 80:80 + NT],
                                     AF.Exp, scale=-0.5))

            @blk.vector
            def _(v):
                sq0 = sq.n - NT
                for i in range(NT):
                    inst = v.tensor_reduce(ctx.S[:, 48 + i:49 + i], ctx.b_tok(i),
                                           axis=AX.X, op=ALU.add)
                v.wait_ge(sq.h, sq0 + NT)
                ctx.fence(v, "f_dve", inst)
                inst = v.tensor_scalar(ctx.S[:, 112:112 + NT], ctx.S[:, 48:48 + NT],
                                       1.0 / D, None, ALU.mult)
                ctx.fence(v, "f_dve", inst)
                v.tensor_scalar(ctx.S[:, 192:192 + NT], ctx.S[:, 32:32 + NT],
                                1.0 / D, 1e-5, ALU.mult, ALU.add)
                inst = v.tensor_tensor(ctx.S[:, 208:208 + NT], ctx.S[:, 112:112 + NT],
                                       ctx.S[:, 112:112 + NT], ALU.mult)
                ctx.fence(v, "f_dve", inst)
                inst = v.tensor_tensor(ctx.S[:, 64:64 + NT], ctx.S[:, 192:192 + NT],
                                       ctx.S[:, 208:208 + NT], ALU.subtract)
                var.inc(inst)
                v.wait_ge(rst.h, rst.n)
                for i in range(NT):
                    v.tensor_scalar(ctx.b_tok(i), ctx.b_tok(i),
                                    ctx.S[:, 112 + i:113 + i],
                                    ctx.S[:, 96 + i:97 + i],
                                    ALU.subtract, ALU.mult)

    tp = ctx.sem("tp")
    ev = ctx.sem("tp_ev")
    with nc.Block() as blk:
        @blk.tensor
        def _(t):
            ev0, tp0 = ev.n, tp.n
            g = 0
            for i in range(NT):
                for half in range(2):
                    if g >= 4:
                        t.wait_ge(ev.h, ev0 + g - 3)
                    bank = ctx.pb[g % 4]
                    for q in range(4):
                        k = half * 4 + q
                        inst = t.transpose(bank[:, q * 128:(q + 1) * 128],
                                           ctx.b_tok(i)[:, k * 128:(k + 1) * 128],
                                           ctx.ident[:, :])
                        if q == 3:
                            tp.inc(inst)
                    g += 1

        @blk.vector
        def _(v):
            ev0, tp0 = ev.n, tp.n - 2 * NT
            g = 0
            for i in range(NT):
                for half in range(2):
                    v.wait_ge(tp.h, tp0 + g + 1)
                    bank = ctx.pb[g % 4]
                    for q in range(4):
                        k = half * 4 + q
                        inst = v.tensor_copy(ctx.a_fm(k, i * 128, 128),
                                             bank[:, q * 128:(q + 1) * 128])
                        if q == 3:
                            ev.inc(inst)
                    g += 1



def emit_out_phase(ctx, x_in, x_out, b, proj=None):
    nc = ctx.nc
    ldx = ctx.sem("out_ldx")
    ldw = ctx.sem("out_ldw")
    mm = ctx.sem("out_mm")
    evs = ctx.sem("out_ev")
    tps = ctx.sem("out_tp")
    add = ctx.sem("out_add")
    stw = ctx.sem("out_st")
    wtp = ctx.sem("out_wtp")
    wte = ctx.sem("out_wte")

    has_bias = proj is not None and proj.get("bias") is not None
    kchunks = proj["kchunks"] if proj else 0
    kpart = proj.get("kpart", 128) if proj else 128
    wtrans = proj.get("transpose", False) if proj else False

    with nc.Block() as blk:
        @blk.sync
        def _(sync):
            if has_bias:
                bsrc = proj["bias"].rearrange("(a c) -> a c", c=128)
                ldx.inc(sync.dma_start(out=ctx.S[0:8, 128:256], in_=bsrc), 16)
            if proj and not wtrans:
                for of in range(8):
                    wdst = ctx.W[0:kpart,
                                 of * kchunks * 128:(of + 1) * kchunks * 128]
                    ldw.inc(sync.dma_start(out=wdst, in_=proj["w_src"](of)), 16)
            if proj and wtrans:
                wtp0 = wtp.n
                ngrp = (kchunks + 3) // 4
                for of in range(8):
                    if of >= 2:
                        sync.wait_ge(wtp.h, wtp0 + (of - 1) * ngrp)
                    sl = ctx.sem(f"out_ldw{of % 2}")
                    sl.inc(sync.dma_start(
                        out=ctx.C[:, (of % 2) * 1024:(of % 2) * 1024 + 1024],
                        in_=proj["w_rows"](of)), 16)
            sync.wait_ge(ldx.h, ldx.n)

        if has_bias:
            @blk.tensor
            def _(t):
                t.wait_ge(ldx.h, ldx.n)
                tps.inc(t.transpose(ctx.pb[6][0:128, 0:8], ctx.S[0:8, 128:256],
                                    ctx.ident[0:8, 0:8]))

            @blk.vector
            def _(v):
                v.wait_ge(tps.h, tps.n)
                v.tensor_copy(ctx.S[:, 256:264], ctx.pb[6][:, 0:8])

        if proj and wtrans:
            ngrp = (kchunks + 3) // 4

            @blk.tensor
            def _(t):
                slb = [ctx.sem("out_ldw0").n - 4 * 16, ctx.sem("out_ldw1").n - 4 * 16]
                wte0 = wte.n
                gcnt = 0
                for of in range(8):
                    sl = ctx.sem(f"out_ldw{of % 2}")
                    t.wait_ge(sl.h, slb[of % 2] + 16 * (of // 2 + 1))
                    for g in range(ngrp):
                        if gcnt >= 2:
                            t.wait_ge(wte.h, wte0 + gcnt - 1)
                        bank = ctx.pb[4 + gcnt % 2]
                        for k in range(g * 4, min(g * 4 + 4, kchunks)):
                            inst = t.transpose(
                                bank[:, (k % 4) * 128:(k % 4) * 128 + 128],
                                ctx.C[:, (of % 2) * 1024 + k * 128:
                                      (of % 2) * 1024 + (k + 1) * 128],
                                ctx.ident[:, :])
                        wtp.inc(inst)
                        gcnt += 1

            @blk.vector
            def _(v):
                wtp0 = wtp.n - 8 * ngrp
                gcnt = 0
                for of in range(8):
                    for g in range(ngrp):
                        v.wait_ge(wtp.h, wtp0 + gcnt + 1)
                        bank = ctx.pb[4 + gcnt % 2]
                        for k in range(g * 4, min(g * 4 + 4, kchunks)):
                            inst = v.tensor_copy(
                                ctx.W[:, of * kchunks * 128 + k * 128:
                                      of * kchunks * 128 + (k + 1) * 128],
                                bank[:, (k % 4) * 128:(k % 4) * 128 + 128])
                        wte.inc(inst)
                        gcnt += 1

    ldw_needed = (ldw.n if proj and not wtrans else 0)
    ldx_base = ldx.n
    add_base = add.n
    with nc.Block() as blk:
        @blk.sync
        def _(sync):
            for n in range(4):
                if n >= 1:
                    sync.wait_ge(add.h, add_base + n * 8)
                dst = bass.AP(ctx.XS, 0, [[4096, 128], [1024, 4], [1, 1024]])
                src = ap_off(x_in, (b * T + n * 512) * D,
                             [[1024, 128], [128 * 1024, 4], [1, 1024]])
                ldx.inc(sync.dma_start(out=dst, in_=src), 16)

        @blk.tensor
        def _(t):
            mm0, ev0, add0 = mm.n, evs.n, add.n
            if proj and not wtrans:
                t.wait_ge(ldw.h, ldw_needed)
            for i in range(32):
                n, of = divmod(i, 8)
                if proj:
                    if i >= 2:
                        t.wait_ge(evs.h, ev0 + i - 1)
                    wbase = of * kchunks * 128
                    for k in range(kchunks):
                        inst = t.matmul(ctx.pb[i % 2][:, :],
                                        ctx.W[0:kpart,
                                              wbase + k * 128:wbase + (k + 1) * 128],
                                        proj["src"](k, n * 512, 512),
                                        start=(k == 0), stop=(k == kchunks - 1))
                    mm.inc(inst)
                if proj:
                    t.wait_ge(evs.h, ev0 + i + 1)
                    srct = ctx.C[:, (i % 4) * 512:(i % 4) * 512 + 512]
                else:
                    srct = ctx.b_fm(of, n * 512, 512)
                if i >= 2:
                    t.wait_ge(add.h, add0 + i - 1)
                bank = ctx.pb[2 + i % 2]
                for q in range(4):
                    inst = t.transpose(bank[:, q * 128:(q + 1) * 128],
                                       srct[:, q * 128:(q + 1) * 128],
                                       ctx.ident[:, :])
                tps.inc(inst)

        @blk.vector
        def _(v):
            mm0 = mm.n - (32 if proj else 0)
            tp0 = tps.n - 32
            st0 = stw.n
            for i in range(32):
                n, of = divmod(i, 8)
                if proj:
                    v.wait_ge(mm.h, mm0 + i + 1)
                    dstc = ctx.C[:, (i % 4) * 512:(i % 4) * 512 + 512]
                    if has_bias:
                        inst = v.tensor_scalar(dstc, ctx.pb[i % 2][:, :],
                                               ctx.S[:, 256 + of:257 + of],
                                               None, ALU.add)
                    else:
                        inst = v.tensor_copy(dstc, ctx.pb[i % 2][:, :])
                    evs.inc(inst)
                v.wait_ge(tps.h, tp0 + i + 1)
                if of == 0 and n >= 1:
                    v.wait_ge(stw.h, st0 + 16 * n)
                if of == 0:
                    v.wait_ge(ldx.h, ldx_base + 16 * (n + 1))
                bank = ctx.pb[2 + i % 2]
                for q in range(4):
                    inst = v.tensor_tensor(
                        ctx.C[:, 2048 + q * 1024 + of * 128:
                              2048 + q * 1024 + (of + 1) * 128],
                        bank[:, q * 128:(q + 1) * 128],
                        ctx.XS[:, q * 1024 + of * 128:q * 1024 + (of + 1) * 128],
                        ALU.add)
                add.inc(inst)

        @blk.gpsimd
        def _(g):
            add0 = add.n - 32
            for n in range(4):
                g.wait_ge(add.h, add0 + (n + 1) * 8)
                src = bass.AP(ctx.C, 2048, [[8192, 128], [1024, 4], [1, 1024]])
                dst = ap_off(x_out, (b * T + n * 512) * D,
                             [[1024, 128], [128 * 1024, 4], [1, 1024]])
                stw.inc(g.dma_start(out=dst, in_=src), 16)
            g.wait_ge(stw.h, stw.n)



def emit_l1_ffn_mix(ctx, b, w):
    nc = ctx.nc
    ld = ctx.sem("ffn1_ld")
    mm = ctx.sem("ffn1_mm")
    evs = ctx.sem("ffn1_ev")

    with nc.Block() as blk:
        @blk.sync
        def _(sync):
            wsrc = ap_off(w["Win"], 0, [[64, 128], [128 * 64, KC], [1, 64]])
            ld.inc(sync.dma_start(out=ctx.W[:, 4096:4096 + KC * 64], in_=wsrc), 16)
            ld.inc(sync.dma_start(out=ctx.W[0:64, 4608:4624], in_=w["W1"]), 16)
            ld.inc(sync.dma_start(out=ctx.W[0:64, 4624:4640], in_=w["b1"]), 16)
            ld.inc(sync.dma_start(out=ctx.W[0:64, 4640:4656], in_=w["W2"]), 16)
            ld.inc(sync.dma_start(out=ctx.W[0:64, 4656:4657],
                                  in_=w["bin_"].rearrange("(m o) -> m o", o=1)), 16)
            ld.inc(sync.dma_start(out=ctx.W[0:64, 4657:4658],
                                  in_=w["b2"].rearrange("(m o) -> m o", o=1)), 16)
            sync.wait_ge(ld.h, ld.n)

        @blk.gpsimd
        def _(g):
            g.wait_ge(ld.h, ld.n)
            g.tensor_copy(ctx.Wb[:, 0:KC * 64], ctx.W[:, 4096:4096 + KC * 64])

    with nc.Block() as blk:
        @blk.tensor
        def _(t):
            ev0 = evs.n
            for n in range(4):
                if n >= 2:
                    t.wait_ge(evs.h, ev0 + n - 1)
                for k in range(KC):
                    inst = t.matmul(ctx.pb[4 + n % 2][0:64, :],
                                    ctx.Wb[:, k * 64:(k + 1) * 64],
                                    ctx.a_fm(k, n * 512, 512),
                                    start=(k == 0), stop=(k == KC - 1))
                mm.inc(inst)

        @blk.scalar
        def _(s):
            mm0 = mm.n - 4
            for n in range(4):
                s.wait_ge(mm.h, mm0 + n + 1)
                inst = s.activation(ctx.C[0:64, 0 + n * 512: 512 + n * 512],
                                    ctx.pb[4 + n % 2][0:64, :], AF.Identity,
                                    bias=ctx.W[0:64, 4656:4657])
                evs.inc(inst)

    hsem = ctx.sem("ffn1_h")
    gsem = ctx.sem("ffn1_g")
    with nc.Block() as blk:
        @blk.scalar
        def _(s):
            g0 = gsem.n
            for i in range(16):
                if i >= 2:
                    s.wait_ge(gsem.h, g0 + i - 1)
                hb = ctx.C[0:64, 2048 + (i % 2) * T:2048 + (i % 2) * T + T]
                hsem.inc(s.activation(hb, ctx.C[0:64, 0:T], AF.Relu,
                                      bias=ctx.W[0:64, 4624 + i:4625 + i],
                                      scale=ctx.W[0:64, 4608 + i:4609 + i]))

        @blk.vector
        def _(v):
            h0 = hsem.n - 16
            g = ctx.B[0:64, 0:T]
            for i in range(16):
                v.wait_ge(hsem.h, h0 + i + 1)
                hb = ctx.C[0:64, 2048 + (i % 2) * T:2048 + (i % 2) * T + T]
                if i == 0:
                    inst = v.tensor_scalar(g, hb, ctx.W[0:64, 4640:4641],
                                           None, ALU.mult)
                else:
                    tg = ctx.C[0:64, 6144:6144 + T]
                    v.tensor_scalar(tg, hb, ctx.W[0:64, 4640 + i:4641 + i],
                                    None, ALU.mult)
                    inst = v.tensor_tensor(g, g, tg, ALU.add)
                gsem.inc(inst)
            v.tensor_scalar(g, g, ctx.W[0:64, 4657:4658], None, ALU.add)
    return


SUBLAYERS = {}


def build_test_program(which, weights_spec):
    nc = bass.Bass("TRN2", target_bir_lowering=False)
    x_in = nc.dram_tensor("x", [2, T, D], F32, kind="ExternalInput").ap()
    x_out = nc.dram_tensor("y", [2, T, D], F32, kind="ExternalOutput").ap()
    waps = {k: nc.dram_tensor(k, list(v), F32, kind="ExternalInput").ap()
            for k, v in weights_spec.items()}
    wsc = nc.dram_tensor("wsc", [192 * 128, 2048], BF, kind="Internal").ap()
    with contextlib.ExitStack() as es:
        ctx = Ctx(nc, es)
        ctx.wsc = wsc
        emit_prelude(ctx)
        for b in range(2):
            SUBLAYERS[which](ctx, x_in, x_out, b, waps)
    return nc


def sub_l1_ffn(ctx, x_in, x_out, b, w):
    emit_in_ln_tp(ctx, x_in, b)
    emit_l1_ffn_mix(ctx, b, w)
    proj = dict(
        kchunks=1, kpart=64,
        w_src=lambda of: ap_off(w["Wout"], of * 128, [[1024, 64], [1, 128]]),
        src=lambda k, t0, n: ctx.B[0:64, t0:t0 + n],
        bias=w["bout"],
    )
    emit_out_phase(ctx, x_in, x_out, b, proj=proj)


SUBLAYERS["l1_ffn"] = sub_l1_ffn





def emit_l0_ffn_wcast(ctx, w):
    nc = ctx.nc
    wcl = ctx.sem("wc_ld")
    wc2 = ctx.sem("wc_l2")
    wct = ctx.sem("wc_ct")
    wcs = ctx.sem("wc_st")
    wcs_base = wcs.n
    wcl_base, wc2_base = wcl.n, wc2.n
    with nc.Block() as blk:
        @blk.sync
        def _(sync):
            idx = 0
            for e in range(6):
                w1 = w["sw1"] if e < 2 else w["rw1"]
                ei = e if e < 2 else e - 2
                for hc in range(32):
                    slot = idx % 2
                    if idx >= 2:
                        sync.wait_ge(wcs.h, wcs_base + 16 * (idx - 1))
                    src1 = ap_off(w1, ei * D * 4096 + hc * 128,
                                  [[4096, 128], [128 * 4096, KC], [1, 128]])
                    wcl.inc(sync.dma_start(
                        out=ctx.C[:, slot * 2048:slot * 2048 + 1024],
                        in_=src1), 16)
                    idx += 1

        @blk.scalar
        def _(s):
            idx = 0
            for e in range(6):
                w2 = w["sw2"] if e < 2 else w["rw2"]
                ei = e if e < 2 else e - 2
                for hc in range(32):
                    slot = idx % 2
                    if idx >= 2:
                        s.wait_ge(wcs.h, wcs_base + 16 * (idx - 1))
                    wc2.inc(s.dma_start(
                        out=ctx.C[:, slot * 2048 + 1024:slot * 2048 + 2048],
                        in_=ap_off(w2, ei * 4096 * D + hc * 128 * D,
                                   [[D, 128], [1, D]])), 16)
                    idx += 1

        @blk.gpsimd
        def _(g):
            for idx in range(192):
                slot = idx % 2
                g.wait_ge(wcl.h, wcl_base + 16 * (idx + 1))
                g.wait_ge(wc2.h, wc2_base + 16 * (idx + 1))
                if idx >= 2:
                    g.wait_ge(wcs.h, wcs_base + 16 * (idx - 1))
                wct.inc(g.tensor_copy(
                    ctx.Wb[:, slot * 2048:(slot + 1) * 2048],
                    ctx.C[:, slot * 2048:(slot + 1) * 2048]))
                dstw = ap_off(ctx.wsc, idx * 128 * 2048,
                              [[2048, 128], [1, 2048]])
                wcs.inc(g.dma_start(
                    out=dstw, in_=ctx.Wb[:, slot * 2048:(slot + 1) * 2048]), 16)
            g.wait_ge(wcs.h, wcs.n)


def emit_l0_ffn_prep(ctx, w):
    nc = ctx.nc
    ld = ctx.sem("fp_ld")
    tp = ctx.sem("fp_tp")
    with nc.Block() as blk:
        @blk.sync
        def _(sync):
            for e in range(6):
                wsrc = w["sw1"] if e < 2 else w["rw1"]
                b1 = w["sb1"] if e < 2 else w["rb1"]
                b2 = w["sb2"] if e < 2 else w["rb2"]
                ei = e if e < 2 else e - 2
                ld.inc(sync.dma_start(
                    out=ctx.C[0:32, e * 128:(e + 1) * 128],
                    in_=ap_off(b1, ei * 4096, [[128, 32], [1, 128]])), 16)
                ld.inc(sync.dma_start(
                    out=ctx.C[0:8, 768 + e * 128:768 + e * 128 + 128],
                    in_=ap_off(b2, ei * 1024, [[128, 8], [1, 128]])), 16)
            for k in range(KC):
                ld.inc(sync.dma_start(
                    out=ctx.W[:, 6424 + 4 * k:6428 + 4 * k],
                    in_=ap_off(w["gate_w"], k * 128 * 4, [[4, 128], [1, 4]])), 16)
            ld.inc(sync.dma_start(out=ctx.S[0:1, 130:134],
                                  in_=w["gate_b"].rearrange("(o e) -> o e", o=1)), 16)
            ld.inc(sync.dma_start(out=ctx.W[0:4, 4096:5120],
                                  in_=w["rb2"]), 16)
            ld.inc(sync.dma_start(out=ctx.W[4:6, 4096:5120],
                                  in_=w["sb2"]), 16)
            sync.wait_ge(ld.h, ld.n)

        @blk.gpsimd
        def _(g):
            g.iota(ctx.W[0:4, 6460:6460 + 128], [[0, 128]], channel_multiplier=1,
                   allow_small_or_imprecise_dtypes=True)
            g.wait_ge(ld.h, ld.n)
            g.tensor_copy(ctx.Wb[:, 4864:4896], ctx.W[:, 6424:6456])
            g.tensor_copy(ctx.Wb[0:6, 4896:5920], ctx.W[0:6, 4096:5120])

    with nc.Block() as blk:
        @blk.tensor
        def _(t):
            feb = ctx.sem("fp_ev").n
            for e in range(6):
                if e >= 1:
                    t.wait_ge(ctx.sem("fp_ev").h, feb + e)
                t.transpose(ctx.pb[7][:, 0:32], ctx.C[0:32, e * 128:(e + 1) * 128],
                            ctx.ident[0:32, 0:32])
                tp.inc(t.transpose(ctx.pb[7][:, 32:40],
                                   ctx.C[0:8, 768 + e * 128:768 + e * 128 + 128],
                                   ctx.ident[0:8, 0:8]))
            tp.inc(t.matmul(ctx.pb[6][:, 0:4], ctx.ones_row[0:1, 0:128],
                            ctx.S[0:1, 130:134], start=True, stop=True))

        @blk.vector
        def _(v):
            fe = ctx.sem("fp_ev")
            tp0 = tp.n - 7
            for e in range(6):
                v.wait_ge(tp.h, tp0 + e + 1)
                v.tensor_copy(ctx.W[:, 6176 + 32 * e:6176 + 32 * e + 32],
                              ctx.pb[7][:, 0:32])
                fe.inc(v.tensor_copy(ctx.W[:, 6368 + 8 * e:6368 + 8 * e + 8],
                                     ctx.pb[7][:, 32:40]))
            v.wait_ge(tp.h, tp0 + 7)
            v.tensor_copy(ctx.W[:, 6456:6460], ctx.pb[6][:, 0:4])
            v.tensor_scalar(ctx.W[:, 6176:6240], ctx.W[:, 6176:6240],
                            0.5, None, ALU.mult)
            v.tensor_tensor(ctx.W[:, 6416:6424], ctx.W[:, 6368:6376],
                            ctx.W[:, 6376:6384], ALU.add)
            for e in range(4):
                v.tensor_scalar(ctx.W[0:4, 6460 + 128 * (e + 1):6460 + 128 * (e + 2)]
                                if False else
                                ctx.W[0:4, 6588 + 128 * e:6588 + 128 * e + 128],
                                ctx.W[0:4, 6460:6588][0:4, 0:128], float(e), None,
                                ALU.is_equal)


def emit_l0_ffn_gate(ctx, b):
    nc = ctx.nc
    gm = ctx.sem("fg_mm")
    ge = ctx.sem("fg_ev")
    with nc.Block() as blk:
        @blk.tensor
        def _(t):
            ge0 = ge.n
            for i in range(NT):
                if i >= 2:
                    t.wait_ge(ge.h, ge0 + i - 1)
                for k in range(KC):
                    inst = t.matmul(ctx.pb[6 + i % 2][0:128, 0:4],
                                    ctx.a_fm(k, i * 128, 128),
                                    ctx.Wb[:, 4864 + 4 * k:4868 + 4 * k],
                                    start=(k == 0), stop=(k == KC - 1))
                gm.inc(inst)

        @blk.vector
        def _(v):
            gm0 = gm.n - NT
            for i in range(NT):
                v.wait_ge(gm.h, gm0 + i + 1)
                ge.inc(v.tensor_tensor(ctx.C[:, 6144 + 4 * i:6148 + 4 * i],
                                       ctx.pb[6 + i % 2][:, 0:4],
                                       ctx.W[:, 6456:6460], ALU.add))

    sm = ctx.sem("fg_sm")
    with nc.Block() as blk:
        @blk.vector
        def _(v):
            L = ctx.C[:, 6144:6208]
            E3 = L.rearrange("p (g e) -> p g e", e=4)
            MX = ctx.C[:, 6208:6224]
            SM = ctx.C[:, 6224:6240]
            W4 = ctx.C[:, 6240:6304]
            M1 = ctx.C[:, 6304:6320]
            M2 = ctx.C[:, 6320:6336]
            TMP = ctx.C[:, 6336:6400]
            def bc16(x):
                return bass.AP(x.tensor, x.offset, [list(x.ap[0]), list(x.ap[1]), [0, 4]])
            smb = sm.n
            inst = v.tensor_reduce(MX, E3, axis=AX.X, op=ALU.max)
            ctx.fence(v, "f_dve", inst)
            inst = v.tensor_tensor(W4.rearrange("p (g e) -> p g e", e=4), E3,
                                   bc16(MX), ALU.subtract)
            sm.inc(inst)
            v.wait_ge(sm.h, smb + 2)
            inst = v.tensor_reduce(SM, W4.rearrange("p (g e) -> p g e", e=4),
                                   axis=AX.X, op=ALU.add)
            ctx.fence(v, "f_dve", inst)
            inst = v.reciprocal(SM, SM)
            ctx.fence(v, "f_dve", inst)
            inst = v.tensor_tensor(W4.rearrange("p (g e) -> p g e", e=4),
                                   W4.rearrange("p (g e) -> p g e", e=4),
                                   bc16(SM), ALU.mult)
            ctx.fence(v, "f_dve", inst)
            inst = v.tensor_reduce(M1, W4.rearrange("p (g e) -> p g e", e=4),
                                   axis=AX.X, op=ALU.max)
            ctx.fence(v, "f_dve", inst)
            inst = v.tensor_tensor(TMP.rearrange("p (g e) -> p g e", e=4),
                                   W4.rearrange("p (g e) -> p g e", e=4),
                                   bc16(M1), ALU.subtract)
            ctx.fence(v, "f_dve", inst)
            inst = v.tensor_scalar(TMP, TMP, 0.0, None, ALU.is_ge)
            ctx.fence(v, "f_dve", inst)
            inst = v.scalar_tensor_tensor(TMP, TMP, -1e30, W4, ALU.mult, ALU.add)
            ctx.fence(v, "f_dve", inst)
            inst = v.tensor_reduce(M2, TMP.rearrange("p (g e) -> p g e", e=4),
                                   axis=AX.X, op=ALU.max)
            ctx.fence(v, "f_dve", inst)
            inst = v.tensor_tensor(TMP.rearrange("p (g e) -> p g e", e=4),
                                   W4.rearrange("p (g e) -> p g e", e=4),
                                   bc16(M2), ALU.subtract)
            ctx.fence(v, "f_dve", inst)
            inst = v.tensor_scalar(TMP, TMP, 0.0, None, ALU.is_ge)
            ctx.fence(v, "f_dve", inst)
            v.tensor_tensor(W4, W4, TMP, ALU.mult)

        @blk.scalar
        def _(s):
            s.wait_ge(sm.h, sm.n - 1 + 1)
            sm.inc(s.activation(ctx.C[:, 6240:6304], ctx.C[:, 6240:6304], AF.Exp))

        @blk.gpsimd
        def _(g):
            g.memset(ctx.Wb[0:6, 5920:7968], 0.5)

    tp = ctx.sem("fg_tp")
    ev = ctx.sem("fg_tpe")
    with nc.Block() as blk:
        @blk.tensor
        def _(t):
            ev0 = ev.n
            for i in range(NT):
                if i >= 8:
                    t.wait_ge(ev.h, ev0 + (i - 8) // 4 + 1)
                tp.inc(t.transpose(ctx.pb[6 + (i // 4) % 2][0:4, (i % 4) * 128:(i % 4) * 128 + 128],
                                   ctx.C[:, 6240 + 4 * i:6244 + 4 * i],
                                   ctx.ident[:, :]))

        @blk.vector
        def _(v):
            tp0 = tp.n - NT
            for gsel in range(4):
                v.wait_ge(tp.h, tp0 + gsel * 4 + 4)
                v.tensor_copy(
                    ctx.C[0:4, 2048 + gsel * 512:2048 + gsel * 512 + 512],
                    ctx.pb[6 + gsel % 2][0:4, 0:512])
                ev.inc(v.tensor_copy(
                    ctx.Wb[0:4, 5920 + gsel * 512:5920 + gsel * 512 + 512],
                    ctx.pb[6 + gsel % 2][0:4, 0:512]))



def emit_l0_ffn_mix(ctx, b, w):
    nc = ctx.nc
    hmm = ctx.sem("fx_hmm")
    hrl = ctx.sem("fx_hrl")
    hsc = ctx.sem("fx_hsc")
    omm = ctx.sem("fx_omm")
    oev = ctx.sem("fx_oev")
    wbm = ctx.sem("fx_wbm")
    wbe = ctx.sem("fx_wbe")

    TILES = [(0, 384), (384, 384), (768, 384), (1152, 384), (1536, 384),
             (1920, 128)]
    NIDX = len(TILES) * 6 * 32
    omm_base = omm.n
    ldw_base = [ctx.sem(f"fx_ldw{q}").n for q in range(2)]

    def emit_load(eng, idx, e, hc):
        slot = idx % 2
        if idx >= 2:
            eng.wait_ge(omm.h, omm_base + idx - 1)
        sl = ctx.sem(f"fx_ldw{slot}")
        ti = e * 32 + hc
        srcw = ap_off(ctx.wsc, ti * 128 * 2048, [[2048, 128], [1, 2048]])
        sl.inc(eng.dma_start(
            out=ctx.Wb[:, slot * 2048:slot * 2048 + 2048], in_=srcw), 16)

    with nc.Block() as blk:
        def emit_loads(eng, qid):
            idx = 0
            for j in range(len(TILES)):
                for e in range(6):
                    for hc in range(32):
                        if idx % 2 == qid:
                            emit_load(eng, idx, e, hc)
                        idx += 1

        @blk.sync
        def _(sync):
            emit_loads(sync, 0)

        @blk.gpsimd
        def _(g):
            emit_loads(g, 1)

        @blk.tensor
        def _(t):
            hrl0, oev0, wbe0, hsc0 = hrl.n, oev.n, wbe.n, hsc.n
            idx = 0
            rcnt = 0
            ne_done = 0
            for j, (ts0, tlen) in enumerate(TILES):
                ntc = tlen // 128
                t.wait_ge(hrl.h, hrl0 + idx)
                for e4 in range(4):
                    if e4 >= 2:
                        t.wait_ge(wbe.h, wbe0 + j * 4 + e4 - 1)
                    wbm.inc(t.matmul(
                        ctx.pb[6 + e4 % 2][:, 0:tlen],
                        ctx.W[0:4, 6588 + 128 * e4:6588 + 128 * e4 + 128],
                        ctx.C[0:4, 2048 + ts0:2048 + ts0 + tlen],
                        start=True, stop=True))
                if j >= 1:
                    t.wait_ge(oev.h, oev0 + ne_done)
                for tc in range(ntc):
                    for oh in range(2):
                        t.matmul(ctx.pb[tc * 2 + oh][:, 0:512],
                                 ctx.Wb[0:6, 5920 + ts0 + tc * 128:
                                        5920 + ts0 + tc * 128 + 128],
                                 ctx.Wb[0:6, 4896 + oh * 512:5408 + oh * 512],
                                 start=True, stop=False, skip_group_check=True)
                t.wait_ge(wbe.h, wbe0 + (j + 1) * 4)

                def emit_out(pidx, pe, prof):
                    if pe < 2:
                        t.wait_ge(hrl.h, hrl0 + pidx + 1)
                    else:
                        t.wait_ge(hsc.h, hsc0 + prof)
                    pslot = pidx % 2
                    hs0 = 4096 + pslot * 384
                    plast = (pe == 5 and pidx % 32 == 31)
                    for tc in range(ntc):
                        for oh in range(2):
                            inst = t.matmul(
                                ctx.pb[tc * 2 + oh][:, 0:512],
                                ctx.Wb[:, hs0 + tc * 128:hs0 + tc * 128 + 128],
                                ctx.Wb[:, pslot * 2048 + 1024 + oh * 512:
                                       pslot * 2048 + 1536 + oh * 512],
                                start=False, stop=(plast and tc == ntc - 1),
                                skip_group_check=True)
                    omm.inc(inst)

                pend = None
                for e in range(6):
                    for hc in range(32):
                        slot = idx % 2
                        t.wait_ge(ctx.sem(f"fx_ldw{slot}").h,
                                  ldw_base[slot] + 16 * (idx // 2 + 1))
                        if idx >= 2:
                            t.wait_ge(hrl.h, hrl0 + idx - 1)
                        for k in range(KC):
                            inst = t.matmul(
                                ctx.pb[6 + idx % 2][:, 0:tlen],
                                ctx.Wb[:, slot * 2048 + k * 128:
                                       slot * 2048 + (k + 1) * 128],
                                ctx.a_fm(k, ts0, tlen),
                                start=(k == 0), stop=(k == KC - 1))
                        hmm.inc(inst)
                        if e >= 2:
                            rcnt += 1
                        if pend is not None:
                            emit_out(*pend)
                        pend = (idx, e, rcnt)
                        idx += 1
                emit_out(*pend)
                ne_done += ntc * 2

        @blk.scalar
        def _(s):
            hmm0 = hmm.n - NIDX
            idx = 0
            for j, (ts0, tlen) in enumerate(TILES):
                for e in range(6):
                    for hc in range(32):
                        s.wait_ge(hmm.h, hmm0 + idx + 1)
                        dsth = (ctx.Wb[:, 4096 + (idx % 2) * 384:
                                       4096 + (idx % 2) * 384 + tlen]
                                if e < 2 else
                                ctx.C[:, (idx % 2) * 384:(idx % 2) * 384 + tlen])
                        hrl.inc(s.activation(
                            dsth,
                            ctx.pb[6 + idx % 2][:, 0:tlen], AF.Relu,
                            scale=(0.5 if e < 2 else 1.0),
                            bias=ctx.W[:, 6176 + 32 * e + hc:
                                       6176 + 32 * e + hc + 1]))
                        idx += 1

        @blk.vector
        def _(v):
            omm0 = omm.n - NIDX
            hrl0 = hrl.n - NIDX
            wbm0 = wbm.n - len(TILES) * 4
            idx = 0
            rcnt = 0
            for j, (ts0, tlen) in enumerate(TILES):
                ntc = tlen // 128
                for e4 in range(4):
                    v.wait_ge(wbm.h, wbm0 + j * 4 + e4 + 1)
                    inst = v.tensor_copy(
                        ctx.C[:, 5120 + e4 * 384:5120 + e4 * 384 + tlen],
                        ctx.pb[6 + e4 % 2][:, 0:tlen])
                    wbe.inc(inst)
                    v.wait_ge(wbe.h, wbe.n)
                for e in range(6):
                    for hc in range(32):
                        if e >= 2:
                            v.wait_ge(hrl.h, hrl0 + idx + 1)
                            hs = ctx.C[:, (idx % 2) * 384:
                                       (idx % 2) * 384 + tlen]
                            hsc.inc(v.tensor_tensor(
                                ctx.Wb[:, 4096 + (idx % 2) * 384:
                                       4096 + (idx % 2) * 384 + tlen], hs,
                                ctx.C[:, 5120 + (e - 2) * 384:
                                      5120 + (e - 2) * 384 + tlen], ALU.mult))
                            rcnt += 1
                        idx += 1
                v.wait_ge(omm.h, omm0 + idx)
                for tc in range(ntc):
                    for oh in range(2):
                        nchunk = j * 3 + tc
                        oev.inc(v.tensor_copy(
                            ctx.B[:, nchunk * 1024 + oh * 512:
                                  nchunk * 1024 + oh * 512 + 512],
                            ctx.pb[tc * 2 + oh][:, 0:512]))


def emit_out_resid_tm(ctx, x_in, x_out, b):
    nc = ctx.nc
    ldx = ctx.sem("out_ldx")
    add = ctx.sem("out_add")
    stw = ctx.sem("out_st")
    ldx_base = ldx.n
    add_base = add.n
    with nc.Block() as blk:
        @blk.sync
        def _(sync):
            for g in range(4):
                if g >= 1:
                    sync.wait_ge(add.h, add_base + 4 * g)
                dst = bass.AP(ctx.XS, 0, [[4096, 128], [1024, 4], [1, 1024]])
                srcx = ap_off(x_in, (b * T + g * 512) * D,
                              [[1024, 128], [128 * 1024, 4], [1, 1024]])
                ldx.inc(sync.dma_start(out=dst, in_=srcx), 16)

        @blk.vector
        def _(v):
            for n in range(NT):
                if n % 4 == 0:
                    v.wait_ge(ldx.h, ldx_base + 16 * (n // 4 + 1))
                add.inc(v.tensor_tensor(
                    ctx.B[:, n * 1024:(n + 1) * 1024],
                    ctx.B[:, n * 1024:(n + 1) * 1024],
                    ctx.XS[:, (n % 4) * 1024:(n % 4 + 1) * 1024], ALU.add))

        @blk.gpsimd
        def _(g):
            add0 = add.n - NT
            st0 = stw.n
            for n in range(4):
                g.wait_ge(add.h, add0 + (n + 1) * 4)
                srcb = bass.AP(ctx.B, n * 4096,
                               [[KC * T, 128], [1024, 4], [1, 1024]])
                dst = ap_off(x_out, (b * T + n * 512) * D,
                             [[1024, 128], [128 * 1024, 4], [1, 1024]])
                stw.inc(g.dma_start(out=dst, in_=srcb), 16)
            g.wait_ge(stw.h, stw.n)


def sub_l0_ffn(ctx, x_in, x_out, b, w):
    if b == 0 and not getattr(ctx, "wsc_cast_done", False):
        emit_l0_ffn_wcast(ctx, w)
        ctx.wsc_cast_done = True
    if b == 0:
        emit_l0_ffn_prep(ctx, w)
    emit_in_ln_tp(ctx, x_in, b)
    emit_l0_ffn_gate(ctx, b)
    emit_l0_ffn_mix(ctx, b, w)
    emit_out_resid_tm(ctx, x_in, x_out, b)


SUBLAYERS["l0_ffn"] = sub_l0_ffn





def emit_l0_attn_b_mix(ctx, b, w):
    nc = ctx.nc
    EPS = float(np.finfo(np.float32).eps)
    ld = ctx.sem("ab_ld")
    wtp = ctx.sem("ab_wtp")
    wte = ctx.sem("ab_wte")
    mm = ctx.sem("ab_mm")
    aev = ctx.sem("ab_ev")
    tp = ctx.sem("ab_tp")
    te = ctx.sem("ab_te")
    ks = ctx.sem("ab_ks")
    cx = ctx.sem("ab_cx")
    cwe = ctx.sem("ab_cwe")
    om = ctx.sem("ab_om")
    zr = ctx.sem("ab_zr")
    zb = ctx.sem("ab_zb")
    ze = ctx.sem("ab_ze")
    fin = ctx.sem("ab_fin")

    for hp in range(8):
        with nc.Block() as blk:
            @blk.sync
            def _(sync):
                wtp0 = wtp.n
                for wi in range(3):
                    if wi >= 2:
                        sync.wait_ge(wtp.h, wtp0 + (wi - 1) * 2)
                    off = (wi * 1024 + hp * 128) * D
                    ctx.sem(f"ab_ld{wi % 2}").inc(sync.dma_start(
                        out=ctx.C[:, (wi % 2) * 1024:(wi % 2) * 1024 + 1024],
                        in_=ap_off(w["w_qkv"], off, [[D, 128], [1, D]])), 16)

            @blk.tensor
            def _(t):
                slb = [ctx.sem("ab_ld0").n - 2 * 16, ctx.sem("ab_ld1").n - 16]
                wte0 = wte.n
                for wi in range(3):
                    t.wait_ge(ctx.sem(f"ab_ld{wi % 2}").h,
                              slb[wi % 2] + 16 * (wi // 2 + 1))
                    if wi >= 1:
                        t.wait_ge(wte.h, wte0 + wi)
                    for k in range(KC):
                        inst = t.transpose(
                            ctx.pb[2 + (k // 4) % 2][:, (k % 4) * 128:(k % 4) * 128 + 128],
                            ctx.C[:, (wi % 2) * 1024 + k * 128:
                                  (wi % 2) * 1024 + (k + 1) * 128],
                            ctx.ident[:, :])
                        if k == 3 or k == 7:
                            wtp.inc(inst)

            @blk.vector
            def _(v):
                wtp0 = wtp.n - 6
                wte0 = wte.n
                for wi in range(3):
                    for g in range(2):
                        v.wait_ge(wtp.h, wtp0 + wi * 2 + g + 1)
                        for k in range(g * 4, g * 4 + 4):
                            inst = v.tensor_copy(
                                ctx.Wb[:, wi * 1024 + k * 128:wi * 1024 + (k + 1) * 128],
                                ctx.pb[2 + g][:, (k % 4) * 128:(k % 4) * 128 + 128])
                    wte.inc(inst)

        with nc.Block() as blk:
            @blk.tensor
            def _(t):
                aev0, te0, cwe0, zr0, ze0, fin0 = aev.n, te.n, cwe.n, zr.n, ze.n, fin.n
                for gi in range(12):
                    wi, ncn = divmod(gi, 4)
                    if gi >= 2:
                        t.wait_ge(aev.h, aev0 + gi - 1)
                    for k in range(KC):
                        inst = t.matmul(ctx.pb[gi % 2][:, :],
                                        ctx.Wb[:, wi * 1024 + k * 128:wi * 1024 + (k + 1) * 128],
                                        ctx.a_fm(k, ncn * 512, 512),
                                        start=(k == 0), stop=(k == KC - 1))
                    mm.inc(inst)
                    if gi == 7:
                        t.wait_ge(aev.h, aev0 + 8)
                        for tc in range(16):
                            g = tc // 4
                            if g >= 2:
                                t.wait_ge(te.h, te0 + g - 1)
                            inst = t.transpose(
                                ctx.pb[2 + g % 2][:, (tc % 4) * 128:(tc % 4) * 128 + 128],
                                ctx.C[:, 2048 + tc * 128:2048 + (tc + 1) * 128],
                                ctx.ident[:, :])
                            if tc % 4 == 3:
                                tp.inc(inst)
                t.wait_ge(aev.h, aev0 + 12)
                for tc in range(16):
                    g = tc // 4
                    t.wait_ge(te.h, te0 + 3 + g)
                    inst = t.transpose(
                        ctx.pb[2 + g % 2][:, (tc % 4) * 128:(tc % 4) * 128 + 128],
                        ctx.C[:, 2048 + tc * 128:2048 + (tc + 1) * 128],
                        ctx.ident[:, :])
                    if tc % 4 == 3:
                        tp.inc(inst)
                if hp > 0:
                    t.wait_ge(cwe.h, cwe0)
                for tc in range(16):
                    t.wait_ge(te.h, te0 + 4 + tc // 4 + 1)
                    for h in range(2):
                        inst = t.matmul(
                            ctx.pb[4][0:128, h * 64:h * 64 + 64],
                            ctx.C[:, 4096 + tc * 128:4096 + (tc + 1) * 128],
                            ctx.C[:, 6144 + tc * 128 + h * 64:6144 + tc * 128 + h * 64 + 64],
                            start=(tc == 0 and h == 0), stop=(tc == 15),
                            skip_group_check=True)
                cx.inc(inst)
                t.wait_ge(cwe.h, cwe0 + 1)
                for g in range(8):
                    h, ncn = divmod(g, 4)
                    if g >= 2:
                        t.wait_ge(fin.h, fin0 + g - 1)
                    om.inc(t.matmul(ctx.pb[5 + g % 2][0:65, :],
                                    ctx.S[h * 64:h * 64 + 64, 300:365],
                                    ctx.C[h * 64:h * 64 + 64,
                                          ncn * 512:ncn * 512 + 512],
                                    start=True, stop=True))
                    t.wait_ge(zr.h, zr0 + g + 1)
                    if g >= 1:
                        t.wait_ge(ze.h, ze0 + g)
                    zb.inc(t.matmul(ctx.pb[7][:, :], ctx.ones_row[0:1, 0:128],
                                    ctx.S[0:1, 365:877], start=True, stop=True))

            @blk.scalar
            def _(s):
                mm0 = mm.n - 12
                tp0 = tp.n - 8
                ks0 = ks.n
                for gi in range(12):
                    wi, ncn = divmod(gi, 4)
                    s.wait_ge(mm.h, mm0 + gi + 1)
                    if wi == 2 and ncn == 0:
                        s.wait_ge(tp.h, tp0 + 4)
                        s.wait_ge(ks.h, ks0 + 1)
                    dst = ctx.C[:, wi * 2048 + ncn * 512:wi * 2048 + ncn * 512 + 512] \
                        if wi < 2 else \
                        ctx.C[:, 2048 + ncn * 512:2048 + ncn * 512 + 512]
                    aev.inc(s.activation(dst, ctx.pb[gi % 2][:, :],
                                         AF.Relu if wi < 2 else AF.Identity))

            @blk.vector
            def _(v):
                aev0 = aev.n - 12
                tp0 = tp.n - 8
                cx0 = cx.n - 1
                om0 = om.n - 8
                zb0 = zb.n - 8
                v.wait_ge(aev.h, aev0 + 8)
                ks.inc(v.tensor_reduce(ctx.S[:, 364:365], ctx.C[:, 2048:4096],
                                       axis=AX.X, op=ALU.add))
                for g in range(8):
                    v.wait_ge(tp.h, tp0 + g + 1)
                    base = 4096 if g < 4 else 6144
                    gg = g % 4
                    for tc in range(gg * 4, gg * 4 + 4):
                        inst = v.tensor_copy(
                            ctx.C[:, base + tc * 128:base + (tc + 1) * 128],
                            ctx.pb[2 + g % 2][:, (tc % 4) * 128:(tc % 4) * 128 + 128])
                    te.inc(inst)
                v.wait_ge(cx.h, cx0 + 1)
                v.tensor_copy(ctx.S[0:64, 300:364], ctx.pb[4][0:64, 0:64])
                cwe.inc(v.tensor_copy(ctx.S[64:128, 300:364],
                                      ctx.pb[4][64:128, 64:128]))
                for g in range(8):
                    h, ncn = divmod(g, 4)
                    v.wait_ge(om.h, om0 + g + 1)
                    inst = v.tensor_scalar(ctx.S[0:1, 365:877],
                                           ctx.pb[5 + g % 2][64:65, :], EPS,
                                           None, ALU.add)
                    ctx.fence(v, "f_dve", inst)
                    zr.inc(v.reciprocal(ctx.S[0:1, 365:877],
                                        ctx.S[0:1, 365:877]))
                    v.wait_ge(zb.h, zb0 + g + 1)
                    inst = v.tensor_copy(ctx.S[:, 896:1408], ctx.pb[7][:, :])
                    ze.inc(inst)
                    v.wait_ge(ze.h, ze.n)
                    inst = v.tensor_tensor(
                        ctx.B[h * 64:h * 64 + 64,
                              hp * T + ncn * 512:hp * T + ncn * 512 + 512],
                        ctx.pb[5 + g % 2][0:64, :],
                        ctx.S[0:64, 896:1408], ALU.mult)
                    fin.inc(inst)


def sub_l0_attn_b(ctx, x_in, x_out, b, w):
    emit_in_ln_tp(ctx, x_in, b)
    emit_l0_attn_b_mix(ctx, b, w)
    proj = dict(
        kchunks=8, kpart=128, transpose=True,
        w_rows=lambda of: ap_off(w["w_proj"], of * 128 * D, [[D, 128], [1, D]]),
        src=lambda k, t0, n: ctx.b_fm(k, t0, n),
        bias=w["b_proj"],
    )
    emit_out_phase(ctx, x_in, x_out, b, proj=proj)


SUBLAYERS["l0_attn_b"] = sub_l0_attn_b





def emit_l1_attn_a_prep(ctx, w):
    nc = ctx.nc
    ld = ctx.sem("aa_pld")
    tp = ctx.sem("aa_ptp")
    rc = ctx.sem("aa_prc")
    with nc.Block() as blk:
        @blk.sync
        def _(sync):
            ld.inc(sync.dma_start(out=ctx.R[0:1, 0:256],
                                  in_=w["bq"].rearrange("(o c) -> o c", o=1)), 16)
            ld.inc(sync.dma_start(out=ctx.R[0:1, 256:512],
                                  in_=w["bk"].rearrange("(o c) -> o c", o=1)), 16)
            ld.inc(sync.dma_start(
                out=ctx.R[0:1, 512:768],
                in_=ap_off(w["q_phase"], 0, [[256, 1], [1, 256]])), 16)
            ld.inc(sync.dma_start(
                out=ctx.R[0:1, 768:1024],
                in_=ap_off(w["k_phase"], 0, [[256, 1], [1, 256]])), 16)
            ld.inc(sync.dma_start(out=ctx.R[0:1, 1024:1025],
                                  in_=w["temperature"].rearrange("(o c) -> o c", o=1)), 16)
            ld.inc(sync.dma_start(out=ctx.S[0:8, 0:128],
                                  in_=w["bv"].rearrange("(a c) -> a c", c=128)), 16)
            sync.wait_ge(ld.h, ld.n)
        @blk.gpsimd
        def _(g):
            g.iota(ctx.W[:, 7296:8192], [[1, 896]], channel_multiplier=-1,
                   allow_small_or_imprecise_dtypes=True)
    with nc.Block() as blk:
        @blk.vector
        def _(v):
            v.tensor_scalar(ctx.W[:, 7296:8192], ctx.W[:, 7296:8192], 383.5,
                            None, ALU.is_ge)
            v.tensor_tensor(ctx.R[0:1, 0:256], ctx.R[0:1, 0:256],
                            ctx.R[0:1, 512:768], ALU.add)
            v.tensor_tensor(ctx.R[0:1, 256:512], ctx.R[0:1, 256:512],
                            ctx.R[0:1, 768:1024], ALU.add)
            inst = v.reciprocal(ctx.R[0:1, 1025:1026], ctx.R[0:1, 1024:1025])
            rc.inc(inst)
    with nc.Block() as blk:
        @blk.gpsimd
        def _(g):
            g.tensor_copy(ctx.Wb[:, 5632:6528], ctx.W[:, 7296:8192])

        @blk.tensor
        def _(t):
            t.wait_ge(rc.h, rc.n)
            t.matmul(ctx.pb[6][:, 0:1], ctx.ones_row[0:1, 0:128],
                     ctx.R[0:1, 1025:1026], start=True, stop=True)
            tp.inc(t.transpose(ctx.pb[6][:, 8:16], ctx.S[0:8, 0:128],
                               ctx.ident[0:8, 0:8]))
        @blk.vector
        def _(v):
            v.wait_ge(tp.h, tp.n)
            inst = v.tensor_copy(ctx.W[:, 7294:7295], ctx.pb[6][:, 0:1])
            ctx.fence(v, "f_dve", inst)
            v.tensor_copy(ctx.W[:, 7286:7294], ctx.pb[6][:, 8:16])
            inst = v.tensor_scalar(ctx.W[:, 6803:6804], ctx.W[:, 7294:7295],
                                   -1.0, None, ALU.mult)
            ctx.fence(v, "f_dve", inst)
            v.tensor_tensor(ctx.W[:, 6802:6803], ctx.W[:, 7294:7295],
                            ctx.W[:, 6803:6804], ALU.min)


def emit_l1_attn_a_mix(ctx, b, w, first):
    nc = ctx.nc
    ldw = ctx.sem("aa_ldw")
    qk = ctx.sem("aa_qk")
    qe = ctx.sem("aa_qe")
    sqd = ctx.sem("aa_sq")
    red = ctx.sem("aa_rd")
    lne = ctx.sem("aa_le")
    nrm = ctx.sem("aa_nm")
    bcs = ctx.sem("aa_bc")

    with nc.Block() as blk:
        @blk.sync
        def _(sync):
            for wi, wname in ((0, "Wq"), (1, "Wk")):
                src = ap_off(w[wname], 0, [[256, 128], [128 * 256, KC], [1, 256]])
                ldw.inc(sync.dma_start(
                    out=ctx.W[:, wi * 2048:(wi + 1) * 2048], in_=src), 16)
            sync.wait_ge(ldw.h, ldw.n)

        @blk.gpsimd
        def _(g):
            g.wait_ge(ldw.h, ldw.n)
            g.tensor_copy(ctx.Wb[:, 0:4096], ctx.W[:, 0:4096])

    NG = 2 * NT
    with nc.Block() as blk:
        @blk.tensor
        def _(t):
            bcs.inc(t.matmul(ctx.pb[6][:, 0:256], ctx.ones_row[0:1, 0:128],
                             ctx.R[0:1, 0:256], start=True, stop=True))
            bcs.inc(t.matmul(ctx.pb[7][:, 0:256], ctx.ones_row[0:1, 0:128],
                             ctx.R[0:1, 256:512], start=True, stop=True))
            qe0 = qe.n
            for gi in range(NG):
                wi, i = divmod(gi, NT)
                if gi >= 2:
                    t.wait_ge(qe.h, qe0 + gi - 1)
                for k in range(KC):
                    inst = t.matmul(ctx.pb[gi % 2][:, 0:256],
                                    ctx.a_fm(k, i * 128, 128),
                                    ctx.Wb[:, wi * 2048 + k * 256:wi * 2048 + (k + 1) * 256],
                                    start=(k == 0), stop=(k == KC - 1))
                qk.inc(inst)

        @blk.scalar
        def _(s):
            qe0 = qe.n
            red0 = red.n
            nrm0 = nrm.n
            for gi in range(NG + 1):
                if gi < NG:
                    s.wait_ge(qe.h, qe0 + gi + 1)
                    if gi >= 2:
                        s.wait_ge(red.h, red0 + gi - 1)
                    wi, i = divmod(gi, NT)
                    dst = ctx.C[:, wi * 4096 + i * 256:wi * 4096 + (i + 1) * 256]
                    sqd.inc(s.activation(
                        ctx.W[:, 4624 + (gi % 2) * 256:4624 + (gi % 2) * 256 + 256],
                        dst, AF.Square))
                if gi >= 1:
                    gp = gi - 1
                    s.wait_ge(red.h, red0 + gp + 1)
                    if gp >= 2:
                        s.wait_ge(nrm.h, nrm0 + gp - 1)
                    nsq = ctx.S[:, 224 + (gp % 2) * 16:240 + (gp % 2) * 16]
                    lnt = ctx.S[:, 192 + (gp % 2) * 16:208 + (gp % 2) * 16]
                    nn = ctx.S[:, 160 + (gp % 2) * 16:176 + (gp % 2) * 16]
                    inst = s.activation(lnt, nsq, AF.Ln)
                    ctx.fence(s, "f_act", inst)
                    lne.inc(s.activation(nn, lnt, AF.Exp, scale=0.5))

        @blk.vector
        def _(v):
            qk0, bc0 = qk.n - NG, bcs.n - 2
            sq0 = sqd.n - NG
            lne0 = lne.n - NG
            for gi in range(NG + 2):
                if gi < NG:
                    wi, i = divmod(gi, NT)
                    v.wait_ge(qk.h, qk0 + gi + 1)
                    if gi == 0:
                        v.wait_ge(bcs.h, bc0 + 2)
                        inst = v.tensor_copy(ctx.W[:, 4096:4352],
                                             ctx.pb[6][:, 0:256])
                        inst = v.tensor_copy(ctx.W[:, 4352:4608],
                                             ctx.pb[7][:, 0:256])
                        ctx.fence(v, "f_dve", inst)
                    dst = ctx.C[:, wi * 4096 + i * 256:wi * 4096 + (i + 1) * 256]
                    qe.inc(v.tensor_tensor(dst, ctx.pb[gi % 2][:, 0:256],
                                           ctx.W[:, 4096 + wi * 256:4352 + wi * 256],
                                           ALU.add))
                if gi >= 1 and gi - 1 < NG:
                    gp = gi - 1
                    v.wait_ge(sqd.h, sq0 + gp + 1)
                    sq3 = ctx.W[:, 4624 + (gp % 2) * 256:4624 + (gp % 2) * 256 + 256]
                    red.inc(v.tensor_reduce(
                        ctx.S[:, 224 + (gp % 2) * 16:240 + (gp % 2) * 16],
                        sq3.rearrange("p (h w) -> p h w", w=16),
                        axis=AX.X, op=ALU.add))
                if gi >= 2:
                    gp = gi - 2
                    wi, i = divmod(gp, NT)
                    v.wait_ge(lne.h, lne0 + gp + 1)
                    nn = ctx.S[:, 160 + (gp % 2) * 16:176 + (gp % 2) * 16]
                    rn = ctx.S[:, 128 + (gp % 2) * 16:144 + (gp % 2) * 16]
                    inst = v.tensor_scalar(rn, nn, 1e-12, None, ALU.max)
                    ctx.fence(v, "f_dve", inst)
                    inst = v.reciprocal(rn, rn)
                    ctx.fence(v, "f_dve", inst)
                    if wi == 0:
                        inst = v.tensor_scalar(rn, rn, ctx.W[:, 7294:7295],
                                               None, ALU.mult)
                        ctx.fence(v, "f_dve", inst)
                    dst = ctx.C[:, wi * 4096 + i * 256:wi * 4096 + (i + 1) * 256]
                    rnb = bass.AP(rn.tensor, rn.offset,
                                  [[2048, 128], [1, 16], [0, 16]])
                    d3 = dst.rearrange("p (h w) -> p h w", w=16)
                    nrm.inc(v.tensor_tensor(d3, d3, rnb, ALU.mult))

    stg = ctx.sem("aa_st")
    ste = ctx.sem("aa_se")
    vld = ctx.sem("aa_vld")
    vm = ctx.sem("aa_vm")
    ve = ctx.sem("aa_ve")
    sc = ctx.sem("aa_sc")
    ex = ctx.sem("aa_ex")
    mk = ctx.sem("aa_mk")
    ym = ctx.sem("aa_ym")
    zr = ctx.sem("aa_zr")
    zb = ctx.sem("aa_zb")
    zbe = ctx.sem("aa_zbe")
    fin = ctx.sem("aa_fin")

    with nc.Block() as blk:
        @blk.gpsimd
        def _(g):
            g.memset(bass.AP(ctx.Wb, 6528 + 64, [[8192, 128], [65, 16]]), 1.0)

    CHUNKS = [(scn, tc) for scn in range(4) for tc in range(4 * scn + 4)]
    NCH = len(CHUNKS)
    for h in range(16):
        with nc.Block() as blk:
            @blk.sync
            def _(sync):
                src = ap_off(w["Wv"], h * 64,
                             [[D, 128], [128 * D, KC], [1, 64]])
                vld.inc(sync.dma_start(out=ctx.W[:, 0:512], in_=src), 16)
                sync.wait_ge(vld.h, vld.n)

            @blk.gpsimd
            def _(g):
                vct = ctx.sem("aa_vc")
                g.wait_ge(vld.h, vld.n)
                vct.inc(g.tensor_copy(ctx.Wb[:, 4096:4608], ctx.W[:, 0:512]))

            @blk.tensor
            def _(t):
                ste0, ve0, ex0, mk0 = ste.n, ve.n, ex.n, mk.n
                zr0, zbe0, fin0 = zr.n, zbe.n, fin.n
                gq = 0
                for wi in range(2):
                    for i in range(NT):
                        bank = ctx.pb[6 + gq % 2]
                        if gq >= 2:
                            t.wait_ge(ste.h, ste0 + gq - 1)
                        inst = t.transpose(
                            bank[0:16, (i % 4) * 128:(i % 4) * 128 + 128],
                            ctx.C[:, wi * 4096 + i * 256 + h * 16:
                                  wi * 4096 + i * 256 + h * 16 + 16],
                            ctx.ident[:, :])
                        if i % 4 == 3:
                            stg.inc(inst)
                            gq += 1
                t.wait_ge(ctx.sem("aa_vc").h, ctx.sem("aa_vc").n)
                for i in range(NT):
                    if i >= 2:
                        t.wait_ge(ve.h, ve0 + i - 1)
                    for k in range(KC):
                        inst = t.matmul(ctx.pb[i % 2][:, 0:64],
                                        ctx.a_fm(k, i * 128, 128),
                                        ctx.Wb[:, 4096 + k * 64:4096 + (k + 1) * 64],
                                        start=(k == 0), stop=(k == KC - 1))
                    vm.inc(inst)
                t.wait_ge(ve.h, ve0 + NT)
                t.wait_ge(ste.h, ste0 + 8)
                npart = 0
                for g, (scn, tc) in enumerate(CHUNKS):
                    partial = tc >= 4 * scn
                    if g >= 2:
                        t.wait_ge(ex.h, ex0 + g - 1)
                    sc.inc(t.matmul(
                        ctx.pb[g % 2][:, :],
                        ctx.Wb[0:16, 2048 + tc * 128:2048 + tc * 128 + 128],
                        ctx.Wb[0:16, scn * 512:(scn + 1) * 512],
                        start=True, stop=True))
                    if partial:
                        npart += 1
                        t.wait_ge(mk.h, mk0 + npart)
                    else:
                        t.wait_ge(ex.h, ex0 + g + 1)
                    if tc == 0 and scn >= 2:
                        t.wait_ge(fin.h, fin0 + scn - 1)
                    ym.inc(t.matmul(
                        ctx.pb[2 + scn % 2][0:65, :],
                        ctx.Wb[:, 6528 + tc * 65:6528 + tc * 65 + 65],
                        ctx.Wb[:, 4608 + (g % 2) * 512:4608 + (g % 2) * 512 + 512],
                        start=(tc == 0), stop=(tc == 4 * scn + 3),
                        skip_group_check=True))
                    if tc == 4 * scn + 3:
                        t.wait_ge(zr.h, zr0 + scn + 1)
                        if scn >= 2:
                            t.wait_ge(zbe.h, zbe0 + scn - 1)
                        zb.inc(t.matmul(ctx.pb[6 + scn % 2][0:64, :],
                                        ctx.ones_row[0:1, 0:64],
                                        ctx.W[0:1, 3600:4112],
                                        start=True, stop=True))

            @blk.scalar
            def _(s):
                sc0 = sc.n - NCH
                ym0s = ym.n - NCH
                for g, (scn, tc) in enumerate(CHUNKS):
                    s.wait_ge(sc.h, sc0 + g + 1)
                    if g >= 2:
                        s.wait_ge(ym.h, ym0s + g - 1)
                    ex.inc(s.activation(
                        ctx.Wb[:, 4608 + (g % 2) * 512:4608 + (g % 2) * 512 + 512],
                        ctx.pb[g % 2][:, :], AF.Exp,
                        bias=ctx.W[:, 6802:6803]))

            @blk.vector
            def _(v):
                stg0 = stg.n - 8
                vm0 = vm.n - NT
                ex0v = ex.n - NCH
                ym0v = ym.n - NCH
                zb0 = zb.n - 4
                for gq in range(8):
                    v.wait_ge(stg.h, stg0 + gq + 1)
                    bank = ctx.pb[6 + gq % 2]
                    wi, grp = divmod(gq, 4)
                    if wi == 0:
                        dst = ctx.Wb[0:16, grp * 512:grp * 512 + 512]
                    else:
                        dst = ctx.Wb[0:16, 2048 + grp * 512:2048 + grp * 512 + 512]
                    ste.inc(v.tensor_copy(dst, bank[0:16, 0:512]))
                for i in range(NT):
                    v.wait_ge(vm.h, vm0 + i + 1)
                    ve.inc(v.tensor_copy(ctx.Wb[:, 6528 + i * 65:6528 + i * 65 + 64],
                                         ctx.pb[i % 2][:, 0:64]))
                for g, (scn, tc) in enumerate(CHUNKS):
                    if tc >= 4 * scn:
                        v.wait_ge(ex.h, ex0v + g + 1)
                        esl = ctx.Wb[:, 4608 + (g % 2) * 512:
                                     4608 + (g % 2) * 512 + 512]
                        off = 5632 + 384 - 128 * (tc - 4 * scn)
                        mk.inc(v.tensor_tensor(esl, esl,
                                               ctx.Wb[:, off:off + 512],
                                               ALU.mult))
                    if tc == 4 * scn + 3:
                        v.wait_ge(ym.h, ym0v + g + 1)
                        ybank = ctx.pb[2 + scn % 2]
                        zr.inc(v.reciprocal(ctx.W[0:1, 3600:4112],
                                            ybank[64:65, :]))
                        v.wait_ge(zb.h, zb0 + scn + 1)
                        zbe.inc(v.tensor_copy(ctx.W[0:64, 2576:3088],
                                              ctx.pb[6 + scn % 2][0:64, :]))
                        v.wait_ge(zbe.h, zbe.n)
                        bslice = ctx.B[(h % 2) * 64:(h % 2) * 64 + 64,
                                       (h // 2) * T + scn * 512:
                                       (h // 2) * T + scn * 512 + 512]
                        inst = v.tensor_tensor(bslice, ybank[0:64, :],
                                               ctx.W[0:64, 2576:3088], ALU.mult)
                        ctx.fence(v, "f_dve", inst)
                        bvcol = ctx.W[(h % 2) * 64:(h % 2) * 64 + 64,
                                      7286 + h // 2:7287 + h // 2]
                        inst = v.tensor_scalar(bslice, bslice, bvcol, None,
                                               ALU.add)
                        fin.inc(inst)


def sub_l1_attn_a(ctx, x_in, x_out, b, w):
    emit_l1_attn_a_prep(ctx, w)
    emit_in_ln_tp(ctx, x_in, b)
    emit_l1_attn_a_mix(ctx, b, w, b == 0)
    proj = dict(
        kchunks=8, kpart=128,
        w_src=lambda of: ap_off(w["Wo"], of * 128,
                                [[D, 128], [128 * D, KC], [1, 128]]),
        src=lambda k, t0, n: ctx.b_fm(k, t0, n),
        bias=w["bo"],
    )
    emit_out_phase(ctx, x_in, x_out, b, proj=proj)


SUBLAYERS["l1_attn_a"] = sub_l1_attn_a





def emit_l0_attn_a_prep(ctx, w):
    nc = ctx.nc
    ld = ctx.sem("da_pld")
    pr = ctx.sem("da_pr")
    with nc.Block() as blk:
        @blk.sync
        def _(sync):
            for i, nm in enumerate(["lambda_q1", "lambda_k1", "lambda_q2",
                                    "lambda_k2"]):
                ld.inc(sync.dma_start(out=ctx.R[0:1, i * 64:(i + 1) * 64],
                                      in_=w[nm].rearrange("(o c) -> o c", o=1)), 16)
            sync.wait_ge(ld.h, ld.n)
    with nc.Block() as blk:
        @blk.vector
        def _(v):
            prb = pr.n
            inst = v.tensor_tensor(ctx.R[0:1, 512:576], ctx.R[0:1, 0:64],
                                   ctx.R[0:1, 64:128], ALU.mult)
            ctx.fence(v, "f_dve", inst)
            v.tensor_reduce(ctx.R[0:1, 256:257], ctx.R[0:1, 512:576],
                            axis=AX.X, op=ALU.add)
            inst = v.tensor_tensor(ctx.R[0:1, 576:640], ctx.R[0:1, 128:192],
                                   ctx.R[0:1, 192:256], ALU.mult)
            ctx.fence(v, "f_dve", inst)
            inst = v.tensor_reduce(ctx.R[0:1, 257:258], ctx.R[0:1, 576:640],
                                   axis=AX.X, op=ALU.add)
            pr.inc(inst)
            v.wait_ge(pr.h, prb + 2)
            inst = v.tensor_tensor(ctx.R[0:1, 259:260], ctx.R[0:1, 259:260],
                                   ctx.R[0:1, 260:261], ALU.subtract)
            ctx.fence(v, "f_dve", inst)
            v.tensor_scalar(ctx.R[0:1, 258:259], ctx.R[0:1, 259:260],
                            0.8, None, ALU.add)
        @blk.scalar
        def _(s):
            s.wait_ge(pr.h, pr.n - 1 + 1)
            pr.inc(s.activation(ctx.R[0:1, 259:261], ctx.R[0:1, 256:258],
                                AF.Exp))


def emit_l0_attn_a_mix(ctx, b, w):
    nc = ctx.nc
    SCALE = 0.125
    lam = ctx.sem("da_lam")
    lme = ctx.sem("da_lme")
    nwl = ctx.sem("da_nwl")
    ldw = ctx.sem("da_ldw")
    qm = ctx.sem("da_qm")
    qv = ctx.sem("da_qv")
    nwb = ctx.sem("da_nw")
    nwe = ctx.sem("da_nwe")
    vt = ctx.sem("da_vt")
    vte = ctx.sem("da_vte")
    sc = ctx.sem("da_sc")
    ex = ctx.sem("da_ex")
    ym = ctx.sem("da_ym")
    cb = ctx.sem("da_cb")
    cbe = ctx.sem("da_cbe")
    ytp = ctx.sem("da_ytp")
    rm1 = ctx.sem("da_rm1")
    rm2 = ctx.sem("da_rm2")
    rm3 = ctx.sem("da_rm3")

    with nc.Block() as blk:
        @blk.sync
        def _(sync):
            nwl.inc(sync.dma_start(out=ctx.W[0:1, 4692:5716],
                                   in_=ap_off(w["norm_w"], 0,
                                              [[1024, 1], [1, 1024]])), 16)
            sync.wait_ge(nwl.h, nwl.n)
        @blk.tensor
        def _(t):
            lam.inc(t.matmul(ctx.pb[7][:, 256:257], ctx.ones_row[0:1, 0:128],
                             ctx.R[0:1, 258:259], start=True, stop=True))
        @blk.vector
        def _(v):
            v.wait_ge(lam.h, lam.n)
            lme.inc(v.tensor_copy(ctx.S[:, 1800:1801], ctx.pb[7][:, 256:257]))
        @blk.gpsimd
        def _(g):
            g.memset(bass.AP(ctx.Wb, 2560 + 64, [[8192, 128], [65, 16]]), 1.0)
    with nc.Block() as blk:
        @blk.vector
        def _(v):
            v.tensor_scalar(ctx.W[0:1, 4692:5716], ctx.W[0:1, 4692:5716],
                            0.2, None, ALU.mult)

    for h in range(16):
        with nc.Block() as blk:
            @blk.sync
            def _(sync):
                for wi in range(3):
                    srcw = ap_off(w["w_qkv"], wi * 1024 + h * 64,
                                  [[3072, 128], [128 * 3072, KC], [1, 64]])
                    ldw.inc(sync.dma_start(
                        out=ctx.W[:, wi * 512:wi * 512 + 512], in_=srcw), 16)
                sync.wait_ge(ldw.h, ldw.n)

            @blk.gpsimd
            def _(g):
                dct = ctx.sem("da_ct")
                g.wait_ge(ldw.h, ldw.n)
                dct.inc(g.tensor_copy(ctx.Wb[:, 0:1536], ctx.W[:, 0:1536]))

            @blk.tensor
            def _(t):
                qv0, vte0, nwe0 = qv.n, vte.n, nwe.n
                ex0, ym0 = ex.n, ym.n
                cbe0, ytp0, rm10, rm30 = cbe.n, ytp.n, rm1.n, rm3.n
                rm20t = ctx.sem("da_rm2").n
                t.wait_ge(ctx.sem("da_ct").h, ctx.sem("da_ct").n)
                nwb.inc(t.matmul(ctx.pb[7][:, 320:384],
                                 ctx.ones_row[0:1, 0:128],
                                 ctx.W[0:1, 4692 + h * 64:4692 + h * 64 + 64],
                                 start=True, stop=True))
                for g in range(3):
                    for ncn in range(4):
                        gi = g * 4 + ncn
                        if gi >= 2:
                            t.wait_ge(qv.h, qv0 + gi - 1)
                        pslice = ctx.pb[gi % 2][0:64, :] if g < 2 else \
                            ctx.pb[gi % 2][64:128, :]
                        for k in range(KC):
                            inst = t.matmul(
                                pslice,
                                ctx.Wb[:, g * 512 + k * 64:g * 512 + (k + 1) * 64],
                                ctx.a_fm(k, ncn * 512, 512),
                                start=(k == 0), stop=(k == KC - 1))
                        qm.inc(inst)
                t.wait_ge(qv.h, qv0 + 12)
                for tc in range(NT):
                    g2 = tc // 4
                    if g2 >= 2:
                        t.wait_ge(vte.h, vte0 + g2 - 1)
                    inst = t.transpose(
                        ctx.pb[4 + g2 % 2][:, (tc % 4) * 128:(tc % 4) * 128 + 64],
                        ctx.C[64:128, tc * 128:(tc + 1) * 128],
                        ctx.ident[64:128, 64:128])
                    if tc % 4 == 3:
                        vt.inc(inst)
                t.wait_ge(vte.h, vte0 + 4)
                t.wait_ge(nwe.h, nwe0 + 1)
                nit = 0
                for scn in range(4):
                    for mat in range(2):
                        ybank = 2 + mat
                        for tc in range(NT):
                            git = nit * NT + tc
                            if git >= 2:
                                t.wait_ge(ex.h, ex0 + git - 1)
                            inst = t.matmul(
                                ctx.pb[tc % 2][:, :],
                                ctx.Wb[mat * 32:mat * 32 + 32,
                                       5648 + tc * 128:5648 + (tc + 1) * 128],
                                ctx.Wb[mat * 32:mat * 32 + 32,
                                       3600 + scn * 512:3600 + (scn + 1) * 512],
                                start=True, stop=True)
                            sc.inc(inst)
                            if tc >= 1:
                                t.wait_ge(ex.h, ex0 + git)
                                ti = tc - 1
                                inst = t.matmul(
                                    ctx.pb[ybank][0:65, :],
                                    ctx.Wb[:, 2560 + ti * 65:2560 + ti * 65 + 65],
                                    ctx.Wb[:, 1536 + (ti % 2) * 512:
                                           1536 + (ti % 2) * 512 + 512],
                                    start=(ti == 0), stop=False,
                                    skip_group_check=True)
                        t.wait_ge(ex.h, ex0 + nit * NT + NT)
                        inst = t.matmul(
                            ctx.pb[ybank][0:65, :],
                            ctx.Wb[:, 2560 + 15 * 65:2560 + 15 * 65 + 65],
                            ctx.Wb[:, 1536 + (15 % 2) * 512:
                                   1536 + (15 % 2) * 512 + 512],
                            start=False, stop=True, skip_group_check=True)
                        ym.inc(inst)
                        nit += 1
                    t.wait_ge(rm1.h, rm10 + scn + 1)
                    t.wait_ge(cbe.h, cbe0 + max(0, (scn - 1) * 2))
                    cb.inc(t.matmul(ctx.pb[6][0:64, :], ctx.ones_row[0:1, 0:64],
                                    ctx.S[0:1, 1280:1792], start=True,
                                    stop=True))
                    cb.inc(t.matmul(ctx.pb[7][0:64, 0:512],
                                    ctx.ones_row[0:1, 0:64],
                                    ctx.W[0:1, 4112:4624], start=True,
                                    stop=True))
                    t.wait_ge(cbe.h, cbe0 + scn * 2 + 2)
                    if scn >= 1:
                        t.wait_ge(rm2.h, rm20t + (scn - 1) * 4 + 4)
                    for q4 in range(4):
                        inst = t.transpose(
                            ctx.pb[4][:, q4 * 128:q4 * 128 + 64],
                            ctx.W[0:64, 3600 + q4 * 128:3600 + (q4 + 1) * 128],
                            ctx.ident[0:64, 0:64])
                    ytp.inc(inst)

            @blk.scalar
            def _(s):
                sc0 = sc.n - 4 * 2 * NT
                rm20 = rm2.n
                rstd = ctx.sem("da_rstd")

                def rms_chain(scn2):
                    for q4 in range(4):
                        s.wait_ge(rm2.h, rm20 + scn2 * 4 + q4 + 1)
                        buf = ctx.W[:, 5716 + (scn2 % 2) * 256 + q4 * 64:
                                    5716 + (scn2 % 2) * 256 + q4 * 64 + 64]
                        inst = s.activation(ctx.W[:, 6228:6292], buf,
                                            AF.Square,
                                            accum_out=ctx.W[:, 6292:6293])
                        ctx.fence(s, "f_act", inst)
                        inst = s.activation(ctx.W[:, 6293:6294],
                                            ctx.W[:, 6292:6293], AF.Ln,
                                            scale=1.0 / 64.0,
                                            bias=ctx.ceps[:, :])
                        ctx.fence(s, "f_act", inst)
                        rstd.inc(s.activation(
                            ctx.W[:, 6294 + (scn2 % 2) * 4 + q4:
                                  6295 + (scn2 % 2) * 4 + q4],
                            ctx.W[:, 6293:6294], AF.Exp, scale=-0.5))

                nit = 0
                for scn in range(4):
                    for mat in range(2):
                        for tc in range(NT):
                            git = nit * NT + tc
                            s.wait_ge(sc.h, sc0 + git + 1)
                            ex.inc(s.activation(
                                ctx.Wb[:, 1536 + (tc % 2) * 512:
                                       1536 + (tc % 2) * 512 + 512],
                                ctx.pb[tc % 2][:, :], AF.Exp, scale=SCALE))
                        nit += 1
                    if scn >= 1:
                        rms_chain(scn - 1)
                rms_chain(3)

            @blk.vector
            def _(v):
                qm0 = qm.n - 12
                vt0 = vt.n - 4
                nwb0 = nwb.n - 1
                ym0 = ym.n - 8
                cb0 = cb.n - 8
                ytp0 = ytp.n - 4
                rm20 = rm2.n
                for gi in range(12):
                    g, ncn = divmod(gi, 4)
                    v.wait_ge(qm.h, qm0 + gi + 1)
                    if g == 0:
                        dst = ctx.Wb[0:64, 3600 + ncn * 512:3600 + ncn * 512 + 512]
                        srcp = ctx.pb[gi % 2][0:64, :]
                    elif g == 1:
                        dst = ctx.Wb[0:64, 5648 + ncn * 512:5648 + ncn * 512 + 512]
                        srcp = ctx.pb[gi % 2][0:64, :]
                    else:
                        dst = ctx.C[64:128, ncn * 512:ncn * 512 + 512]
                        srcp = ctx.pb[gi % 2][64:128, :]
                    qv.inc(v.tensor_copy(dst, srcp))
                v.wait_ge(nwb.h, nwb0 + 1)
                nwe.inc(v.tensor_copy(ctx.W[:, 4624:4688], ctx.pb[7][:, 320:384]))
                for g2 in range(4):
                    v.wait_ge(vt.h, vt0 + g2 + 1)
                    bank = ctx.pb[4 + g2 % 2]
                    srcv = bass.AP(bank, 0, [[512, 128], [128, 4], [1, 64]])
                    dstv = bass.AP(ctx.Wb, 2560 + g2 * 4 * 65,
                                   [[8192, 128], [65, 4], [1, 64]])
                    vte.inc(v.tensor_copy(dstv, srcv))
                rstd = ctx.sem("da_rstd")
                rstd0 = rstd.n - 16

                def rstd_mults(scn2):
                    for q4 in range(4):
                        si = scn2 * 4 + q4
                        v.wait_ge(rstd.h, rstd0 + scn2 * 4 + q4 + 1)
                        buf = ctx.W[:, 5716 + (scn2 % 2) * 256 + q4 * 64:
                                    5716 + (scn2 % 2) * 256 + q4 * 64 + 64]
                        inst = v.tensor_scalar(buf, buf,
                                               ctx.W[:, 6294 + (scn2 % 2) * 4 + q4:
                                                     6295 + (scn2 % 2) * 4 + q4],
                                               None, ALU.mult)
                        ctx.fence(v, "f_dve", inst)
                        dstB = bass.AP(ctx.B,
                                       (si % 8) * T + 2 * h + (si // 8),
                                       [[KC * T, 128], [32, 64]])
                        inst = v.tensor_tensor(dstB, buf,
                                               ctx.W[:, 4624:4688], ALU.mult)
                        rm3.inc(inst)

                for scn in range(4):
                    v.wait_ge(ym.h, ym0 + scn * 2 + 2)
                    inst = v.reciprocal(ctx.S[0:1, 1280:1792],
                                        ctx.pb[2][64:65, :])
                    ctx.fence(v, "f_dve", inst)
                    inst = v.reciprocal(ctx.W[0:1, 4112:4624],
                                        ctx.pb[3][64:65, :])
                    ctx.fence(v, "f_dve", inst)
                    inst = v.tensor_scalar(ctx.W[0:1, 4112:4624],
                                           ctx.W[0:1, 4112:4624],
                                           ctx.S[0:1, 1800:1801], None,
                                           ALU.mult)
                    rm1.inc(inst)
                    v.wait_ge(cb.h, cb0 + scn * 2 + 2)
                    inst = v.tensor_copy(ctx.W[0:64, 2576:3088],
                                         ctx.pb[6][0:64, :])
                    cbe.inc(inst)
                    inst = v.tensor_copy(ctx.W[0:64, 3088:3600],
                                         ctx.pb[7][0:64, 0:512])
                    ctx.fence(v, "f_dve", inst)
                    inst = v.tensor_tensor(ctx.W[0:64, 3600:4112],
                                           ctx.pb[2][0:64, :],
                                           ctx.W[0:64, 2576:3088], ALU.mult)
                    ctx.fence(v, "f_dve", inst)
                    inst = v.tensor_tensor(ctx.W[0:64, 2576:3088],
                                           ctx.pb[3][0:64, :],
                                           ctx.W[0:64, 3088:3600], ALU.mult)
                    ctx.fence(v, "f_dve", inst)
                    inst = v.tensor_tensor(ctx.W[0:64, 3600:4112],
                                           ctx.W[0:64, 3600:4112],
                                           ctx.W[0:64, 2576:3088], ALU.subtract)
                    cbe.inc(inst)
                    v.wait_ge(ytp.h, ytp0 + scn + 1)
                    for q4 in range(4):
                        buf = ctx.W[:, 5716 + (scn % 2) * 256 + q4 * 64:
                                    5716 + (scn % 2) * 256 + q4 * 64 + 64]
                        rm2.inc(v.tensor_copy(buf, ctx.pb[4][:, q4 * 128:
                                                             q4 * 128 + 64]))
                    if scn >= 1:
                        rstd_mults(scn - 1)
                rstd_mults(3)


def sub_l0_attn_a(ctx, x_in, x_out, b, w):
    if b == 0:
        emit_l0_attn_a_prep(ctx, w)
    emit_in_ln_tp(ctx, x_in, b)
    emit_l0_attn_a_mix(ctx, b, w)
    proj = dict(
        kchunks=8, kpart=128,
        w_src=lambda of: ap_off(w["wo"], of * 128,
                                [[D, 128], [128 * D, KC], [1, 128]]),
        src=lambda k, t0, n: ctx.b_fm(k, t0, n),
        bias=None,
    )
    emit_out_phase(ctx, x_in, x_out, b, proj=proj)


SUBLAYERS["l0_attn_a"] = sub_l0_attn_a





def emit_l1_attn_b_masks(ctx):
    nc = ctx.nc
    mk = ctx.sem("rk_mk")
    mke = ctx.sem("rk_mke")
    SC = 7042
    for li, lv in enumerate([1, 2, 4, 8, 16, 32, 64]):
        with nc.Block() as blk:
            @blk.gpsimd
            def _(g):
                g.iota(ctx.W[0:64, SC:SC + 128], [[1, 128]],
                       channel_multiplier=-2 * lv,
                       allow_small_or_imprecise_dtypes=True)
        with nc.Block() as blk:
            @blk.vector
            def _(v):
                it = ctx.W[0:64, SC:SC + 128]
                lb = ctx.W[0:64, SC + 128:SC + 256]
                ub = ctx.W[0:64, SC + 256:SC + 384]
                t1 = ctx.W[0:64, SC + 384:SC + 512]
                inst = v.tensor_scalar(t1, it, -0.5, None, ALU.is_ge)
                ctx.fence(v, "f_dve", inst)
                inst = v.tensor_scalar(lb, it, lv - 0.5, None, ALU.is_le)
                ctx.fence(v, "f_dve", inst)
                inst = v.tensor_tensor(lb, lb, t1, ALU.mult)
                ctx.fence(v, "f_dve", inst)
                inst = v.tensor_scalar(ub, it, lv - 0.5, None, ALU.is_ge)
                ctx.fence(v, "f_dve", inst)
                inst = v.tensor_scalar(t1, it, 2 * lv - 0.5, None, ALU.is_le)
                ctx.fence(v, "f_dve", inst)
                mk.inc(v.tensor_tensor(ub, ub, t1, ALU.mult))
        with nc.Block() as blk:
            @blk.tensor
            def _(t):
                mke.inc(t.matmul(ctx.pb[6][:, 0:128],
                                 ctx.W[0:64, SC + 128:SC + 256],
                                 ctx.W[0:64, SC + 256:SC + 384],
                                 start=True, stop=True))
            @blk.vector
            def _(v):
                v.wait_ge(mke.h, mke.n)
                v.tensor_copy(ctx.W[:, (li + 1) * 128:(li + 2) * 128],
                              ctx.pb[6][:, 0:128])
                if li == 0:
                    v.tensor_copy(ctx.W[:, 0:128], ctx.ident[:, :])


def emit_l1_attn_b_mix(ctx, b, w):
    nc = ctx.nc
    LV = [1, 2, 4, 8, 16, 32, 64]
    NCH = 16
    ld = ctx.sem("rk_ld")
    dtp = ctx.sem("rk_dtp")
    wtp = ctx.sem("rk_wtp")
    wte = ctx.sem("rk_wte")
    pm = ctx.sem("rk_pm")
    pv = ctx.sem("rk_pv")
    sig = ctx.sem("rk_sig")
    ktp = ctx.sem("rk_ktp")
    kte = ctx.sem("rk_kte")
    tbl = ctx.sem("rk_tbl")
    fac = ctx.sem("rk_fac")
    lm = ctx.sem("rk_lm")
    la = ctx.sem("rk_la")
    ym = ctx.sem("rk_ym")
    yev = ctx.sem("rk_yev")
    smm = ctx.sem("rk_sm")
    sev = ctx.sem("rk_sev")

    emit_l1_attn_b_masks(ctx)

    with nc.Block() as blk:
        @blk.sync
        def _(sync):
            ld.inc(sync.dma_start(out=ctx.W[0:16, 7042:7106],
                                  in_=w["time_decay"]), 16)
            ld.inc(sync.dma_start(out=ctx.W[0:16, 7106:7170],
                                  in_=w["time_first"]), 16)
            sync.wait_ge(ld.h, ld.n)
    with nc.Block() as blk:
        @blk.tensor
        def _(t):
            t.transpose(ctx.pb[6][0:64, 0:16], ctx.W[0:16, 7042:7106],
                        ctx.ident[0:16, 0:16])
            dtp.inc(t.transpose(ctx.pb[6][0:64, 16:32], ctx.W[0:16, 7106:7170],
                                ctx.ident[0:16, 0:16]))
        @blk.vector
        def _(v):
            v.wait_ge(dtp.h, dtp.n)
            v.tensor_copy(ctx.S[0:64, 224:240], ctx.pb[6][0:64, 0:16])
            v.tensor_copy(ctx.S[0:64, 240:256], ctx.pb[6][0:64, 16:32])

    for h in range(16):
        with nc.Block() as blk:
            @blk.sync
            def _(sync):
                wtp0 = wtp.n
                for wi in range(3):
                    nm = ["Wk", "Wv", "Wr"][wi]
                    if wi >= 1:
                        sync.wait_ge(wtp.h, wtp0 + wi * 2)
                    ld.inc(sync.dma_start(
                        out=ctx.W[0:64, 7042:8066],
                        in_=ap_off(w[nm], h * 64 * D, [[D, 64], [1, D]])), 16)

            @blk.tensor
            def _(t):
                ld0 = ld.n - 3 * 16
                wte0 = wte.n
                for wi in range(3):
                    t.wait_ge(ld.h, ld0 + 16 * (wi + 1))
                    if wi >= 1:
                        t.wait_ge(wte.h, wte0 + wi * 2)
                    for k in range(KC):
                        if k == 4:
                            t.wait_ge(wte.h, wte0 + wi * 2 + 1)
                        inst = t.transpose(
                            ctx.pb[6 + k % 2][:, (k // 2 % 2) * 128:
                                              (k // 2 % 2) * 128 + 64],
                            ctx.W[0:64, 7042 + k * 128:7042 + (k + 1) * 128],
                            ctx.ident[0:64, 0:64])
                        if k % 4 == 3:
                            wtp.inc(inst)

            @blk.vector
            def _(v):
                wtp0 = wtp.n - 6
                for wi in range(3):
                    for g2 in range(2):
                        v.wait_ge(wtp.h, wtp0 + wi * 2 + g2 + 1)
                        for k in range(g2 * 4, g2 * 4 + 4):
                            inst = v.tensor_copy(
                                ctx.Wb[:, 4096 + wi * 512 + k * 64:
                                       4096 + wi * 512 + (k + 1) * 64],
                                ctx.pb[6 + k % 2][:, (k // 2 % 2) * 128:
                                                  (k // 2 % 2) * 128 + 64])
                        wte.inc(inst)

        with nc.Block() as blk:
            @blk.tensor
            def _(t):
                pv0, kte0, tbl0 = pv.n, kte.n, tbl.n
                fac0, la0, yev0, sev0 = fac.n, la.n, yev.n, sev.n
                for g in range(3):
                    for ncn in range(4):
                        gi = g * 4 + ncn
                        if gi >= 2:
                            t.wait_ge(pv.h, pv0 + gi - 1)
                        for k in range(KC):
                            inst = t.matmul(
                                ctx.pb[gi % 2][0:64, :],
                                ctx.Wb[:, 4096 + g * 512 + k * 64:
                                       4096 + g * 512 + (k + 1) * 64],
                                ctx.a_fm(k, ncn * 512, 512),
                                start=(k == 0), stop=(k == KC - 1))
                        pm.inc(inst)
                t.wait_ge(pv.h, pv0 + 12)
                for g in range(2):
                    for tc in range(NCH):
                        g2 = (g * NCH + tc) // 4
                        if g2 >= 2:
                            t.wait_ge(kte.h, kte0 + g2 - 1)
                        inst = t.transpose(
                            ctx.pb[6 + g2 % 2][:, (tc % 4) * 128:(tc % 4) * 128 + 64],
                            ctx.C[0:64, g * 2048 + tc * 128:g * 2048 + (tc + 1) * 128],
                            ctx.ident[0:64, 0:64])
                        if tc % 4 == 3:
                            ktp.inc(inst)
                t.wait_ge(tbl.h, tbl0 + 1)
                t.wait_ge(kte.h, kte0 + 8)
                ktp.inc(t.transpose(ctx.pb[6][:, 448:512],
                                    ctx.W[0:64, 2946:3074],
                                    ctx.ident[0:64, 0:64]))
                t.wait_ge(kte.h, kte0 + 9)
                for c in range(NCH):
                    t.wait_ge(fac.h, fac0 + c + 1)
                    if c >= 1:
                        t.wait_ge(la.h, la0 + c)
                    for li in range(8):
                        bank = ctx.pb[li // 4]
                        qof = (li % 4) * 128
                        if li == 0:
                            lhs = ctx.W[0:64, 3394:3522]
                            rhs = ctx.C[0:64, 4096 + c * 128:4096 + (c + 1) * 128]
                        else:
                            lhs = ctx.W[0:64, 3522 + (li - 1) * 128:
                                        3522 + li * 128]
                            rhs = ctx.W[0:64, 4418 + (li - 1) * 128:
                                        4418 + li * 128]
                        inst = t.matmul(bank[:, qof:qof + 128], lhs, rhs,
                                        start=(li % 4 == 0), stop=True,
                                        skip_group_check=True)
                    lm.inc(inst)
                    t.wait_ge(la.h, la0 + c + 1)
                    if c >= 1:
                        t.wait_ge(yev.h, yev0 + c)
                        t.wait_ge(sev.h, sev0 + c)
                    t.matmul(ctx.pb[2][0:64, 0:128],
                             ctx.C[:, 7168 + c * 64:7168 + (c + 1) * 64],
                             ctx.W[:, 3138:3266], start=True, stop=False,
                             skip_group_check=True)
                    ym.inc(t.matmul(ctx.pb[2][0:64, 0:128],
                                    ctx.W[0:64, 3266 + (c % 2) * 64:
                                          3330 + (c % 2) * 64],
                                    ctx.W[0:64, 5378:5506],
                                    start=False, stop=True,
                                    skip_group_check=True))
                    if c >= 1:
                        t.wait_ge(sev.h, sev0 + c)
                    smm.inc(t.matmul(ctx.pb[5][0:64, 0:64],
                                     ctx.W[:, 5314:5378],
                                     ctx.C[:, 7168 + c * 64:7168 + (c + 1) * 64],
                                     start=True, stop=True))

            @blk.scalar
            def _(s):
                pm0 = pm.n - 12
                for ncn in range(4):
                    gi = 8 + ncn
                    s.wait_ge(pm.h, pm0 + gi + 1)
                    sig.inc(s.activation(
                        ctx.C[0:64, 4096 + ncn * 512:4096 + (ncn + 1) * 512],
                        ctx.pb[gi % 2][0:64, :], AF.Exp, scale=-1.0))

            @blk.vector
            def _(v):
                pm0 = pm.n - 12
                sg0 = sig.n - 4
                ktp0 = ktp.n - 9
                lm0 = lm.n - NCH
                ym0 = ym.n - NCH
                sm0 = smm.n - NCH
                for gi in range(8):
                    g, ncn = divmod(gi, 4)
                    v.wait_ge(pm.h, pm0 + gi + 1)
                    pv.inc(v.tensor_copy(
                        ctx.C[0:64, g * 2048 + ncn * 512:g * 2048 + (ncn + 1) * 512],
                        ctx.pb[gi % 2][0:64, :]))
                for ncn in range(4):
                    v.wait_ge(sig.h, sg0 + ncn + 1)
                    dst = ctx.C[0:64, 4096 + ncn * 512:4096 + (ncn + 1) * 512]
                    inst = v.tensor_scalar(dst, dst, 1.0, None, ALU.add)
                    ctx.fence(v, "f_dve", inst)
                    pv.inc(v.reciprocal(dst, dst))
                dcol = ctx.S[0:64, 224 + h:225 + h]
                tcol = ctx.S[0:64, 240 + h:241 + h]
                dp = lambda a, bb: ctx.W[0:64, 2816 + a:2816 + bb]
                rp = lambda a, bb: ctx.W[0:64, 2945 + a:2945 + bb]
                v.memset(dp(0, 1), 1.0)
                inst = v.tensor_scalar(dp(1, 2), dcol, 1.0, None, ALU.mult)
                ctx.fence(v, "f_dve", inst)
                p = 1
                while p < 128:
                    q = min(p, 128 - p)
                    inst = v.tensor_scalar(dp(p + 1, p + 1 + q), dp(1, 1 + q),
                                           dp(p, p + 1), None, ALU.mult)
                    ctx.fence(v, "f_dve", inst)
                    p *= 2
                inst = v.tensor_scalar(rp(128, 129), tcol, 1.0, None,
                                       ALU.mult)
                ctx.fence(v, "f_dve", inst)
                inst = v.tensor_scalar(rp(127, 128), rp(128, 129),
                                       dp(1, 2), None, ALU.mult)
                ctx.fence(v, "f_dve", inst)
                p = 2
                while p <= 64:
                    inst = v.tensor_scalar(rp(129 - 2 * p, 129 - p),
                                           rp(129 - p, 129), dp(p, p + 1),
                                           None, ALU.mult)
                    if p < 64:
                        ctx.fence(v, "f_dve", inst)
                    p *= 2
                tbl.inc(inst)
                v.wait_ge(tbl.h, tbl.n)
                for li, lv in enumerate(LV):
                    pr = ctx.W[0:64, 1024 + li * 128:1024 + (li + 1) * 128]
                    pk = ctx.W[0:64, 1920 + li * 128:1920 + (li + 1) * 128]
                    v.memset(pr, 0.0)
                    inst = v.memset(pk, 0.0)
                    ctx.fence(v, "f_dve", inst)
                    nb = 128 // (2 * lv)
                    prv = bass.AP(pr.tensor, pr.offset + lv,
                                  [[8192, 64], [2 * lv, nb], [1, lv]])
                    dpv = bass.AP(ctx.W, 2816, [[8192, 64], [0, nb], [1, lv]])
                    v.tensor_scalar(prv, dpv, 1.0, None, ALU.mult)
                    pkv = bass.AP(pk.tensor, pk.offset,
                                  [[8192, 64], [2 * lv, nb], [1, lv]])
                    rpv = bass.AP(ctx.W, 2945 + 128 - lv,
                                  [[8192, 64], [0, nb], [1, lv]])
                    inst = v.tensor_scalar(pkv, rpv, 1.0, None, ALU.mult)
                ctx.fence(v, "f_dve", inst)
                for g2 in range(8):
                    v.wait_ge(ktp.h, ktp0 + g2 + 1)
                    bank = ctx.pb[6 + g2 % 2]
                    base = 6144 if g2 < 4 else 7168
                    gg = g2 % 4
                    srcv = bass.AP(bank, 0, [[512, 128], [128, 4], [1, 64]])
                    dstv = bass.AP(ctx.C, base + gg * 4 * 64,
                                   [[8192, 128], [64, 4], [1, 64]])
                    kte.inc(v.tensor_copy(dstv, srcv))
                v.wait_ge(ktp.h, ktp0 + 9)
                kte.inc(v.tensor_copy(ctx.W[:, 3074:3138], ctx.pb[6][:, 448:512]))
                v.memset(ctx.W[0:64, 3266:3330], 0.0)
                inst = v.memset(ctx.W[0:64, 3330:3394], 0.0)
                ctx.fence(v, "f_dve", inst)
                for c in range(NCH):
                    kch = ctx.C[0:64, c * 128:(c + 1) * 128]
                    rch = ctx.C[0:64, 4096 + c * 128:4096 + (c + 1) * 128]
                    ktk = ctx.C[:, 6144 + c * 64:6144 + (c + 1) * 64]
                    v.tensor_scalar(ctx.W[0:64, 3394:3522], kch,
                                    rp(128, 129), None, ALU.mult)
                    kbc = bass.AP(ctx.C, c * 128, [[8192, 64], [0, 7], [1, 128]])
                    v.tensor_tensor(ctx.W[0:64, 3522:4418], kbc,
                                    ctx.W[0:64, 1920:2816], ALU.mult)
                    rbc = bass.AP(ctx.C, 4096 + c * 128,
                                  [[8192, 64], [0, 7], [1, 128]])
                    v.tensor_tensor(ctx.W[0:64, 4418:5314], rbc,
                                    ctx.W[0:64, 1024:1920], ALU.mult)
                    v.tensor_tensor(ctx.W[:, 5314:5378], ktk,
                                    ctx.W[:, 3074:3138], ALU.mult)
                    inst = v.tensor_tensor(ctx.W[0:64, 5378:5506], rch,
                                           dp(1, 129), ALU.mult)
                    fac.inc(inst)
                    v.wait_ge(lm.h, lm0 + c + 1)
                    v.tensor_tensor(ctx.S[:, 512:1024], ctx.pb[0][:, 0:512],
                                    ctx.W[:, 0:512], ALU.mult)
                    inst = v.tensor_tensor(ctx.S[:, 1024:1536],
                                           ctx.pb[1][:, 0:512],
                                           ctx.W[:, 512:1024], ALU.mult)
                    ctx.fence(v, "f_dve", inst)
                    inst = v.tensor_tensor(ctx.S[:, 512:1024],
                                           ctx.S[:, 512:1024],
                                           ctx.S[:, 1024:1536], ALU.add)
                    ctx.fence(v, "f_dve", inst)
                    inst = v.tensor_tensor(ctx.S[:, 512:768],
                                           ctx.S[:, 512:768],
                                           ctx.S[:, 768:1024], ALU.add)
                    ctx.fence(v, "f_dve", inst)
                    inst = v.tensor_tensor(ctx.W[:, 3138:3266],
                                           ctx.S[:, 512:640],
                                           ctx.S[:, 640:768], ALU.add)
                    la.inc(inst)
                    v.wait_ge(ym.h, ym0 + c + 1)
                    yev.inc(v.tensor_copy(
                        ctx.B[(h % 2) * 64:(h % 2) * 64 + 64,
                              (h // 2) * T + c * 128:(h // 2) * T + (c + 1) * 128],
                        ctx.pb[2][0:64, 0:128]))
                    v.wait_ge(smm.h, sm0 + c + 1)
                    inst = v.tensor_scalar(ctx.S[0:64, 1536:1600],
                                           ctx.W[0:64, 3266 + (c % 2) * 64:
                                                 3330 + (c % 2) * 64],
                                           dp(128, 129), None, ALU.mult)
                    ctx.fence(v, "f_dve", inst)
                    inst = v.tensor_tensor(ctx.W[0:64, 3266 + ((c + 1) % 2) * 64:
                                                 3330 + ((c + 1) % 2) * 64],
                                           ctx.pb[5][0:64, 0:64],
                                           ctx.S[0:64, 1536:1600], ALU.add)
                    sev.inc(inst)
                    v.wait_ge(sev.h, sev.n)


def sub_l1_attn_b(ctx, x_in, x_out, b, w):
    emit_in_ln_tp(ctx, x_in, b)
    emit_l1_attn_b_mix(ctx, b, w)
    proj = dict(
        kchunks=8, kpart=128, transpose=True,
        w_rows=lambda of: ap_off(w["Wo"], of * 128 * D, [[D, 128], [1, D]]),
        src=lambda k, t0, n: ctx.b_fm(k, t0, n),
        bias=None,
    )
    emit_out_phase(ctx, x_in, x_out, b, proj=proj)


SUBLAYERS["l1_attn_b"] = sub_l1_attn_b





CHAIN = [
    ("l0_attn_a", "L0_attn_a", ["w_qkv", "wo", "lambda_q1", "lambda_q2",
                                "lambda_k1", "lambda_k2", "norm_w"]),
    ("l0_ffn", "L0_ffn", ["gate_w", "gate_b", "sw1", "sb1", "sw2", "sb2",
                          "rw1", "rb1", "rw2", "rb2"]),
    ("l0_attn_b", "L0_attn_b", ["w_qkv", "w_proj", "b_proj"]),
    ("l0_ffn", "L0_ffn", ["gate_w", "gate_b", "sw1", "sb1", "sw2", "sb2",
                          "rw1", "rb1", "rw2", "rb2"]),
    ("l1_attn_a", "L1_attn_a", ["Wq", "bq", "Wk", "bk", "Wv", "bv", "Wo",
                                "bo", "q_phase", "k_phase", "temperature"]),
    ("l1_ffn", "L1_ffn", ["Win", "bin_", "W1", "b1", "W2", "b2", "Wout",
                          "bout"]),
    ("l1_attn_b", "L1_attn_b", ["Wk", "Wv", "Wr", "Wo", "time_decay",
                                "time_first"]),
    ("l1_ffn", "L1_ffn", ["Win", "bin_", "W1", "b1", "W2", "b2", "Wout",
                          "bout"]),
]



def build_fused(wshapes_by_stage):
    nc = bass.Bass("TRN2", target_bir_lowering=False)
    x_in = nc.dram_tensor("x", [2, T, D], F32, kind="ExternalInput").ap()
    y_out = nc.dram_tensor("y", [2, T, D], F32, kind="ExternalOutput").ap()
    xa = nc.dram_tensor("xa", [2, T, D], F32, kind="Internal").ap()
    xb = nc.dram_tensor("xb", [2, T, D], F32, kind="Internal").ap()
    wsc = nc.dram_tensor("wsc", [192 * 128, 2048], BF, kind="Internal").ap()
    waps = []
    for si, (which, prefix, wnames) in enumerate(CHAIN):
        waps.append({nm: nc.dram_tensor(f"s{si}_{nm}",
                                        list(wshapes_by_stage[si][nm]), F32,
                                        kind="ExternalInput").ap()
                     for nm in wnames})
    with contextlib.ExitStack() as es:
        ctx = Ctx(nc, es)
        ctx.wsc = wsc
        emit_prelude(ctx)
        cur = x_in
        ping = [xa, xb]
        for si, (which, prefix, wnames) in enumerate(CHAIN):
            nxt = y_out if si == len(CHAIN) - 1 else ping[si % 2]
            for b in range(2):
                SUBLAYERS[which](ctx, cur, nxt, b, waps[si])
            cur = nxt
    return nc


_fused_cache = {}


def kernel(**inputs):
    x = np.ascontiguousarray(inputs["L0_attn_a_x"], dtype=np.float32)
    shards = [x[2 * c:2 * c + 2] for c in range(NCORES)]
    wstage = []
    flat = {}
    for si, (which, prefix, wnames) in enumerate(CHAIN):
        w = {nm: np.ascontiguousarray(inputs[prefix + "_" + nm],
                                      dtype=np.float32) for nm in wnames}
        wstage.append({nm: tuple(w[nm].shape) for nm in wnames})
        for nm in wnames:
            flat[f"s{si}_{nm}"] = w[nm]
    key = "fused"
    if key not in _fused_cache:
        _fused_cache[key] = build_fused(wstage)
    nc = _fused_cache[key]
    in_maps = [{"x": shards[c], **flat} for c in range(NCORES)]
    res = run_bass_kernel_spmd(nc, in_maps, core_ids=list(range(NCORES)))
    return np.concatenate([np.asarray(res.results[c]["y"])
                           for c in range(NCORES)], axis=0)


# version 16
# speedup vs baseline: 1.4342x; 1.4342x over previous
import contextlib
import numpy as np

import concourse.bass as bass
import concourse.mybir as mybir
from concourse.bass_utils import run_bass_kernel_spmd

F32 = mybir.dt.float32
BF = mybir.dt.bfloat16
AF = mybir.ActivationFunctionType
ALU = mybir.AluOpType
AX = mybir.AxisListType

NCORES = 8
T = 2048
D = 1024
KC = D // 128
NT = T // 128


def ap_off(t_ap, extra_off, ap):
    return bass.AP(t_ap.tensor, t_ap.offset + extra_off, ap)


class Sem:
    def __init__(self, h):
        self.h = h
        self.n = 0

    def inc(self, inst, k=1):
        inst.then_inc(self.h, k)
        self.n += k
        return self.n


class Ctx:
    def __init__(self, nc, es):
        self.nc = nc
        self.es = es
        E = es.enter_context
        self.A = E(nc.sbuf_tensor("arena_a", [128, KC * T], BF))
        self.B = E(nc.sbuf_tensor("arena_b", [128, KC * T], F32))
        self.C = E(nc.sbuf_tensor("arena_c", [128, 8192], F32))
        self.W = E(nc.sbuf_tensor("arena_w", [128, 8192], F32))
        self.Wb = E(nc.sbuf_tensor("arena_wb", [128, 7968], BF))
        self.XS = E(nc.sbuf_tensor("arena_xs", [128, 4096], F32))
        self.SW = E(nc.sbuf_tensor("arena_sw", [128, 2048], BF))
        self.S = E(nc.sbuf_tensor("arena_s", [128, 1792], F32))
        self.ident = E(nc.sbuf_tensor("ident", [128, 128], F32))
        self.ones_row = E(nc.sbuf_tensor("ones_row", [1, 128], F32))
        self.R = E(nc.sbuf_tensor("row_scratch", [1, 1032], F32))
        self.ceps = E(nc.sbuf_tensor("c_eps6", [128, 1], F32))
        self.pb = [E(nc.psum_tensor(f"pb{i}", [128, 512], F32)) for i in range(8)]
        self._sems = {}

    def sem(self, role):
        DMA_ROLES = {"in_ld", "out_ldx", "out_ldw", "out_ldw0", "out_ldw1",
                     "ffn1_ld", "fp_ld", "fx_ldw", "fx_ldw0", "fx_ldw1",
                     "fx_ldw2", "fx_ldw3", "wc_l2",
                     "ab_ld", "ab_ld0", "ab_ld1", "aa_pld", "aa_ldw",
                     "aa_vld", "da_pld", "da_nwl", "da_ldw", "rk_ld"}
        if role in ("f_dve", "f_act", "out_st"):
            key = role
        else:
            fam = role.split("_")[0]
            if role in DMA_ROLES:
                fam += "D"
            if not hasattr(self, "_fam_slots"):
                self._fam_slots = {}
                self._fam_counts = {}
            if role not in self._fam_slots:
                idx = self._fam_counts.get(fam, 0)
                self._fam_counts[fam] = idx + 1
                pfx = "dpool" if fam.endswith("D") else "pool"
                self._fam_slots[role] = f"{pfx}{idx}"
            key = self._fam_slots[role]
        if key not in self._sems:
            self._sems[key] = Sem(self.es.enter_context(self.nc.semaphore(key)))
        s = self._sems[key]
        return s

    def fence(self, eng, role, inst=None):
        s = self.sem(role)
        if inst is None:
            inst = eng.engine_nop() if hasattr(eng, "engine_nop") else eng.nop()
        s.inc(inst)
        eng.wait_ge(s.h, s.n)

    def a_fm(self, k, t0=0, n=T):
        return self.A[:, k * T + t0: k * T + t0 + n]

    def b_fm(self, k, t0=0, n=T):
        return self.B[:, k * T + t0: k * T + t0 + n]

    def a_tok(self, i):
        return self.A[:, i * 1024:(i + 1) * 1024]

    def b_tok(self, i):
        return self.B[:, i * 1024:(i + 1) * 1024]


def emit_prelude(ctx):
    nc = ctx.nc
    with nc.Block() as blk:
        @blk.gpsimd
        def _(g):
            g.iota(ctx.C[:, 0:128], [[-1, 128]], channel_multiplier=1,
                   allow_small_or_imprecise_dtypes=True)
            g.memset(ctx.S[:, 0:1], 1.0)
            g.memset(ctx.S[:, 1:2], 0.0)
            g.memset(ctx.ones_row[:, :], 1.0)
            g.memset(ctx.ceps[:, :], 1e-6)
    with nc.Block() as blk:
        @blk.vector
        def _(v):
            v.tensor_scalar(ctx.ident[:, :], ctx.C[:, 0:128], 0.0, None,
                            ALU.is_equal)



def emit_in_ln_tp(ctx, x_in, b, ln=True):
    nc = ctx.nc
    ld = ctx.sem("in_ld")
    with nc.Block() as blk:
        @blk.sync
        def _(sync):
            dst = bass.AP(ctx.B, 0, [[KC * T, 128], [1024, NT], [1, 1024]])
            src = ap_off(x_in, b * T * D,
                         [[1024, 128], [128 * 1024, NT], [1, 1024]])
            ld.inc(sync.dma_start(out=dst, in_=src), 16)
            sync.wait_ge(ld.h, ld.n)

    if ln:
        sq = ctx.sem("ln_sq")
        var = ctx.sem("ln_var")
        rst = ctx.sem("ln_rst")
        with nc.Block() as blk:
            @blk.scalar
            def _(s):
                for i in range(NT):
                    sq.inc(s.activation(ctx.XS[:, 0:1024], ctx.b_tok(i), AF.Square,
                                        accum_out=ctx.S[:, 32 + i:33 + i]))
                s.wait_ge(var.h, var.n + 1)
                inst = s.activation(ctx.S[:, 80:80 + NT], ctx.S[:, 64:64 + NT],
                                    AF.Ln)
                ctx.fence(s, "f_act", inst)
                rst.inc(s.activation(ctx.S[:, 96:96 + NT], ctx.S[:, 80:80 + NT],
                                     AF.Exp, scale=-0.5))

            @blk.vector
            def _(v):
                sq0 = sq.n - NT
                for i in range(NT):
                    inst = v.tensor_reduce(ctx.S[:, 48 + i:49 + i], ctx.b_tok(i),
                                           axis=AX.X, op=ALU.add)
                v.wait_ge(sq.h, sq0 + NT)
                ctx.fence(v, "f_dve", inst)
                inst = v.tensor_scalar(ctx.S[:, 112:112 + NT], ctx.S[:, 48:48 + NT],
                                       1.0 / D, None, ALU.mult)
                ctx.fence(v, "f_dve", inst)
                v.tensor_scalar(ctx.S[:, 192:192 + NT], ctx.S[:, 32:32 + NT],
                                1.0 / D, 1e-5, ALU.mult, ALU.add)
                inst = v.tensor_tensor(ctx.S[:, 208:208 + NT], ctx.S[:, 112:112 + NT],
                                       ctx.S[:, 112:112 + NT], ALU.mult)
                ctx.fence(v, "f_dve", inst)
                inst = v.tensor_tensor(ctx.S[:, 64:64 + NT], ctx.S[:, 192:192 + NT],
                                       ctx.S[:, 208:208 + NT], ALU.subtract)
                var.inc(inst)
                v.wait_ge(rst.h, rst.n)
                for i in range(NT):
                    v.tensor_scalar(ctx.b_tok(i), ctx.b_tok(i),
                                    ctx.S[:, 112 + i:113 + i],
                                    ctx.S[:, 96 + i:97 + i],
                                    ALU.subtract, ALU.mult)

    tp = ctx.sem("tp")
    ev = ctx.sem("tp_ev")
    with nc.Block() as blk:
        @blk.tensor
        def _(t):
            ev0, tp0 = ev.n, tp.n
            g = 0
            for i in range(NT):
                for half in range(2):
                    if g >= 4:
                        t.wait_ge(ev.h, ev0 + g - 3)
                    bank = ctx.pb[g % 4]
                    for q in range(4):
                        k = half * 4 + q
                        inst = t.transpose(bank[:, q * 128:(q + 1) * 128],
                                           ctx.b_tok(i)[:, k * 128:(k + 1) * 128],
                                           ctx.ident[:, :])
                        if q == 3:
                            tp.inc(inst)
                    g += 1

        @blk.vector
        def _(v):
            ev0, tp0 = ev.n, tp.n - 2 * NT
            g = 0
            for i in range(NT):
                for half in range(2):
                    v.wait_ge(tp.h, tp0 + g + 1)
                    bank = ctx.pb[g % 4]
                    for q in range(4):
                        k = half * 4 + q
                        inst = v.tensor_copy(ctx.a_fm(k, i * 128, 128),
                                             bank[:, q * 128:(q + 1) * 128])
                        if q == 3:
                            ev.inc(inst)
                    g += 1



def emit_out_phase(ctx, x_in, x_out, b, proj=None):
    nc = ctx.nc
    ldx = ctx.sem("out_ldx")
    ldw = ctx.sem("out_ldw")
    mm = ctx.sem("out_mm")
    evs = ctx.sem("out_ev")
    tps = ctx.sem("out_tp")
    add = ctx.sem("out_add")
    stw = ctx.sem("out_st")
    wtp = ctx.sem("out_wtp")
    wte = ctx.sem("out_wte")

    has_bias = proj is not None and proj.get("bias") is not None
    kchunks = proj["kchunks"] if proj else 0
    kpart = proj.get("kpart", 128) if proj else 128
    wtrans = proj.get("transpose", False) if proj else False

    with nc.Block() as blk:
        @blk.sync
        def _(sync):
            if has_bias:
                bsrc = proj["bias"].rearrange("(a c) -> a c", c=128)
                ldx.inc(sync.dma_start(out=ctx.S[0:8, 128:256], in_=bsrc), 16)
            if proj and not wtrans:
                for of in range(8):
                    wdst = ctx.W[0:kpart,
                                 of * kchunks * 128:(of + 1) * kchunks * 128]
                    ldw.inc(sync.dma_start(out=wdst, in_=proj["w_src"](of)), 16)
            if proj and wtrans:
                wtp0 = wtp.n
                ngrp = (kchunks + 3) // 4
                for of in range(8):
                    if of >= 2:
                        sync.wait_ge(wtp.h, wtp0 + (of - 1) * ngrp)
                    sl = ctx.sem(f"out_ldw{of % 2}")
                    sl.inc(sync.dma_start(
                        out=ctx.C[:, (of % 2) * 1024:(of % 2) * 1024 + 1024],
                        in_=proj["w_rows"](of)), 16)
            sync.wait_ge(ldx.h, ldx.n)

        if has_bias:
            @blk.tensor
            def _(t):
                t.wait_ge(ldx.h, ldx.n)
                tps.inc(t.transpose(ctx.pb[6][0:128, 0:8], ctx.S[0:8, 128:256],
                                    ctx.ident[0:8, 0:8]))

            @blk.vector
            def _(v):
                v.wait_ge(tps.h, tps.n)
                v.tensor_copy(ctx.S[:, 256:264], ctx.pb[6][:, 0:8])

        if proj and wtrans:
            ngrp = (kchunks + 3) // 4

            @blk.tensor
            def _(t):
                slb = [ctx.sem("out_ldw0").n - 4 * 16, ctx.sem("out_ldw1").n - 4 * 16]
                wte0 = wte.n
                gcnt = 0
                for of in range(8):
                    sl = ctx.sem(f"out_ldw{of % 2}")
                    t.wait_ge(sl.h, slb[of % 2] + 16 * (of // 2 + 1))
                    for g in range(ngrp):
                        if gcnt >= 2:
                            t.wait_ge(wte.h, wte0 + gcnt - 1)
                        bank = ctx.pb[4 + gcnt % 2]
                        for k in range(g * 4, min(g * 4 + 4, kchunks)):
                            inst = t.transpose(
                                bank[:, (k % 4) * 128:(k % 4) * 128 + 128],
                                ctx.C[:, (of % 2) * 1024 + k * 128:
                                      (of % 2) * 1024 + (k + 1) * 128],
                                ctx.ident[:, :])
                        wtp.inc(inst)
                        gcnt += 1

            @blk.vector
            def _(v):
                wtp0 = wtp.n - 8 * ngrp
                gcnt = 0
                for of in range(8):
                    for g in range(ngrp):
                        v.wait_ge(wtp.h, wtp0 + gcnt + 1)
                        bank = ctx.pb[4 + gcnt % 2]
                        for k in range(g * 4, min(g * 4 + 4, kchunks)):
                            inst = v.tensor_copy(
                                ctx.W[:, of * kchunks * 128 + k * 128:
                                      of * kchunks * 128 + (k + 1) * 128],
                                bank[:, (k % 4) * 128:(k % 4) * 128 + 128])
                        wte.inc(inst)
                        gcnt += 1

    ldw_needed = (ldw.n if proj and not wtrans else 0)
    ldx_base = ldx.n
    add_base = add.n
    with nc.Block() as blk:
        @blk.sync
        def _(sync):
            for n in range(4):
                if n >= 1:
                    sync.wait_ge(add.h, add_base + n * 8)
                dst = bass.AP(ctx.XS, 0, [[4096, 128], [1024, 4], [1, 1024]])
                src = ap_off(x_in, (b * T + n * 512) * D,
                             [[1024, 128], [128 * 1024, 4], [1, 1024]])
                ldx.inc(sync.dma_start(out=dst, in_=src), 16)

        @blk.tensor
        def _(t):
            mm0, ev0, add0 = mm.n, evs.n, add.n
            if proj and not wtrans:
                t.wait_ge(ldw.h, ldw_needed)
            for i in range(32):
                n, of = divmod(i, 8)
                if proj:
                    if i >= 2:
                        t.wait_ge(evs.h, ev0 + i - 1)
                    wbase = of * kchunks * 128
                    for k in range(kchunks):
                        inst = t.matmul(ctx.pb[i % 2][:, :],
                                        ctx.W[0:kpart,
                                              wbase + k * 128:wbase + (k + 1) * 128],
                                        proj["src"](k, n * 512, 512),
                                        start=(k == 0), stop=(k == kchunks - 1))
                    mm.inc(inst)
                if proj:
                    t.wait_ge(evs.h, ev0 + i + 1)
                    srct = ctx.C[:, (i % 4) * 512:(i % 4) * 512 + 512]
                else:
                    srct = ctx.b_fm(of, n * 512, 512)
                if i >= 2:
                    t.wait_ge(add.h, add0 + i - 1)
                bank = ctx.pb[2 + i % 2]
                for q in range(4):
                    inst = t.transpose(bank[:, q * 128:(q + 1) * 128],
                                       srct[:, q * 128:(q + 1) * 128],
                                       ctx.ident[:, :])
                tps.inc(inst)

        @blk.vector
        def _(v):
            mm0 = mm.n - (32 if proj else 0)
            tp0 = tps.n - 32
            st0 = stw.n
            for i in range(32):
                n, of = divmod(i, 8)
                if proj:
                    v.wait_ge(mm.h, mm0 + i + 1)
                    dstc = ctx.C[:, (i % 4) * 512:(i % 4) * 512 + 512]
                    if has_bias:
                        inst = v.tensor_scalar(dstc, ctx.pb[i % 2][:, :],
                                               ctx.S[:, 256 + of:257 + of],
                                               None, ALU.add)
                    else:
                        inst = v.tensor_copy(dstc, ctx.pb[i % 2][:, :])
                    evs.inc(inst)
                v.wait_ge(tps.h, tp0 + i + 1)
                if of == 0 and n >= 1:
                    v.wait_ge(stw.h, st0 + 16 * n)
                if of == 0:
                    v.wait_ge(ldx.h, ldx_base + 16 * (n + 1))
                bank = ctx.pb[2 + i % 2]
                for q in range(4):
                    inst = v.tensor_tensor(
                        ctx.C[:, 2048 + q * 1024 + of * 128:
                              2048 + q * 1024 + (of + 1) * 128],
                        bank[:, q * 128:(q + 1) * 128],
                        ctx.XS[:, q * 1024 + of * 128:q * 1024 + (of + 1) * 128],
                        ALU.add)
                add.inc(inst)

        @blk.gpsimd
        def _(g):
            add0 = add.n - 32
            for n in range(4):
                g.wait_ge(add.h, add0 + (n + 1) * 8)
                src = bass.AP(ctx.C, 2048, [[8192, 128], [1024, 4], [1, 1024]])
                dst = ap_off(x_out, (b * T + n * 512) * D,
                             [[1024, 128], [128 * 1024, 4], [1, 1024]])
                stw.inc(g.dma_start(out=dst, in_=src), 16)
            g.wait_ge(stw.h, stw.n)



def emit_l1_ffn_mix(ctx, b, w):
    nc = ctx.nc
    ld = ctx.sem("ffn1_ld")
    mm = ctx.sem("ffn1_mm")
    evs = ctx.sem("ffn1_ev")

    with nc.Block() as blk:
        @blk.sync
        def _(sync):
            wsrc = ap_off(w["Win"], 0, [[64, 128], [128 * 64, KC], [1, 64]])
            ld.inc(sync.dma_start(out=ctx.W[:, 4096:4096 + KC * 64], in_=wsrc), 16)
            ld.inc(sync.dma_start(out=ctx.W[0:64, 4608:4624], in_=w["W1"]), 16)
            ld.inc(sync.dma_start(out=ctx.W[0:64, 4624:4640], in_=w["b1"]), 16)
            ld.inc(sync.dma_start(out=ctx.W[0:64, 4640:4656], in_=w["W2"]), 16)
            ld.inc(sync.dma_start(out=ctx.W[0:64, 4656:4657],
                                  in_=w["bin_"].rearrange("(m o) -> m o", o=1)), 16)
            ld.inc(sync.dma_start(out=ctx.W[0:64, 4657:4658],
                                  in_=w["b2"].rearrange("(m o) -> m o", o=1)), 16)
            sync.wait_ge(ld.h, ld.n)

        @blk.gpsimd
        def _(g):
            g.wait_ge(ld.h, ld.n)
            g.tensor_copy(ctx.Wb[:, 0:KC * 64], ctx.W[:, 4096:4096 + KC * 64])

    with nc.Block() as blk:
        @blk.tensor
        def _(t):
            ev0 = evs.n
            for n in range(4):
                if n >= 2:
                    t.wait_ge(evs.h, ev0 + n - 1)
                for k in range(KC):
                    inst = t.matmul(ctx.pb[4 + n % 2][0:64, :],
                                    ctx.Wb[:, k * 64:(k + 1) * 64],
                                    ctx.a_fm(k, n * 512, 512),
                                    start=(k == 0), stop=(k == KC - 1))
                mm.inc(inst)

        @blk.scalar
        def _(s):
            mm0 = mm.n - 4
            for n in range(4):
                s.wait_ge(mm.h, mm0 + n + 1)
                inst = s.activation(ctx.C[0:64, 0 + n * 512: 512 + n * 512],
                                    ctx.pb[4 + n % 2][0:64, :], AF.Identity,
                                    bias=ctx.W[0:64, 4656:4657])
                evs.inc(inst)

    hsem = ctx.sem("ffn1_h")
    gsem = ctx.sem("ffn1_g")
    with nc.Block() as blk:
        @blk.scalar
        def _(s):
            g0 = gsem.n
            for i in range(16):
                if i >= 2:
                    s.wait_ge(gsem.h, g0 + i - 1)
                hb = ctx.C[0:64, 2048 + (i % 2) * T:2048 + (i % 2) * T + T]
                hsem.inc(s.activation(hb, ctx.C[0:64, 0:T], AF.Relu,
                                      bias=ctx.W[0:64, 4624 + i:4625 + i],
                                      scale=ctx.W[0:64, 4608 + i:4609 + i]))

        @blk.vector
        def _(v):
            h0 = hsem.n - 16
            g = ctx.B[0:64, 0:T]
            for i in range(16):
                v.wait_ge(hsem.h, h0 + i + 1)
                hb = ctx.C[0:64, 2048 + (i % 2) * T:2048 + (i % 2) * T + T]
                if i == 0:
                    inst = v.tensor_scalar(g, hb, ctx.W[0:64, 4640:4641],
                                           None, ALU.mult)
                else:
                    tg = ctx.C[0:64, 6144:6144 + T]
                    v.tensor_scalar(tg, hb, ctx.W[0:64, 4640 + i:4641 + i],
                                    None, ALU.mult)
                    inst = v.tensor_tensor(g, g, tg, ALU.add)
                gsem.inc(inst)
            v.tensor_scalar(g, g, ctx.W[0:64, 4657:4658], None, ALU.add)
    return


SUBLAYERS = {}


def build_test_program(which, weights_spec):
    nc = bass.Bass("TRN2", target_bir_lowering=False)
    x_in = nc.dram_tensor("x", [2, T, D], F32, kind="ExternalInput").ap()
    x_out = nc.dram_tensor("y", [2, T, D], F32, kind="ExternalOutput").ap()
    waps = {k: nc.dram_tensor(k, list(v), F32, kind="ExternalInput").ap()
            for k, v in weights_spec.items()}
    wsc = nc.dram_tensor("wsc", [192 * 128, 2048], BF, kind="Internal").ap()
    with contextlib.ExitStack() as es:
        ctx = Ctx(nc, es)
        ctx.wsc = wsc
        emit_prelude(ctx)
        for b in range(2):
            SUBLAYERS[which](ctx, x_in, x_out, b, waps)
    return nc


def sub_l1_ffn(ctx, x_in, x_out, b, w):
    emit_in_ln_tp(ctx, x_in, b)
    emit_l1_ffn_mix(ctx, b, w)
    proj = dict(
        kchunks=1, kpart=64,
        w_src=lambda of: ap_off(w["Wout"], of * 128, [[1024, 64], [1, 128]]),
        src=lambda k, t0, n: ctx.B[0:64, t0:t0 + n],
        bias=w["bout"],
    )
    emit_out_phase(ctx, x_in, x_out, b, proj=proj)


SUBLAYERS["l1_ffn"] = sub_l1_ffn





def emit_l0_ffn_wcast(ctx, w):
    nc = ctx.nc
    wcl = ctx.sem("wc_ld")
    wc2 = ctx.sem("wc_l2")
    wct = ctx.sem("wc_ct")
    wcs = ctx.sem("wc_st")
    wcs_base = wcs.n
    wcl_base, wc2_base = wcl.n, wc2.n
    with nc.Block() as blk:
        @blk.sync
        def _(sync):
            idx = 0
            for e in range(6):
                w1 = w["sw1"] if e < 2 else w["rw1"]
                ei = e if e < 2 else e - 2
                for hc in range(32):
                    slot = idx % 2
                    if idx >= 2:
                        sync.wait_ge(wcs.h, wcs_base + 16 * (idx - 1))
                    src1 = ap_off(w1, ei * D * 4096 + hc * 128,
                                  [[4096, 128], [128 * 4096, KC], [1, 128]])
                    wcl.inc(sync.dma_start(
                        out=ctx.C[:, slot * 2048:slot * 2048 + 1024],
                        in_=src1), 16)
                    idx += 1

        @blk.scalar
        def _(s):
            idx = 0
            for e in range(6):
                w2 = w["sw2"] if e < 2 else w["rw2"]
                ei = e if e < 2 else e - 2
                for hc in range(32):
                    slot = idx % 2
                    if idx >= 2:
                        s.wait_ge(wcs.h, wcs_base + 16 * (idx - 1))
                    wc2.inc(s.dma_start(
                        out=ctx.C[:, slot * 2048 + 1024:slot * 2048 + 2048],
                        in_=ap_off(w2, ei * 4096 * D + hc * 128 * D,
                                   [[D, 128], [1, D]])), 16)
                    idx += 1

        @blk.gpsimd
        def _(g):
            for idx in range(192):
                slot = idx % 2
                g.wait_ge(wcl.h, wcl_base + 16 * (idx + 1))
                g.wait_ge(wc2.h, wc2_base + 16 * (idx + 1))
                if idx >= 2:
                    g.wait_ge(wcs.h, wcs_base + 16 * (idx - 1))
                wct.inc(g.tensor_copy(
                    ctx.Wb[:, slot * 2048:(slot + 1) * 2048],
                    ctx.C[:, slot * 2048:(slot + 1) * 2048]))
                dstw = ap_off(ctx.wsc, idx * 128 * 2048,
                              [[2048, 128], [1, 2048]])
                wcs.inc(g.dma_start(
                    out=dstw, in_=ctx.Wb[:, slot * 2048:(slot + 1) * 2048]), 16)
            g.wait_ge(wcs.h, wcs.n)


def emit_l0_ffn_prep(ctx, w):
    nc = ctx.nc
    ld = ctx.sem("fp_ld")
    tp = ctx.sem("fp_tp")
    with nc.Block() as blk:
        @blk.sync
        def _(sync):
            for e in range(6):
                wsrc = w["sw1"] if e < 2 else w["rw1"]
                b1 = w["sb1"] if e < 2 else w["rb1"]
                b2 = w["sb2"] if e < 2 else w["rb2"]
                ei = e if e < 2 else e - 2
                ld.inc(sync.dma_start(
                    out=ctx.C[0:32, e * 128:(e + 1) * 128],
                    in_=ap_off(b1, ei * 4096, [[128, 32], [1, 128]])), 16)
                ld.inc(sync.dma_start(
                    out=ctx.C[0:8, 768 + e * 128:768 + e * 128 + 128],
                    in_=ap_off(b2, ei * 1024, [[128, 8], [1, 128]])), 16)
            for k in range(KC):
                ld.inc(sync.dma_start(
                    out=ctx.W[:, 6424 + 4 * k:6428 + 4 * k],
                    in_=ap_off(w["gate_w"], k * 128 * 4, [[4, 128], [1, 4]])), 16)
            ld.inc(sync.dma_start(out=ctx.S[0:1, 130:134],
                                  in_=w["gate_b"].rearrange("(o e) -> o e", o=1)), 16)
            ld.inc(sync.dma_start(out=ctx.W[0:4, 4096:5120],
                                  in_=w["rb2"]), 16)
            ld.inc(sync.dma_start(out=ctx.W[4:6, 4096:5120],
                                  in_=w["sb2"]), 16)
            sync.wait_ge(ld.h, ld.n)

        @blk.gpsimd
        def _(g):
            g.iota(ctx.W[0:4, 6460:6460 + 128], [[0, 128]], channel_multiplier=1,
                   allow_small_or_imprecise_dtypes=True)
            g.wait_ge(ld.h, ld.n)
            g.tensor_copy(ctx.Wb[:, 4864:4896], ctx.W[:, 6424:6456])
            g.tensor_copy(ctx.Wb[0:6, 4896:5920], ctx.W[0:6, 4096:5120])

    with nc.Block() as blk:
        @blk.tensor
        def _(t):
            feb = ctx.sem("fp_ev").n
            for e in range(6):
                if e >= 1:
                    t.wait_ge(ctx.sem("fp_ev").h, feb + e)
                t.transpose(ctx.pb[7][:, 0:32], ctx.C[0:32, e * 128:(e + 1) * 128],
                            ctx.ident[0:32, 0:32])
                tp.inc(t.transpose(ctx.pb[7][:, 32:40],
                                   ctx.C[0:8, 768 + e * 128:768 + e * 128 + 128],
                                   ctx.ident[0:8, 0:8]))
            tp.inc(t.matmul(ctx.pb[6][:, 0:4], ctx.ones_row[0:1, 0:128],
                            ctx.S[0:1, 130:134], start=True, stop=True))

        @blk.vector
        def _(v):
            fe = ctx.sem("fp_ev")
            tp0 = tp.n - 7
            for e in range(6):
                v.wait_ge(tp.h, tp0 + e + 1)
                v.tensor_copy(ctx.W[:, 6176 + 32 * e:6176 + 32 * e + 32],
                              ctx.pb[7][:, 0:32])
                fe.inc(v.tensor_copy(ctx.W[:, 6368 + 8 * e:6368 + 8 * e + 8],
                                     ctx.pb[7][:, 32:40]))
            v.wait_ge(tp.h, tp0 + 7)
            v.tensor_copy(ctx.W[:, 6456:6460], ctx.pb[6][:, 0:4])
            v.tensor_scalar(ctx.W[:, 6176:6240], ctx.W[:, 6176:6240],
                            0.5, None, ALU.mult)
            v.tensor_tensor(ctx.W[:, 6416:6424], ctx.W[:, 6368:6376],
                            ctx.W[:, 6376:6384], ALU.add)
            for e in range(4):
                v.tensor_scalar(ctx.W[0:4, 6460 + 128 * (e + 1):6460 + 128 * (e + 2)]
                                if False else
                                ctx.W[0:4, 6588 + 128 * e:6588 + 128 * e + 128],
                                ctx.W[0:4, 6460:6588][0:4, 0:128], float(e), None,
                                ALU.is_equal)


def emit_l0_ffn_gate(ctx, b):
    nc = ctx.nc
    gm = ctx.sem("fg_mm")
    ge = ctx.sem("fg_ev")
    with nc.Block() as blk:
        @blk.tensor
        def _(t):
            ge0 = ge.n
            for i in range(NT):
                if i >= 2:
                    t.wait_ge(ge.h, ge0 + i - 1)
                for k in range(KC):
                    inst = t.matmul(ctx.pb[6 + i % 2][0:128, 0:4],
                                    ctx.a_fm(k, i * 128, 128),
                                    ctx.Wb[:, 4864 + 4 * k:4868 + 4 * k],
                                    start=(k == 0), stop=(k == KC - 1))
                gm.inc(inst)

        @blk.vector
        def _(v):
            gm0 = gm.n - NT
            for i in range(NT):
                v.wait_ge(gm.h, gm0 + i + 1)
                ge.inc(v.tensor_tensor(ctx.C[:, 6144 + 4 * i:6148 + 4 * i],
                                       ctx.pb[6 + i % 2][:, 0:4],
                                       ctx.W[:, 6456:6460], ALU.add))

    sm = ctx.sem("fg_sm")
    with nc.Block() as blk:
        @blk.vector
        def _(v):
            L = ctx.C[:, 6144:6208]
            E3 = L.rearrange("p (g e) -> p g e", e=4)
            MX = ctx.C[:, 6208:6224]
            SM = ctx.C[:, 6224:6240]
            W4 = ctx.C[:, 6240:6304]
            M1 = ctx.C[:, 6304:6320]
            M2 = ctx.C[:, 6320:6336]
            TMP = ctx.C[:, 6336:6400]
            def bc16(x):
                return bass.AP(x.tensor, x.offset, [list(x.ap[0]), list(x.ap[1]), [0, 4]])
            smb = sm.n
            inst = v.tensor_reduce(MX, E3, axis=AX.X, op=ALU.max)
            ctx.fence(v, "f_dve", inst)
            inst = v.tensor_tensor(W4.rearrange("p (g e) -> p g e", e=4), E3,
                                   bc16(MX), ALU.subtract)
            sm.inc(inst)
            v.wait_ge(sm.h, smb + 2)
            inst = v.tensor_reduce(SM, W4.rearrange("p (g e) -> p g e", e=4),
                                   axis=AX.X, op=ALU.add)
            ctx.fence(v, "f_dve", inst)
            inst = v.reciprocal(SM, SM)
            ctx.fence(v, "f_dve", inst)
            inst = v.tensor_tensor(W4.rearrange("p (g e) -> p g e", e=4),
                                   W4.rearrange("p (g e) -> p g e", e=4),
                                   bc16(SM), ALU.mult)
            ctx.fence(v, "f_dve", inst)
            inst = v.tensor_reduce(M1, W4.rearrange("p (g e) -> p g e", e=4),
                                   axis=AX.X, op=ALU.max)
            ctx.fence(v, "f_dve", inst)
            inst = v.tensor_tensor(TMP.rearrange("p (g e) -> p g e", e=4),
                                   W4.rearrange("p (g e) -> p g e", e=4),
                                   bc16(M1), ALU.subtract)
            ctx.fence(v, "f_dve", inst)
            inst = v.tensor_scalar(TMP, TMP, 0.0, None, ALU.is_ge)
            ctx.fence(v, "f_dve", inst)
            inst = v.scalar_tensor_tensor(TMP, TMP, -1e30, W4, ALU.mult, ALU.add)
            ctx.fence(v, "f_dve", inst)
            inst = v.tensor_reduce(M2, TMP.rearrange("p (g e) -> p g e", e=4),
                                   axis=AX.X, op=ALU.max)
            ctx.fence(v, "f_dve", inst)
            inst = v.tensor_tensor(TMP.rearrange("p (g e) -> p g e", e=4),
                                   W4.rearrange("p (g e) -> p g e", e=4),
                                   bc16(M2), ALU.subtract)
            ctx.fence(v, "f_dve", inst)
            inst = v.tensor_scalar(TMP, TMP, 0.0, None, ALU.is_ge)
            ctx.fence(v, "f_dve", inst)
            v.tensor_tensor(W4, W4, TMP, ALU.mult)

        @blk.scalar
        def _(s):
            s.wait_ge(sm.h, sm.n - 1 + 1)
            sm.inc(s.activation(ctx.C[:, 6240:6304], ctx.C[:, 6240:6304], AF.Exp))

        @blk.gpsimd
        def _(g):
            g.memset(ctx.SW[0:6, 0:2048], 0.5)

    tp = ctx.sem("fg_tp")
    ev = ctx.sem("fg_tpe")
    with nc.Block() as blk:
        @blk.tensor
        def _(t):
            ev0 = ev.n
            for i in range(NT):
                if i >= 8:
                    t.wait_ge(ev.h, ev0 + (i - 8) // 4 + 1)
                tp.inc(t.transpose(ctx.pb[6 + (i // 4) % 2][0:4, (i % 4) * 128:(i % 4) * 128 + 128],
                                   ctx.C[:, 6240 + 4 * i:6244 + 4 * i],
                                   ctx.ident[:, :]))

        @blk.vector
        def _(v):
            tp0 = tp.n - NT
            for gsel in range(4):
                v.wait_ge(tp.h, tp0 + gsel * 4 + 4)
                v.tensor_copy(
                    ctx.C[0:4, 2048 + gsel * 512:2048 + gsel * 512 + 512],
                    ctx.pb[6 + gsel % 2][0:4, 0:512])
                ev.inc(v.tensor_copy(
                    ctx.SW[0:4, gsel * 512:gsel * 512 + 512],
                    ctx.pb[6 + gsel % 2][0:4, 0:512]))



def emit_l0_ffn_mix(ctx, b, w):
    nc = ctx.nc
    hmm = ctx.sem("fx_hmm")
    hrl = ctx.sem("fx_hrl")
    hsc = ctx.sem("fx_hsc")
    omm = ctx.sem("fx_omm")
    oev = ctx.sem("fx_oev")
    wbm = ctx.sem("fx_wbm")
    wbe = ctx.sem("fx_wbe")

    TILES = [(0, 384), (384, 384), (768, 384), (1152, 384), (1536, 384),
             (1920, 128)]
    NIDX = len(TILES) * 6 * 32
    omm_base = omm.n
    ldw_base = [ctx.sem(f"fx_ldw{q}").n for q in range(2)]
    SLOT = [0, 2048, 5920]

    def emit_load(eng, idx, e, hc):
        sb = SLOT[idx % 3]
        if idx >= 3:
            eng.wait_ge(omm.h, omm_base + idx - 2)
        sl = ctx.sem(f"fx_ldw{idx % 2}")
        ti = e * 32 + hc
        srcw = ap_off(ctx.wsc, ti * 128 * 2048, [[2048, 128], [1, 2048]])
        sl.inc(eng.dma_start(out=ctx.Wb[:, sb:sb + 2048], in_=srcw), 16)

    with nc.Block() as blk:
        def emit_loads(eng, qid):
            idx = 0
            for j in range(len(TILES)):
                for e in range(6):
                    for hc in range(32):
                        if idx % 2 == qid:
                            emit_load(eng, idx, e, hc)
                        idx += 1

        @blk.sync
        def _(sync):
            emit_loads(sync, 0)

        @blk.gpsimd
        def _(g):
            emit_loads(g, 1)

        @blk.tensor
        def _(t):
            hrl0, oev0, wbe0, hsc0 = hrl.n, oev.n, wbe.n, hsc.n
            idx = 0
            rcnt = 0
            ne_done = 0
            for j, (ts0, tlen) in enumerate(TILES):
                ntc = tlen // 128
                t.wait_ge(hrl.h, hrl0 + idx)
                for e4 in range(4):
                    if e4 >= 2:
                        t.wait_ge(wbe.h, wbe0 + j * 4 + e4 - 1)
                    wbm.inc(t.matmul(
                        ctx.pb[6 + e4 % 2][:, 0:tlen],
                        ctx.W[0:4, 6588 + 128 * e4:6588 + 128 * e4 + 128],
                        ctx.C[0:4, 2048 + ts0:2048 + ts0 + tlen],
                        start=True, stop=True))
                if j >= 1:
                    t.wait_ge(oev.h, oev0 + ne_done)
                for tc in range(ntc):
                    for oh in range(2):
                        t.matmul(ctx.pb[tc * 2 + oh][:, 0:512],
                                 ctx.SW[0:6, ts0 + tc * 128:
                                        ts0 + tc * 128 + 128],
                                 ctx.Wb[0:6, 4896 + oh * 512:5408 + oh * 512],
                                 start=True, stop=False, skip_group_check=True)
                t.wait_ge(wbe.h, wbe0 + (j + 1) * 4)
                for e in range(6):
                    for hc in range(32):
                        sb = SLOT[idx % 3]
                        t.wait_ge(ctx.sem(f"fx_ldw{idx % 2}").h,
                                  ldw_base[idx % 2] + 16 * (idx // 2 + 1))
                        if idx >= 2:
                            t.wait_ge(hrl.h, hrl0 + idx - 1)
                        for k in range(KC):
                            inst = t.matmul(
                                ctx.pb[6 + idx % 2][:, 0:tlen],
                                ctx.Wb[:, sb + k * 128:sb + (k + 1) * 128],
                                ctx.a_fm(k, ts0, tlen),
                                start=(k == 0), stop=(k == KC - 1))
                        hmm.inc(inst)
                        if e < 2:
                            t.wait_ge(hrl.h, hrl0 + idx + 1)
                        else:
                            rcnt += 1
                            t.wait_ge(hsc.h, hsc0 + rcnt)
                        hs0 = 4096 + (idx % 2) * 384
                        last = (e == 5 and hc == 31)
                        for tc in range(ntc):
                            for oh in range(2):
                                inst = t.matmul(
                                    ctx.pb[tc * 2 + oh][:, 0:512],
                                    ctx.Wb[:, hs0 + tc * 128:hs0 + tc * 128 + 128],
                                    ctx.Wb[:, sb + 1024 + oh * 512:
                                           sb + 1536 + oh * 512],
                                    start=False, stop=(last and tc == ntc - 1),
                                    skip_group_check=True)
                        omm.inc(inst)
                        idx += 1
                ne_done += ntc * 2

        @blk.scalar
        def _(s):
            hmm0 = hmm.n - NIDX
            idx = 0
            for j, (ts0, tlen) in enumerate(TILES):
                for e in range(6):
                    for hc in range(32):
                        s.wait_ge(hmm.h, hmm0 + idx + 1)
                        dsth = (ctx.Wb[:, 4096 + (idx % 2) * 384:
                                       4096 + (idx % 2) * 384 + tlen]
                                if e < 2 else
                                ctx.C[:, (idx % 2) * 384:(idx % 2) * 384 + tlen])
                        hrl.inc(s.activation(
                            dsth,
                            ctx.pb[6 + idx % 2][:, 0:tlen], AF.Relu,
                            scale=(0.5 if e < 2 else 1.0),
                            bias=ctx.W[:, 6176 + 32 * e + hc:
                                       6176 + 32 * e + hc + 1]))
                        idx += 1

        @blk.vector
        def _(v):
            omm0 = omm.n - NIDX
            hrl0 = hrl.n - NIDX
            wbm0 = wbm.n - len(TILES) * 4
            idx = 0
            rcnt = 0
            for j, (ts0, tlen) in enumerate(TILES):
                ntc = tlen // 128
                for e4 in range(4):
                    v.wait_ge(wbm.h, wbm0 + j * 4 + e4 + 1)
                    inst = v.tensor_copy(
                        ctx.C[:, 5120 + e4 * 384:5120 + e4 * 384 + tlen],
                        ctx.pb[6 + e4 % 2][:, 0:tlen])
                    wbe.inc(inst)
                    v.wait_ge(wbe.h, wbe.n)
                for e in range(6):
                    for hc in range(32):
                        if e >= 2:
                            v.wait_ge(hrl.h, hrl0 + idx + 1)
                            hs = ctx.C[:, (idx % 2) * 384:
                                       (idx % 2) * 384 + tlen]
                            hsc.inc(v.tensor_tensor(
                                ctx.Wb[:, 4096 + (idx % 2) * 384:
                                       4096 + (idx % 2) * 384 + tlen], hs,
                                ctx.C[:, 5120 + (e - 2) * 384:
                                      5120 + (e - 2) * 384 + tlen], ALU.mult))
                            rcnt += 1
                        idx += 1
                v.wait_ge(omm.h, omm0 + idx)
                for tc in range(ntc):
                    for oh in range(2):
                        nchunk = j * 3 + tc
                        oev.inc(v.tensor_copy(
                            ctx.B[:, nchunk * 1024 + oh * 512:
                                  nchunk * 1024 + oh * 512 + 512],
                            ctx.pb[tc * 2 + oh][:, 0:512]))


def emit_out_resid_tm(ctx, x_in, x_out, b):
    nc = ctx.nc
    ldx = ctx.sem("out_ldx")
    add = ctx.sem("out_add")
    stw = ctx.sem("out_st")
    ldx_base = ldx.n
    add_base = add.n
    with nc.Block() as blk:
        @blk.sync
        def _(sync):
            for g in range(4):
                if g >= 1:
                    sync.wait_ge(add.h, add_base + 4 * g)
                dst = bass.AP(ctx.XS, 0, [[4096, 128], [1024, 4], [1, 1024]])
                srcx = ap_off(x_in, (b * T + g * 512) * D,
                              [[1024, 128], [128 * 1024, 4], [1, 1024]])
                ldx.inc(sync.dma_start(out=dst, in_=srcx), 16)

        @blk.vector
        def _(v):
            for n in range(NT):
                if n % 4 == 0:
                    v.wait_ge(ldx.h, ldx_base + 16 * (n // 4 + 1))
                add.inc(v.tensor_tensor(
                    ctx.B[:, n * 1024:(n + 1) * 1024],
                    ctx.B[:, n * 1024:(n + 1) * 1024],
                    ctx.XS[:, (n % 4) * 1024:(n % 4 + 1) * 1024], ALU.add))

        @blk.gpsimd
        def _(g):
            add0 = add.n - NT
            st0 = stw.n
            for n in range(4):
                g.wait_ge(add.h, add0 + (n + 1) * 4)
                srcb = bass.AP(ctx.B, n * 4096,
                               [[KC * T, 128], [1024, 4], [1, 1024]])
                dst = ap_off(x_out, (b * T + n * 512) * D,
                             [[1024, 128], [128 * 1024, 4], [1, 1024]])
                stw.inc(g.dma_start(out=dst, in_=srcb), 16)
            g.wait_ge(stw.h, stw.n)


def sub_l0_ffn(ctx, x_in, x_out, b, w):
    if b == 0 and not getattr(ctx, "wsc_cast_done", False):
        emit_l0_ffn_wcast(ctx, w)
        ctx.wsc_cast_done = True
    if b == 0:
        emit_l0_ffn_prep(ctx, w)
    emit_in_ln_tp(ctx, x_in, b)
    emit_l0_ffn_gate(ctx, b)
    emit_l0_ffn_mix(ctx, b, w)
    emit_out_resid_tm(ctx, x_in, x_out, b)


SUBLAYERS["l0_ffn"] = sub_l0_ffn





def emit_l0_attn_b_mix(ctx, b, w):
    nc = ctx.nc
    EPS = float(np.finfo(np.float32).eps)
    ld = ctx.sem("ab_ld")
    wtp = ctx.sem("ab_wtp")
    wte = ctx.sem("ab_wte")
    mm = ctx.sem("ab_mm")
    aev = ctx.sem("ab_ev")
    tp = ctx.sem("ab_tp")
    te = ctx.sem("ab_te")
    ks = ctx.sem("ab_ks")
    cx = ctx.sem("ab_cx")
    cwe = ctx.sem("ab_cwe")
    om = ctx.sem("ab_om")
    zr = ctx.sem("ab_zr")
    zb = ctx.sem("ab_zb")
    ze = ctx.sem("ab_ze")
    fin = ctx.sem("ab_fin")

    for hp in range(8):
        with nc.Block() as blk:
            @blk.sync
            def _(sync):
                wtp0 = wtp.n
                for wi in range(3):
                    if wi >= 2:
                        sync.wait_ge(wtp.h, wtp0 + (wi - 1) * 2)
                    off = (wi * 1024 + hp * 128) * D
                    ctx.sem(f"ab_ld{wi % 2}").inc(sync.dma_start(
                        out=ctx.C[:, (wi % 2) * 1024:(wi % 2) * 1024 + 1024],
                        in_=ap_off(w["w_qkv"], off, [[D, 128], [1, D]])), 16)

            @blk.tensor
            def _(t):
                slb = [ctx.sem("ab_ld0").n - 2 * 16, ctx.sem("ab_ld1").n - 16]
                wte0 = wte.n
                for wi in range(3):
                    t.wait_ge(ctx.sem(f"ab_ld{wi % 2}").h,
                              slb[wi % 2] + 16 * (wi // 2 + 1))
                    if wi >= 1:
                        t.wait_ge(wte.h, wte0 + wi)
                    for k in range(KC):
                        inst = t.transpose(
                            ctx.pb[2 + (k // 4) % 2][:, (k % 4) * 128:(k % 4) * 128 + 128],
                            ctx.C[:, (wi % 2) * 1024 + k * 128:
                                  (wi % 2) * 1024 + (k + 1) * 128],
                            ctx.ident[:, :])
                        if k == 3 or k == 7:
                            wtp.inc(inst)

            @blk.vector
            def _(v):
                wtp0 = wtp.n - 6
                wte0 = wte.n
                for wi in range(3):
                    for g in range(2):
                        v.wait_ge(wtp.h, wtp0 + wi * 2 + g + 1)
                        for k in range(g * 4, g * 4 + 4):
                            inst = v.tensor_copy(
                                ctx.Wb[:, wi * 1024 + k * 128:wi * 1024 + (k + 1) * 128],
                                ctx.pb[2 + g][:, (k % 4) * 128:(k % 4) * 128 + 128])
                    wte.inc(inst)

        with nc.Block() as blk:
            @blk.tensor
            def _(t):
                aev0, te0, cwe0, zr0, ze0, fin0 = aev.n, te.n, cwe.n, zr.n, ze.n, fin.n
                for gi in range(12):
                    wi, ncn = divmod(gi, 4)
                    if gi >= 2:
                        t.wait_ge(aev.h, aev0 + gi - 1)
                    for k in range(KC):
                        inst = t.matmul(ctx.pb[gi % 2][:, :],
                                        ctx.Wb[:, wi * 1024 + k * 128:wi * 1024 + (k + 1) * 128],
                                        ctx.a_fm(k, ncn * 512, 512),
                                        start=(k == 0), stop=(k == KC - 1))
                    mm.inc(inst)
                    if gi == 7:
                        t.wait_ge(aev.h, aev0 + 8)
                        for tc in range(16):
                            g = tc // 4
                            if g >= 2:
                                t.wait_ge(te.h, te0 + g - 1)
                            inst = t.transpose(
                                ctx.pb[2 + g % 2][:, (tc % 4) * 128:(tc % 4) * 128 + 128],
                                ctx.C[:, 2048 + tc * 128:2048 + (tc + 1) * 128],
                                ctx.ident[:, :])
                            if tc % 4 == 3:
                                tp.inc(inst)
                t.wait_ge(aev.h, aev0 + 12)
                for tc in range(16):
                    g = tc // 4
                    t.wait_ge(te.h, te0 + 3 + g)
                    inst = t.transpose(
                        ctx.pb[2 + g % 2][:, (tc % 4) * 128:(tc % 4) * 128 + 128],
                        ctx.C[:, 2048 + tc * 128:2048 + (tc + 1) * 128],
                        ctx.ident[:, :])
                    if tc % 4 == 3:
                        tp.inc(inst)
                if hp > 0:
                    t.wait_ge(cwe.h, cwe0)
                for tc in range(16):
                    t.wait_ge(te.h, te0 + 4 + tc // 4 + 1)
                    for h in range(2):
                        inst = t.matmul(
                            ctx.pb[4][0:128, h * 64:h * 64 + 64],
                            ctx.C[:, 4096 + tc * 128:4096 + (tc + 1) * 128],
                            ctx.C[:, 6144 + tc * 128 + h * 64:6144 + tc * 128 + h * 64 + 64],
                            start=(tc == 0 and h == 0), stop=(tc == 15),
                            skip_group_check=True)
                cx.inc(inst)
                t.wait_ge(cwe.h, cwe0 + 1)
                for g in range(8):
                    h, ncn = divmod(g, 4)
                    if g >= 2:
                        t.wait_ge(fin.h, fin0 + g - 1)
                    om.inc(t.matmul(ctx.pb[5 + g % 2][0:65, :],
                                    ctx.S[h * 64:h * 64 + 64, 300:365],
                                    ctx.C[h * 64:h * 64 + 64,
                                          ncn * 512:ncn * 512 + 512],
                                    start=True, stop=True))
                    t.wait_ge(zr.h, zr0 + g + 1)
                    if g >= 1:
                        t.wait_ge(ze.h, ze0 + g)
                    zb.inc(t.matmul(ctx.pb[7][:, :], ctx.ones_row[0:1, 0:128],
                                    ctx.S[0:1, 365:877], start=True, stop=True))

            @blk.scalar
            def _(s):
                mm0 = mm.n - 12
                tp0 = tp.n - 8
                ks0 = ks.n
                for gi in range(12):
                    wi, ncn = divmod(gi, 4)
                    s.wait_ge(mm.h, mm0 + gi + 1)
                    if wi == 2 and ncn == 0:
                        s.wait_ge(tp.h, tp0 + 4)
                        s.wait_ge(ks.h, ks0 + 1)
                    dst = ctx.C[:, wi * 2048 + ncn * 512:wi * 2048 + ncn * 512 + 512] \
                        if wi < 2 else \
                        ctx.C[:, 2048 + ncn * 512:2048 + ncn * 512 + 512]
                    aev.inc(s.activation(dst, ctx.pb[gi % 2][:, :],
                                         AF.Relu if wi < 2 else AF.Identity))

            @blk.vector
            def _(v):
                aev0 = aev.n - 12
                tp0 = tp.n - 8
                cx0 = cx.n - 1
                om0 = om.n - 8
                zb0 = zb.n - 8
                v.wait_ge(aev.h, aev0 + 8)
                ks.inc(v.tensor_reduce(ctx.S[:, 364:365], ctx.C[:, 2048:4096],
                                       axis=AX.X, op=ALU.add))
                for g in range(8):
                    v.wait_ge(tp.h, tp0 + g + 1)
                    base = 4096 if g < 4 else 6144
                    gg = g % 4
                    for tc in range(gg * 4, gg * 4 + 4):
                        inst = v.tensor_copy(
                            ctx.C[:, base + tc * 128:base + (tc + 1) * 128],
                            ctx.pb[2 + g % 2][:, (tc % 4) * 128:(tc % 4) * 128 + 128])
                    te.inc(inst)
                v.wait_ge(cx.h, cx0 + 1)
                v.tensor_copy(ctx.S[0:64, 300:364], ctx.pb[4][0:64, 0:64])
                cwe.inc(v.tensor_copy(ctx.S[64:128, 300:364],
                                      ctx.pb[4][64:128, 64:128]))
                for g in range(8):
                    h, ncn = divmod(g, 4)
                    v.wait_ge(om.h, om0 + g + 1)
                    inst = v.tensor_scalar(ctx.S[0:1, 365:877],
                                           ctx.pb[5 + g % 2][64:65, :], EPS,
                                           None, ALU.add)
                    ctx.fence(v, "f_dve", inst)
                    zr.inc(v.reciprocal(ctx.S[0:1, 365:877],
                                        ctx.S[0:1, 365:877]))
                    v.wait_ge(zb.h, zb0 + g + 1)
                    inst = v.tensor_copy(ctx.S[:, 896:1408], ctx.pb[7][:, :])
                    ze.inc(inst)
                    v.wait_ge(ze.h, ze.n)
                    inst = v.tensor_tensor(
                        ctx.B[h * 64:h * 64 + 64,
                              hp * T + ncn * 512:hp * T + ncn * 512 + 512],
                        ctx.pb[5 + g % 2][0:64, :],
                        ctx.S[0:64, 896:1408], ALU.mult)
                    fin.inc(inst)


def sub_l0_attn_b(ctx, x_in, x_out, b, w):
    emit_in_ln_tp(ctx, x_in, b)
    emit_l0_attn_b_mix(ctx, b, w)
    proj = dict(
        kchunks=8, kpart=128, transpose=True,
        w_rows=lambda of: ap_off(w["w_proj"], of * 128 * D, [[D, 128], [1, D]]),
        src=lambda k, t0, n: ctx.b_fm(k, t0, n),
        bias=w["b_proj"],
    )
    emit_out_phase(ctx, x_in, x_out, b, proj=proj)


SUBLAYERS["l0_attn_b"] = sub_l0_attn_b





def emit_l1_attn_a_prep(ctx, w):
    nc = ctx.nc
    ld = ctx.sem("aa_pld")
    tp = ctx.sem("aa_ptp")
    rc = ctx.sem("aa_prc")
    with nc.Block() as blk:
        @blk.sync
        def _(sync):
            ld.inc(sync.dma_start(out=ctx.R[0:1, 0:256],
                                  in_=w["bq"].rearrange("(o c) -> o c", o=1)), 16)
            ld.inc(sync.dma_start(out=ctx.R[0:1, 256:512],
                                  in_=w["bk"].rearrange("(o c) -> o c", o=1)), 16)
            ld.inc(sync.dma_start(
                out=ctx.R[0:1, 512:768],
                in_=ap_off(w["q_phase"], 0, [[256, 1], [1, 256]])), 16)
            ld.inc(sync.dma_start(
                out=ctx.R[0:1, 768:1024],
                in_=ap_off(w["k_phase"], 0, [[256, 1], [1, 256]])), 16)
            ld.inc(sync.dma_start(out=ctx.R[0:1, 1024:1025],
                                  in_=w["temperature"].rearrange("(o c) -> o c", o=1)), 16)
            ld.inc(sync.dma_start(out=ctx.S[0:8, 0:128],
                                  in_=w["bv"].rearrange("(a c) -> a c", c=128)), 16)
            sync.wait_ge(ld.h, ld.n)
        @blk.gpsimd
        def _(g):
            g.iota(ctx.C[:, 0:896], [[1, 896]], channel_multiplier=-1,
                   allow_small_or_imprecise_dtypes=True)
    with nc.Block() as blk:
        @blk.vector
        def _(v):
            v.tensor_scalar(ctx.C[:, 0:896], ctx.C[:, 0:896], 383.5,
                            None, ALU.is_ge)
            v.tensor_tensor(ctx.R[0:1, 0:256], ctx.R[0:1, 0:256],
                            ctx.R[0:1, 512:768], ALU.add)
            v.tensor_tensor(ctx.R[0:1, 256:512], ctx.R[0:1, 256:512],
                            ctx.R[0:1, 768:1024], ALU.add)
            inst = v.reciprocal(ctx.R[0:1, 1025:1026], ctx.R[0:1, 1024:1025])
            rc.inc(inst)
    with nc.Block() as blk:
        @blk.gpsimd
        def _(g):
            g.tensor_copy(ctx.Wb[:, 5632:6528], ctx.C[:, 0:896])

        @blk.tensor
        def _(t):
            t.wait_ge(rc.h, rc.n)
            t.matmul(ctx.pb[6][:, 0:1], ctx.ones_row[0:1, 0:128],
                     ctx.R[0:1, 1025:1026], start=True, stop=True)
            tp.inc(t.transpose(ctx.pb[6][:, 8:16], ctx.S[0:8, 0:128],
                               ctx.ident[0:8, 0:8]))
        @blk.vector
        def _(v):
            v.wait_ge(tp.h, tp.n)
            inst = v.tensor_copy(ctx.W[:, 7294:7295], ctx.pb[6][:, 0:1])
            ctx.fence(v, "f_dve", inst)
            v.tensor_copy(ctx.W[:, 7286:7294], ctx.pb[6][:, 8:16])
            inst = v.tensor_scalar(ctx.W[:, 6803:6804], ctx.W[:, 7294:7295],
                                   -1.0, None, ALU.mult)
            ctx.fence(v, "f_dve", inst)
            v.tensor_tensor(ctx.W[:, 6802:6803], ctx.W[:, 7294:7295],
                            ctx.W[:, 6803:6804], ALU.min)


def emit_l1_attn_a_mix(ctx, b, w, first):
    nc = ctx.nc
    ldw = ctx.sem("aa_ldw")
    qk = ctx.sem("aa_qk")
    qe = ctx.sem("aa_qe")
    sqd = ctx.sem("aa_sq")
    red = ctx.sem("aa_rd")
    lne = ctx.sem("aa_le")
    nrm = ctx.sem("aa_nm")
    bcs = ctx.sem("aa_bc")

    with nc.Block() as blk:
        @blk.sync
        def _(sync):
            for wi, wname in ((0, "Wq"), (1, "Wk")):
                src = ap_off(w[wname], 0, [[256, 128], [128 * 256, KC], [1, 256]])
                ldw.inc(sync.dma_start(
                    out=ctx.W[:, wi * 2048:(wi + 1) * 2048], in_=src), 16)
            sync.wait_ge(ldw.h, ldw.n)

        @blk.gpsimd
        def _(g):
            g.wait_ge(ldw.h, ldw.n)
            g.tensor_copy(ctx.Wb[:, 0:4096], ctx.W[:, 0:4096])

    NG = 2 * NT
    with nc.Block() as blk:
        @blk.tensor
        def _(t):
            bcs.inc(t.matmul(ctx.pb[6][:, 0:256], ctx.ones_row[0:1, 0:128],
                             ctx.R[0:1, 0:256], start=True, stop=True))
            bcs.inc(t.matmul(ctx.pb[7][:, 0:256], ctx.ones_row[0:1, 0:128],
                             ctx.R[0:1, 256:512], start=True, stop=True))
            qe0 = qe.n
            for gi in range(NG):
                wi, i = divmod(gi, NT)
                if gi >= 2:
                    t.wait_ge(qe.h, qe0 + gi - 1)
                for k in range(KC):
                    inst = t.matmul(ctx.pb[gi % 2][:, 0:256],
                                    ctx.a_fm(k, i * 128, 128),
                                    ctx.Wb[:, wi * 2048 + k * 256:wi * 2048 + (k + 1) * 256],
                                    start=(k == 0), stop=(k == KC - 1))
                qk.inc(inst)

        @blk.scalar
        def _(s):
            qe0 = qe.n
            red0 = red.n
            nrm0 = nrm.n
            for gi in range(NG + 1):
                if gi < NG:
                    s.wait_ge(qe.h, qe0 + gi + 1)
                    if gi >= 2:
                        s.wait_ge(red.h, red0 + gi - 1)
                    wi, i = divmod(gi, NT)
                    dst = ctx.C[:, wi * 4096 + i * 256:wi * 4096 + (i + 1) * 256]
                    sqd.inc(s.activation(
                        ctx.W[:, 4624 + (gi % 2) * 256:4624 + (gi % 2) * 256 + 256],
                        dst, AF.Square))
                if gi >= 1:
                    gp = gi - 1
                    s.wait_ge(red.h, red0 + gp + 1)
                    if gp >= 2:
                        s.wait_ge(nrm.h, nrm0 + gp - 1)
                    nsq = ctx.S[:, 224 + (gp % 2) * 16:240 + (gp % 2) * 16]
                    lnt = ctx.S[:, 192 + (gp % 2) * 16:208 + (gp % 2) * 16]
                    nn = ctx.S[:, 160 + (gp % 2) * 16:176 + (gp % 2) * 16]
                    inst = s.activation(lnt, nsq, AF.Ln)
                    ctx.fence(s, "f_act", inst)
                    lne.inc(s.activation(nn, lnt, AF.Exp, scale=0.5))

        @blk.vector
        def _(v):
            qk0, bc0 = qk.n - NG, bcs.n - 2
            sq0 = sqd.n - NG
            lne0 = lne.n - NG
            for gi in range(NG + 2):
                if gi < NG:
                    wi, i = divmod(gi, NT)
                    v.wait_ge(qk.h, qk0 + gi + 1)
                    if gi == 0:
                        v.wait_ge(bcs.h, bc0 + 2)
                        inst = v.tensor_copy(ctx.W[:, 4096:4352],
                                             ctx.pb[6][:, 0:256])
                        inst = v.tensor_copy(ctx.W[:, 4352:4608],
                                             ctx.pb[7][:, 0:256])
                        ctx.fence(v, "f_dve", inst)
                    dst = ctx.C[:, wi * 4096 + i * 256:wi * 4096 + (i + 1) * 256]
                    qe.inc(v.tensor_tensor(dst, ctx.pb[gi % 2][:, 0:256],
                                           ctx.W[:, 4096 + wi * 256:4352 + wi * 256],
                                           ALU.add))
                if gi >= 1 and gi - 1 < NG:
                    gp = gi - 1
                    v.wait_ge(sqd.h, sq0 + gp + 1)
                    sq3 = ctx.W[:, 4624 + (gp % 2) * 256:4624 + (gp % 2) * 256 + 256]
                    red.inc(v.tensor_reduce(
                        ctx.S[:, 224 + (gp % 2) * 16:240 + (gp % 2) * 16],
                        sq3.rearrange("p (h w) -> p h w", w=16),
                        axis=AX.X, op=ALU.add))
                if gi >= 2:
                    gp = gi - 2
                    wi, i = divmod(gp, NT)
                    v.wait_ge(lne.h, lne0 + gp + 1)
                    nn = ctx.S[:, 160 + (gp % 2) * 16:176 + (gp % 2) * 16]
                    rn = ctx.S[:, 128 + (gp % 2) * 16:144 + (gp % 2) * 16]
                    inst = v.tensor_scalar(rn, nn, 1e-12, None, ALU.max)
                    ctx.fence(v, "f_dve", inst)
                    inst = v.reciprocal(rn, rn)
                    ctx.fence(v, "f_dve", inst)
                    if wi == 0:
                        inst = v.tensor_scalar(rn, rn, ctx.W[:, 7294:7295],
                                               None, ALU.mult)
                        ctx.fence(v, "f_dve", inst)
                    dst = ctx.C[:, wi * 4096 + i * 256:wi * 4096 + (i + 1) * 256]
                    rnb = bass.AP(rn.tensor, rn.offset,
                                  [[1792, 128], [1, 16], [0, 16]])
                    d3 = dst.rearrange("p (h w) -> p h w", w=16)
                    nrm.inc(v.tensor_tensor(d3, d3, rnb, ALU.mult))

    stg = ctx.sem("aa_st")
    ste = ctx.sem("aa_se")
    vld = ctx.sem("aa_vld")
    vm = ctx.sem("aa_vm")
    ve = ctx.sem("aa_ve")
    sc = ctx.sem("aa_sc")
    ex = ctx.sem("aa_ex")
    mk = ctx.sem("aa_mk")
    ym = ctx.sem("aa_ym")
    zr = ctx.sem("aa_zr")
    zb = ctx.sem("aa_zb")
    zbe = ctx.sem("aa_zbe")
    fin = ctx.sem("aa_fin")

    with nc.Block() as blk:
        @blk.gpsimd
        def _(g):
            g.memset(bass.AP(ctx.Wb, 6528 + 64, [[7968, 128], [65, 16]]), 1.0)

    CHUNKS = [(scn, tc) for scn in range(4) for tc in range(4 * scn + 4)]
    NCH = len(CHUNKS)
    for h in range(16):
        with nc.Block() as blk:
            @blk.sync
            def _(sync):
                src = ap_off(w["Wv"], h * 64,
                             [[D, 128], [128 * D, KC], [1, 64]])
                vld.inc(sync.dma_start(out=ctx.W[:, 0:512], in_=src), 16)
                sync.wait_ge(vld.h, vld.n)

            @blk.gpsimd
            def _(g):
                vct = ctx.sem("aa_vc")
                g.wait_ge(vld.h, vld.n)
                vct.inc(g.tensor_copy(ctx.Wb[:, 4096:4608], ctx.W[:, 0:512]))

            @blk.tensor
            def _(t):
                ste0, ve0, ex0, mk0 = ste.n, ve.n, ex.n, mk.n
                zr0, zbe0, fin0 = zr.n, zbe.n, fin.n
                gq = 0
                for wi in range(2):
                    for i in range(NT):
                        bank = ctx.pb[6 + gq % 2]
                        if gq >= 2:
                            t.wait_ge(ste.h, ste0 + gq - 1)
                        inst = t.transpose(
                            bank[0:16, (i % 4) * 128:(i % 4) * 128 + 128],
                            ctx.C[:, wi * 4096 + i * 256 + h * 16:
                                  wi * 4096 + i * 256 + h * 16 + 16],
                            ctx.ident[:, :])
                        if i % 4 == 3:
                            stg.inc(inst)
                            gq += 1
                t.wait_ge(ctx.sem("aa_vc").h, ctx.sem("aa_vc").n)
                for i in range(NT):
                    if i >= 2:
                        t.wait_ge(ve.h, ve0 + i - 1)
                    for k in range(KC):
                        inst = t.matmul(ctx.pb[i % 2][:, 0:64],
                                        ctx.a_fm(k, i * 128, 128),
                                        ctx.Wb[:, 4096 + k * 64:4096 + (k + 1) * 64],
                                        start=(k == 0), stop=(k == KC - 1))
                    vm.inc(inst)
                t.wait_ge(ve.h, ve0 + NT)
                t.wait_ge(ste.h, ste0 + 8)
                npart = 0
                for g, (scn, tc) in enumerate(CHUNKS):
                    partial = tc >= 4 * scn
                    if g >= 2:
                        t.wait_ge(ex.h, ex0 + g - 1)
                    sc.inc(t.matmul(
                        ctx.pb[g % 2][:, :],
                        ctx.Wb[0:16, 2048 + tc * 128:2048 + tc * 128 + 128],
                        ctx.Wb[0:16, scn * 512:(scn + 1) * 512],
                        start=True, stop=True))
                    if partial:
                        npart += 1
                        t.wait_ge(mk.h, mk0 + npart)
                    else:
                        t.wait_ge(ex.h, ex0 + g + 1)
                    if tc == 0 and scn >= 2:
                        t.wait_ge(fin.h, fin0 + scn - 1)
                    ym.inc(t.matmul(
                        ctx.pb[2 + scn % 2][0:65, :],
                        ctx.Wb[:, 6528 + tc * 65:6528 + tc * 65 + 65],
                        ctx.Wb[:, 4608 + (g % 2) * 512:4608 + (g % 2) * 512 + 512],
                        start=(tc == 0), stop=(tc == 4 * scn + 3),
                        skip_group_check=True))
                    if tc == 4 * scn + 3:
                        t.wait_ge(zr.h, zr0 + scn + 1)
                        if scn >= 2:
                            t.wait_ge(zbe.h, zbe0 + scn - 1)
                        zb.inc(t.matmul(ctx.pb[6 + scn % 2][0:64, :],
                                        ctx.ones_row[0:1, 0:64],
                                        ctx.W[0:1, 3600:4112],
                                        start=True, stop=True))

            @blk.scalar
            def _(s):
                sc0 = sc.n - NCH
                ym0s = ym.n - NCH
                for g, (scn, tc) in enumerate(CHUNKS):
                    s.wait_ge(sc.h, sc0 + g + 1)
                    if g >= 2:
                        s.wait_ge(ym.h, ym0s + g - 1)
                    ex.inc(s.activation(
                        ctx.Wb[:, 4608 + (g % 2) * 512:4608 + (g % 2) * 512 + 512],
                        ctx.pb[g % 2][:, :], AF.Exp,
                        bias=ctx.W[:, 6802:6803]))

            @blk.vector
            def _(v):
                stg0 = stg.n - 8
                vm0 = vm.n - NT
                ex0v = ex.n - NCH
                ym0v = ym.n - NCH
                zb0 = zb.n - 4
                for gq in range(8):
                    v.wait_ge(stg.h, stg0 + gq + 1)
                    bank = ctx.pb[6 + gq % 2]
                    wi, grp = divmod(gq, 4)
                    if wi == 0:
                        dst = ctx.Wb[0:16, grp * 512:grp * 512 + 512]
                    else:
                        dst = ctx.Wb[0:16, 2048 + grp * 512:2048 + grp * 512 + 512]
                    ste.inc(v.tensor_copy(dst, bank[0:16, 0:512]))
                for i in range(NT):
                    v.wait_ge(vm.h, vm0 + i + 1)
                    ve.inc(v.tensor_copy(ctx.Wb[:, 6528 + i * 65:6528 + i * 65 + 64],
                                         ctx.pb[i % 2][:, 0:64]))
                for g, (scn, tc) in enumerate(CHUNKS):
                    if tc >= 4 * scn:
                        v.wait_ge(ex.h, ex0v + g + 1)
                        esl = ctx.Wb[:, 4608 + (g % 2) * 512:
                                     4608 + (g % 2) * 512 + 512]
                        off = 5632 + 384 - 128 * (tc - 4 * scn)
                        mk.inc(v.tensor_tensor(esl, esl,
                                               ctx.Wb[:, off:off + 512],
                                               ALU.mult))
                    if tc == 4 * scn + 3:
                        v.wait_ge(ym.h, ym0v + g + 1)
                        ybank = ctx.pb[2 + scn % 2]
                        zr.inc(v.reciprocal(ctx.W[0:1, 3600:4112],
                                            ybank[64:65, :]))
                        v.wait_ge(zb.h, zb0 + scn + 1)
                        zbe.inc(v.tensor_copy(ctx.W[0:64, 2576:3088],
                                              ctx.pb[6 + scn % 2][0:64, :]))
                        v.wait_ge(zbe.h, zbe.n)
                        bslice = ctx.B[(h % 2) * 64:(h % 2) * 64 + 64,
                                       (h // 2) * T + scn * 512:
                                       (h // 2) * T + scn * 512 + 512]
                        inst = v.tensor_tensor(bslice, ybank[0:64, :],
                                               ctx.W[0:64, 2576:3088], ALU.mult)
                        ctx.fence(v, "f_dve", inst)
                        bvcol = ctx.W[(h % 2) * 64:(h % 2) * 64 + 64,
                                      7286 + h // 2:7287 + h // 2]
                        inst = v.tensor_scalar(bslice, bslice, bvcol, None,
                                               ALU.add)
                        fin.inc(inst)


def sub_l1_attn_a(ctx, x_in, x_out, b, w):
    emit_l1_attn_a_prep(ctx, w)
    emit_in_ln_tp(ctx, x_in, b)
    emit_l1_attn_a_mix(ctx, b, w, b == 0)
    proj = dict(
        kchunks=8, kpart=128,
        w_src=lambda of: ap_off(w["Wo"], of * 128,
                                [[D, 128], [128 * D, KC], [1, 128]]),
        src=lambda k, t0, n: ctx.b_fm(k, t0, n),
        bias=w["bo"],
    )
    emit_out_phase(ctx, x_in, x_out, b, proj=proj)


SUBLAYERS["l1_attn_a"] = sub_l1_attn_a





def emit_l0_attn_a_prep(ctx, w):
    nc = ctx.nc
    ld = ctx.sem("da_pld")
    pr = ctx.sem("da_pr")
    with nc.Block() as blk:
        @blk.sync
        def _(sync):
            for i, nm in enumerate(["lambda_q1", "lambda_k1", "lambda_q2",
                                    "lambda_k2"]):
                ld.inc(sync.dma_start(out=ctx.R[0:1, i * 64:(i + 1) * 64],
                                      in_=w[nm].rearrange("(o c) -> o c", o=1)), 16)
            sync.wait_ge(ld.h, ld.n)
    with nc.Block() as blk:
        @blk.vector
        def _(v):
            prb = pr.n
            inst = v.tensor_tensor(ctx.R[0:1, 512:576], ctx.R[0:1, 0:64],
                                   ctx.R[0:1, 64:128], ALU.mult)
            ctx.fence(v, "f_dve", inst)
            v.tensor_reduce(ctx.R[0:1, 256:257], ctx.R[0:1, 512:576],
                            axis=AX.X, op=ALU.add)
            inst = v.tensor_tensor(ctx.R[0:1, 576:640], ctx.R[0:1, 128:192],
                                   ctx.R[0:1, 192:256], ALU.mult)
            ctx.fence(v, "f_dve", inst)
            inst = v.tensor_reduce(ctx.R[0:1, 257:258], ctx.R[0:1, 576:640],
                                   axis=AX.X, op=ALU.add)
            pr.inc(inst)
            v.wait_ge(pr.h, prb + 2)
            inst = v.tensor_tensor(ctx.R[0:1, 259:260], ctx.R[0:1, 259:260],
                                   ctx.R[0:1, 260:261], ALU.subtract)
            ctx.fence(v, "f_dve", inst)
            v.tensor_scalar(ctx.R[0:1, 258:259], ctx.R[0:1, 259:260],
                            0.8, None, ALU.add)
        @blk.scalar
        def _(s):
            s.wait_ge(pr.h, pr.n - 1 + 1)
            pr.inc(s.activation(ctx.R[0:1, 259:261], ctx.R[0:1, 256:258],
                                AF.Exp))


def emit_l0_attn_a_mix(ctx, b, w):
    nc = ctx.nc
    SCALE = 0.125
    lam = ctx.sem("da_lam")
    lme = ctx.sem("da_lme")
    nwl = ctx.sem("da_nwl")
    ldw = ctx.sem("da_ldw")
    qm = ctx.sem("da_qm")
    qv = ctx.sem("da_qv")
    nwb = ctx.sem("da_nw")
    nwe = ctx.sem("da_nwe")
    vt = ctx.sem("da_vt")
    vte = ctx.sem("da_vte")
    sc = ctx.sem("da_sc")
    ex = ctx.sem("da_ex")
    ym = ctx.sem("da_ym")
    cb = ctx.sem("da_cb")
    cbe = ctx.sem("da_cbe")
    ytp = ctx.sem("da_ytp")
    rm1 = ctx.sem("da_rm1")
    rm2 = ctx.sem("da_rm2")
    rm3 = ctx.sem("da_rm3")

    with nc.Block() as blk:
        @blk.sync
        def _(sync):
            nwl.inc(sync.dma_start(out=ctx.W[0:1, 4692:5716],
                                   in_=ap_off(w["norm_w"], 0,
                                              [[1024, 1], [1, 1024]])), 16)
            sync.wait_ge(nwl.h, nwl.n)
        @blk.tensor
        def _(t):
            lam.inc(t.matmul(ctx.pb[7][:, 256:257], ctx.ones_row[0:1, 0:128],
                             ctx.R[0:1, 258:259], start=True, stop=True))
        @blk.vector
        def _(v):
            v.wait_ge(lam.h, lam.n)
            lme.inc(v.tensor_copy(ctx.S[:, 256:257], ctx.pb[7][:, 256:257]))
        @blk.gpsimd
        def _(g):
            g.memset(bass.AP(ctx.Wb, 2560 + 64, [[7968, 128], [65, 16]]), 1.0)
    with nc.Block() as blk:
        @blk.vector
        def _(v):
            v.tensor_scalar(ctx.W[0:1, 4692:5716], ctx.W[0:1, 4692:5716],
                            0.2, None, ALU.mult)

    for h in range(16):
        with nc.Block() as blk:
            @blk.sync
            def _(sync):
                for wi in range(3):
                    srcw = ap_off(w["w_qkv"], wi * 1024 + h * 64,
                                  [[3072, 128], [128 * 3072, KC], [1, 64]])
                    ldw.inc(sync.dma_start(
                        out=ctx.W[:, wi * 512:wi * 512 + 512], in_=srcw), 16)
                sync.wait_ge(ldw.h, ldw.n)

            @blk.gpsimd
            def _(g):
                dct = ctx.sem("da_ct")
                g.wait_ge(ldw.h, ldw.n)
                dct.inc(g.tensor_copy(ctx.Wb[:, 0:1536], ctx.W[:, 0:1536]))

            @blk.tensor
            def _(t):
                qv0, vte0, nwe0 = qv.n, vte.n, nwe.n
                ex0, ym0 = ex.n, ym.n
                cbe0, ytp0, rm10, rm30 = cbe.n, ytp.n, rm1.n, rm3.n
                rm20t = ctx.sem("da_rm2").n
                t.wait_ge(ctx.sem("da_ct").h, ctx.sem("da_ct").n)
                nwb.inc(t.matmul(ctx.pb[7][:, 320:384],
                                 ctx.ones_row[0:1, 0:128],
                                 ctx.W[0:1, 4692 + h * 64:4692 + h * 64 + 64],
                                 start=True, stop=True))
                for g in range(3):
                    for ncn in range(4):
                        gi = g * 4 + ncn
                        if gi >= 2:
                            t.wait_ge(qv.h, qv0 + gi - 1)
                        pslice = ctx.pb[gi % 2][0:64, :] if g < 2 else \
                            ctx.pb[gi % 2][64:128, :]
                        for k in range(KC):
                            inst = t.matmul(
                                pslice,
                                ctx.Wb[:, g * 512 + k * 64:g * 512 + (k + 1) * 64],
                                ctx.a_fm(k, ncn * 512, 512),
                                start=(k == 0), stop=(k == KC - 1))
                        qm.inc(inst)
                t.wait_ge(qv.h, qv0 + 12)
                for tc in range(NT):
                    g2 = tc // 4
                    if g2 >= 2:
                        t.wait_ge(vte.h, vte0 + g2 - 1)
                    inst = t.transpose(
                        ctx.pb[4 + g2 % 2][:, (tc % 4) * 128:(tc % 4) * 128 + 64],
                        ctx.C[64:128, tc * 128:(tc + 1) * 128],
                        ctx.ident[64:128, 64:128])
                    if tc % 4 == 3:
                        vt.inc(inst)
                t.wait_ge(vte.h, vte0 + 4)
                t.wait_ge(nwe.h, nwe0 + 1)
                nit = 0
                for scn in range(4):
                    for mat in range(2):
                        ybank = 2 + mat
                        for tc in range(NT):
                            git = nit * NT + tc
                            if git >= 2:
                                t.wait_ge(ex.h, ex0 + git - 1)
                            inst = t.matmul(
                                ctx.pb[tc % 2][:, :],
                                ctx.Wb[mat * 32:mat * 32 + 32,
                                       5648 + tc * 128:5648 + (tc + 1) * 128],
                                ctx.Wb[mat * 32:mat * 32 + 32,
                                       3600 + scn * 512:3600 + (scn + 1) * 512],
                                start=True, stop=True)
                            sc.inc(inst)
                            if tc >= 1:
                                t.wait_ge(ex.h, ex0 + git)
                                ti = tc - 1
                                inst = t.matmul(
                                    ctx.pb[ybank][0:65, :],
                                    ctx.Wb[:, 2560 + ti * 65:2560 + ti * 65 + 65],
                                    ctx.Wb[:, 1536 + (ti % 2) * 512:
                                           1536 + (ti % 2) * 512 + 512],
                                    start=(ti == 0), stop=False,
                                    skip_group_check=True)
                        t.wait_ge(ex.h, ex0 + nit * NT + NT)
                        inst = t.matmul(
                            ctx.pb[ybank][0:65, :],
                            ctx.Wb[:, 2560 + 15 * 65:2560 + 15 * 65 + 65],
                            ctx.Wb[:, 1536 + (15 % 2) * 512:
                                   1536 + (15 % 2) * 512 + 512],
                            start=False, stop=True, skip_group_check=True)
                        ym.inc(inst)
                        nit += 1
                    t.wait_ge(rm1.h, rm10 + scn + 1)
                    t.wait_ge(cbe.h, cbe0 + max(0, (scn - 1) * 2))
                    cb.inc(t.matmul(ctx.pb[6][0:64, :], ctx.ones_row[0:1, 0:64],
                                    ctx.S[0:1, 1280:1792], start=True,
                                    stop=True))
                    cb.inc(t.matmul(ctx.pb[7][0:64, 0:512],
                                    ctx.ones_row[0:1, 0:64],
                                    ctx.W[0:1, 4112:4624], start=True,
                                    stop=True))
                    t.wait_ge(cbe.h, cbe0 + scn * 2 + 2)
                    if scn >= 1:
                        t.wait_ge(rm2.h, rm20t + (scn - 1) * 4 + 4)
                    for q4 in range(4):
                        inst = t.transpose(
                            ctx.pb[4][:, q4 * 128:q4 * 128 + 64],
                            ctx.W[0:64, 3600 + q4 * 128:3600 + (q4 + 1) * 128],
                            ctx.ident[0:64, 0:64])
                    ytp.inc(inst)

            @blk.scalar
            def _(s):
                sc0 = sc.n - 4 * 2 * NT
                rm20 = rm2.n
                rstd = ctx.sem("da_rstd")

                def rms_chain(scn2):
                    for q4 in range(4):
                        s.wait_ge(rm2.h, rm20 + scn2 * 4 + q4 + 1)
                        buf = ctx.W[:, 5716 + (scn2 % 2) * 256 + q4 * 64:
                                    5716 + (scn2 % 2) * 256 + q4 * 64 + 64]
                        inst = s.activation(ctx.W[:, 6228:6292], buf,
                                            AF.Square,
                                            accum_out=ctx.W[:, 6292:6293])
                        ctx.fence(s, "f_act", inst)
                        inst = s.activation(ctx.W[:, 6293:6294],
                                            ctx.W[:, 6292:6293], AF.Ln,
                                            scale=1.0 / 64.0,
                                            bias=ctx.ceps[:, :])
                        ctx.fence(s, "f_act", inst)
                        rstd.inc(s.activation(
                            ctx.W[:, 6294 + (scn2 % 2) * 4 + q4:
                                  6295 + (scn2 % 2) * 4 + q4],
                            ctx.W[:, 6293:6294], AF.Exp, scale=-0.5))

                nit = 0
                for scn in range(4):
                    for mat in range(2):
                        for tc in range(NT):
                            git = nit * NT + tc
                            s.wait_ge(sc.h, sc0 + git + 1)
                            ex.inc(s.activation(
                                ctx.Wb[:, 1536 + (tc % 2) * 512:
                                       1536 + (tc % 2) * 512 + 512],
                                ctx.pb[tc % 2][:, :], AF.Exp, scale=SCALE))
                        nit += 1
                    if scn >= 1:
                        rms_chain(scn - 1)
                rms_chain(3)

            @blk.vector
            def _(v):
                qm0 = qm.n - 12
                vt0 = vt.n - 4
                nwb0 = nwb.n - 1
                ym0 = ym.n - 8
                cb0 = cb.n - 8
                ytp0 = ytp.n - 4
                rm20 = rm2.n
                for gi in range(12):
                    g, ncn = divmod(gi, 4)
                    v.wait_ge(qm.h, qm0 + gi + 1)
                    if g == 0:
                        dst = ctx.Wb[0:64, 3600 + ncn * 512:3600 + ncn * 512 + 512]
                        srcp = ctx.pb[gi % 2][0:64, :]
                    elif g == 1:
                        dst = ctx.Wb[0:64, 5648 + ncn * 512:5648 + ncn * 512 + 512]
                        srcp = ctx.pb[gi % 2][0:64, :]
                    else:
                        dst = ctx.C[64:128, ncn * 512:ncn * 512 + 512]
                        srcp = ctx.pb[gi % 2][64:128, :]
                    qv.inc(v.tensor_copy(dst, srcp))
                v.wait_ge(nwb.h, nwb0 + 1)
                nwe.inc(v.tensor_copy(ctx.W[:, 4624:4688], ctx.pb[7][:, 320:384]))
                for g2 in range(4):
                    v.wait_ge(vt.h, vt0 + g2 + 1)
                    bank = ctx.pb[4 + g2 % 2]
                    srcv = bass.AP(bank, 0, [[512, 128], [128, 4], [1, 64]])
                    dstv = bass.AP(ctx.Wb, 2560 + g2 * 4 * 65,
                                   [[7968, 128], [65, 4], [1, 64]])
                    vte.inc(v.tensor_copy(dstv, srcv))
                rstd = ctx.sem("da_rstd")
                rstd0 = rstd.n - 16

                def rstd_mults(scn2):
                    for q4 in range(4):
                        si = scn2 * 4 + q4
                        v.wait_ge(rstd.h, rstd0 + scn2 * 4 + q4 + 1)
                        buf = ctx.W[:, 5716 + (scn2 % 2) * 256 + q4 * 64:
                                    5716 + (scn2 % 2) * 256 + q4 * 64 + 64]
                        inst = v.tensor_scalar(buf, buf,
                                               ctx.W[:, 6294 + (scn2 % 2) * 4 + q4:
                                                     6295 + (scn2 % 2) * 4 + q4],
                                               None, ALU.mult)
                        ctx.fence(v, "f_dve", inst)
                        dstB = bass.AP(ctx.B,
                                       (si % 8) * T + 2 * h + (si // 8),
                                       [[KC * T, 128], [32, 64]])
                        inst = v.tensor_tensor(dstB, buf,
                                               ctx.W[:, 4624:4688], ALU.mult)
                        rm3.inc(inst)

                for scn in range(4):
                    v.wait_ge(ym.h, ym0 + scn * 2 + 2)
                    inst = v.reciprocal(ctx.S[0:1, 1280:1792],
                                        ctx.pb[2][64:65, :])
                    ctx.fence(v, "f_dve", inst)
                    inst = v.reciprocal(ctx.W[0:1, 4112:4624],
                                        ctx.pb[3][64:65, :])
                    ctx.fence(v, "f_dve", inst)
                    inst = v.tensor_scalar(ctx.W[0:1, 4112:4624],
                                           ctx.W[0:1, 4112:4624],
                                           ctx.S[0:1, 256:257], None,
                                           ALU.mult)
                    rm1.inc(inst)
                    v.wait_ge(cb.h, cb0 + scn * 2 + 2)
                    inst = v.tensor_copy(ctx.W[0:64, 2576:3088],
                                         ctx.pb[6][0:64, :])
                    cbe.inc(inst)
                    inst = v.tensor_copy(ctx.W[0:64, 3088:3600],
                                         ctx.pb[7][0:64, 0:512])
                    ctx.fence(v, "f_dve", inst)
                    inst = v.tensor_tensor(ctx.W[0:64, 3600:4112],
                                           ctx.pb[2][0:64, :],
                                           ctx.W[0:64, 2576:3088], ALU.mult)
                    ctx.fence(v, "f_dve", inst)
                    inst = v.tensor_tensor(ctx.W[0:64, 2576:3088],
                                           ctx.pb[3][0:64, :],
                                           ctx.W[0:64, 3088:3600], ALU.mult)
                    ctx.fence(v, "f_dve", inst)
                    inst = v.tensor_tensor(ctx.W[0:64, 3600:4112],
                                           ctx.W[0:64, 3600:4112],
                                           ctx.W[0:64, 2576:3088], ALU.subtract)
                    cbe.inc(inst)
                    v.wait_ge(ytp.h, ytp0 + scn + 1)
                    for q4 in range(4):
                        buf = ctx.W[:, 5716 + (scn % 2) * 256 + q4 * 64:
                                    5716 + (scn % 2) * 256 + q4 * 64 + 64]
                        rm2.inc(v.tensor_copy(buf, ctx.pb[4][:, q4 * 128:
                                                             q4 * 128 + 64]))
                    if scn >= 1:
                        rstd_mults(scn - 1)
                rstd_mults(3)


def sub_l0_attn_a(ctx, x_in, x_out, b, w):
    if b == 0:
        emit_l0_attn_a_prep(ctx, w)
    emit_in_ln_tp(ctx, x_in, b)
    emit_l0_attn_a_mix(ctx, b, w)
    proj = dict(
        kchunks=8, kpart=128,
        w_src=lambda of: ap_off(w["wo"], of * 128,
                                [[D, 128], [128 * D, KC], [1, 128]]),
        src=lambda k, t0, n: ctx.b_fm(k, t0, n),
        bias=None,
    )
    emit_out_phase(ctx, x_in, x_out, b, proj=proj)


SUBLAYERS["l0_attn_a"] = sub_l0_attn_a





def emit_l1_attn_b_masks(ctx):
    nc = ctx.nc
    mk = ctx.sem("rk_mk")
    mke = ctx.sem("rk_mke")
    SC = 5506
    for li, lv in enumerate([1, 2, 4, 8, 16, 32, 64]):
        with nc.Block() as blk:
            @blk.gpsimd
            def _(g):
                g.iota(ctx.W[0:64, SC:SC + 128], [[1, 128]],
                       channel_multiplier=-2 * lv,
                       allow_small_or_imprecise_dtypes=True)
        with nc.Block() as blk:
            @blk.vector
            def _(v):
                it = ctx.W[0:64, SC:SC + 128]
                lb = ctx.W[0:64, SC + 128:SC + 256]
                ub = ctx.W[0:64, SC + 256:SC + 384]
                t1 = ctx.W[0:64, SC + 384:SC + 512]
                inst = v.tensor_scalar(t1, it, -0.5, None, ALU.is_ge)
                ctx.fence(v, "f_dve", inst)
                inst = v.tensor_scalar(lb, it, lv - 0.5, None, ALU.is_le)
                ctx.fence(v, "f_dve", inst)
                inst = v.tensor_tensor(lb, lb, t1, ALU.mult)
                ctx.fence(v, "f_dve", inst)
                inst = v.tensor_scalar(ub, it, lv - 0.5, None, ALU.is_ge)
                ctx.fence(v, "f_dve", inst)
                inst = v.tensor_scalar(t1, it, 2 * lv - 0.5, None, ALU.is_le)
                ctx.fence(v, "f_dve", inst)
                mk.inc(v.tensor_tensor(ub, ub, t1, ALU.mult))
        with nc.Block() as blk:
            @blk.tensor
            def _(t):
                mke.inc(t.matmul(ctx.pb[6][:, 0:128],
                                 ctx.W[0:64, SC + 128:SC + 256],
                                 ctx.W[0:64, SC + 256:SC + 384],
                                 start=True, stop=True))
            @blk.vector
            def _(v):
                v.wait_ge(mke.h, mke.n)
                v.tensor_copy(ctx.W[:, (li + 1) * 128:(li + 2) * 128],
                              ctx.pb[6][:, 0:128])
                if li == 0:
                    v.tensor_copy(ctx.W[:, 0:128], ctx.ident[:, :])


def emit_l1_attn_b_mix(ctx, b, w):
    nc = ctx.nc
    LV = [1, 2, 4, 8, 16, 32, 64]
    NCH = 16
    ld = ctx.sem("rk_ld")
    dtp = ctx.sem("rk_dtp")
    wtp = ctx.sem("rk_wtp")
    wte = ctx.sem("rk_wte")
    pm = ctx.sem("rk_pm")
    pv = ctx.sem("rk_pv")
    sig = ctx.sem("rk_sig")
    ktp = ctx.sem("rk_ktp")
    kte = ctx.sem("rk_kte")
    tbl = ctx.sem("rk_tbl")
    fac = ctx.sem("rk_fac")
    lm = ctx.sem("rk_lm")
    la = ctx.sem("rk_la")
    ym = ctx.sem("rk_ym")
    yev = ctx.sem("rk_yev")
    smm = ctx.sem("rk_sm")
    sev = ctx.sem("rk_sev")

    emit_l1_attn_b_masks(ctx)

    with nc.Block() as blk:
        @blk.sync
        def _(sync):
            ld.inc(sync.dma_start(out=ctx.W[0:16, 6018:6082],
                                  in_=w["time_decay"]), 16)
            ld.inc(sync.dma_start(out=ctx.W[0:16, 6082:6146],
                                  in_=w["time_first"]), 16)
            sync.wait_ge(ld.h, ld.n)
    with nc.Block() as blk:
        @blk.tensor
        def _(t):
            t.transpose(ctx.pb[6][0:64, 0:16], ctx.W[0:16, 6018:6082],
                        ctx.ident[0:16, 0:16])
            dtp.inc(t.transpose(ctx.pb[6][0:64, 16:32], ctx.W[0:16, 6082:6146],
                                ctx.ident[0:16, 0:16]))
        @blk.vector
        def _(v):
            v.wait_ge(dtp.h, dtp.n)
            v.tensor_copy(ctx.S[0:64, 224:240], ctx.pb[6][0:64, 0:16])
            v.tensor_copy(ctx.S[0:64, 240:256], ctx.pb[6][0:64, 16:32])

    for h in range(16):
        with nc.Block() as blk:
            @blk.sync
            def _(sync):
                wtp0 = wtp.n
                for wi in range(3):
                    nm = ["Wk", "Wv", "Wr"][wi]
                    if wi >= 1:
                        sync.wait_ge(wtp.h, wtp0 + wi * 2)
                    ld.inc(sync.dma_start(
                        out=ctx.C[0:64, 6144:7168],
                        in_=ap_off(w[nm], h * 64 * D, [[D, 64], [1, D]])), 16)

            @blk.tensor
            def _(t):
                ld0 = ld.n - 3 * 16
                wte0 = wte.n
                for wi in range(3):
                    t.wait_ge(ld.h, ld0 + 16 * (wi + 1))
                    if wi >= 1:
                        t.wait_ge(wte.h, wte0 + wi * 2)
                    for k in range(KC):
                        if k == 4:
                            t.wait_ge(wte.h, wte0 + wi * 2 + 1)
                        inst = t.transpose(
                            ctx.pb[6 + k % 2][:, (k // 2 % 2) * 128:
                                              (k // 2 % 2) * 128 + 64],
                            ctx.C[0:64, 6144 + k * 128:6144 + (k + 1) * 128],
                            ctx.ident[0:64, 0:64])
                        if k % 4 == 3:
                            wtp.inc(inst)

            @blk.vector
            def _(v):
                wtp0 = wtp.n - 6
                for wi in range(3):
                    for g2 in range(2):
                        v.wait_ge(wtp.h, wtp0 + wi * 2 + g2 + 1)
                        for k in range(g2 * 4, g2 * 4 + 4):
                            inst = v.tensor_copy(
                                ctx.Wb[:, 4096 + wi * 512 + k * 64:
                                       4096 + wi * 512 + (k + 1) * 64],
                                ctx.pb[6 + k % 2][:, (k // 2 % 2) * 128:
                                                  (k // 2 % 2) * 128 + 64])
                        wte.inc(inst)

        with nc.Block() as blk:
            @blk.tensor
            def _(t):
                pv0, kte0, tbl0 = pv.n, kte.n, tbl.n
                fac0, la0, yev0, sev0 = fac.n, la.n, yev.n, sev.n
                for g in range(3):
                    for ncn in range(4):
                        gi = g * 4 + ncn
                        if gi >= 2:
                            t.wait_ge(pv.h, pv0 + gi - 1)
                        for k in range(KC):
                            inst = t.matmul(
                                ctx.pb[gi % 2][0:64, :],
                                ctx.Wb[:, 4096 + g * 512 + k * 64:
                                       4096 + g * 512 + (k + 1) * 64],
                                ctx.a_fm(k, ncn * 512, 512),
                                start=(k == 0), stop=(k == KC - 1))
                        pm.inc(inst)
                t.wait_ge(pv.h, pv0 + 12)
                for g in range(2):
                    for tc in range(NCH):
                        g2 = (g * NCH + tc) // 4
                        if g2 >= 2:
                            t.wait_ge(kte.h, kte0 + g2 - 1)
                        inst = t.transpose(
                            ctx.pb[6 + g2 % 2][:, (tc % 4) * 128:(tc % 4) * 128 + 64],
                            ctx.C[0:64, g * 2048 + tc * 128:g * 2048 + (tc + 1) * 128],
                            ctx.ident[0:64, 0:64])
                        if tc % 4 == 3:
                            ktp.inc(inst)
                t.wait_ge(tbl.h, tbl0 + 1)
                t.wait_ge(kte.h, kte0 + 8)
                ktp.inc(t.transpose(ctx.pb[6][:, 448:512],
                                    ctx.W[0:64, 2946:3074],
                                    ctx.ident[0:64, 0:64]))
                t.wait_ge(kte.h, kte0 + 9)
                for c in range(NCH):
                    t.wait_ge(fac.h, fac0 + c + 1)
                    if c >= 1:
                        t.wait_ge(la.h, la0 + c)
                    for li in range(8):
                        bank = ctx.pb[li // 4]
                        qof = (li % 4) * 128
                        if li == 0:
                            lhs = ctx.W[0:64, 3394:3522]
                            rhs = ctx.C[0:64, 4096 + c * 128:4096 + (c + 1) * 128]
                        else:
                            lhs = ctx.W[0:64, 3522 + (li - 1) * 128:
                                        3522 + li * 128]
                            rhs = ctx.W[0:64, 4418 + (li - 1) * 128:
                                        4418 + li * 128]
                        inst = t.matmul(bank[:, qof:qof + 128], lhs, rhs,
                                        start=(li % 4 == 0), stop=True,
                                        skip_group_check=True)
                    lm.inc(inst)
                    t.wait_ge(la.h, la0 + c + 1)
                    if c >= 1:
                        t.wait_ge(yev.h, yev0 + c)
                        t.wait_ge(sev.h, sev0 + c)
                    t.matmul(ctx.pb[2][0:64, 0:128],
                             ctx.C[:, 7168 + c * 64:7168 + (c + 1) * 64],
                             ctx.W[:, 3138:3266], start=True, stop=False,
                             skip_group_check=True)
                    ym.inc(t.matmul(ctx.pb[2][0:64, 0:128],
                                    ctx.W[0:64, 3266 + (c % 2) * 64:
                                          3330 + (c % 2) * 64],
                                    ctx.W[0:64, 5378:5506],
                                    start=False, stop=True,
                                    skip_group_check=True))
                    if c >= 1:
                        t.wait_ge(sev.h, sev0 + c)
                    smm.inc(t.matmul(ctx.pb[5][0:64, 0:64],
                                     ctx.W[:, 5314:5378],
                                     ctx.C[:, 7168 + c * 64:7168 + (c + 1) * 64],
                                     start=True, stop=True))

            @blk.scalar
            def _(s):
                pm0 = pm.n - 12
                for ncn in range(4):
                    gi = 8 + ncn
                    s.wait_ge(pm.h, pm0 + gi + 1)
                    sig.inc(s.activation(
                        ctx.C[0:64, 4096 + ncn * 512:4096 + (ncn + 1) * 512],
                        ctx.pb[gi % 2][0:64, :], AF.Exp, scale=-1.0))

            @blk.vector
            def _(v):
                pm0 = pm.n - 12
                sg0 = sig.n - 4
                ktp0 = ktp.n - 9
                lm0 = lm.n - NCH
                ym0 = ym.n - NCH
                sm0 = smm.n - NCH
                for gi in range(8):
                    g, ncn = divmod(gi, 4)
                    v.wait_ge(pm.h, pm0 + gi + 1)
                    pv.inc(v.tensor_copy(
                        ctx.C[0:64, g * 2048 + ncn * 512:g * 2048 + (ncn + 1) * 512],
                        ctx.pb[gi % 2][0:64, :]))
                for ncn in range(4):
                    v.wait_ge(sig.h, sg0 + ncn + 1)
                    dst = ctx.C[0:64, 4096 + ncn * 512:4096 + (ncn + 1) * 512]
                    inst = v.tensor_scalar(dst, dst, 1.0, None, ALU.add)
                    ctx.fence(v, "f_dve", inst)
                    pv.inc(v.reciprocal(dst, dst))
                dcol = ctx.S[0:64, 224 + h:225 + h]
                tcol = ctx.S[0:64, 240 + h:241 + h]
                dp = lambda a, bb: ctx.W[0:64, 2816 + a:2816 + bb]
                rp = lambda a, bb: ctx.W[0:64, 2945 + a:2945 + bb]
                v.memset(dp(0, 1), 1.0)
                inst = v.tensor_scalar(dp(1, 2), dcol, 1.0, None, ALU.mult)
                ctx.fence(v, "f_dve", inst)
                p = 1
                while p < 128:
                    q = min(p, 128 - p)
                    inst = v.tensor_scalar(dp(p + 1, p + 1 + q), dp(1, 1 + q),
                                           dp(p, p + 1), None, ALU.mult)
                    ctx.fence(v, "f_dve", inst)
                    p *= 2
                inst = v.tensor_scalar(rp(128, 129), tcol, 1.0, None,
                                       ALU.mult)
                ctx.fence(v, "f_dve", inst)
                inst = v.tensor_scalar(rp(127, 128), rp(128, 129),
                                       dp(1, 2), None, ALU.mult)
                ctx.fence(v, "f_dve", inst)
                p = 2
                while p <= 64:
                    inst = v.tensor_scalar(rp(129 - 2 * p, 129 - p),
                                           rp(129 - p, 129), dp(p, p + 1),
                                           None, ALU.mult)
                    if p < 64:
                        ctx.fence(v, "f_dve", inst)
                    p *= 2
                tbl.inc(inst)
                v.wait_ge(tbl.h, tbl.n)
                for li, lv in enumerate(LV):
                    pr = ctx.W[0:64, 1024 + li * 128:1024 + (li + 1) * 128]
                    pk = ctx.W[0:64, 1920 + li * 128:1920 + (li + 1) * 128]
                    v.memset(pr, 0.0)
                    inst = v.memset(pk, 0.0)
                    ctx.fence(v, "f_dve", inst)
                    nb = 128 // (2 * lv)
                    prv = bass.AP(pr.tensor, pr.offset + lv,
                                  [[8192, 64], [2 * lv, nb], [1, lv]])
                    dpv = bass.AP(ctx.W, 2816, [[8192, 64], [0, nb], [1, lv]])
                    v.tensor_scalar(prv, dpv, 1.0, None, ALU.mult)
                    pkv = bass.AP(pk.tensor, pk.offset,
                                  [[8192, 64], [2 * lv, nb], [1, lv]])
                    rpv = bass.AP(ctx.W, 2945 + 128 - lv,
                                  [[8192, 64], [0, nb], [1, lv]])
                    inst = v.tensor_scalar(pkv, rpv, 1.0, None, ALU.mult)
                ctx.fence(v, "f_dve", inst)
                for g2 in range(8):
                    v.wait_ge(ktp.h, ktp0 + g2 + 1)
                    bank = ctx.pb[6 + g2 % 2]
                    base = 6144 if g2 < 4 else 7168
                    gg = g2 % 4
                    srcv = bass.AP(bank, 0, [[512, 128], [128, 4], [1, 64]])
                    dstv = bass.AP(ctx.C, base + gg * 4 * 64,
                                   [[8192, 128], [64, 4], [1, 64]])
                    kte.inc(v.tensor_copy(dstv, srcv))
                v.wait_ge(ktp.h, ktp0 + 9)
                kte.inc(v.tensor_copy(ctx.W[:, 3074:3138], ctx.pb[6][:, 448:512]))
                v.memset(ctx.W[0:64, 3266:3330], 0.0)
                inst = v.memset(ctx.W[0:64, 3330:3394], 0.0)
                ctx.fence(v, "f_dve", inst)
                for c in range(NCH):
                    kch = ctx.C[0:64, c * 128:(c + 1) * 128]
                    rch = ctx.C[0:64, 4096 + c * 128:4096 + (c + 1) * 128]
                    ktk = ctx.C[:, 6144 + c * 64:6144 + (c + 1) * 64]
                    v.tensor_scalar(ctx.W[0:64, 3394:3522], kch,
                                    rp(128, 129), None, ALU.mult)
                    kbc = bass.AP(ctx.C, c * 128, [[8192, 64], [0, 7], [1, 128]])
                    v.tensor_tensor(ctx.W[0:64, 3522:4418], kbc,
                                    ctx.W[0:64, 1920:2816], ALU.mult)
                    rbc = bass.AP(ctx.C, 4096 + c * 128,
                                  [[8192, 64], [0, 7], [1, 128]])
                    v.tensor_tensor(ctx.W[0:64, 4418:5314], rbc,
                                    ctx.W[0:64, 1024:1920], ALU.mult)
                    v.tensor_tensor(ctx.W[:, 5314:5378], ktk,
                                    ctx.W[:, 3074:3138], ALU.mult)
                    inst = v.tensor_tensor(ctx.W[0:64, 5378:5506], rch,
                                           dp(1, 129), ALU.mult)
                    fac.inc(inst)
                    v.wait_ge(lm.h, lm0 + c + 1)
                    v.tensor_tensor(ctx.S[:, 512:1024], ctx.pb[0][:, 0:512],
                                    ctx.W[:, 0:512], ALU.mult)
                    inst = v.tensor_tensor(ctx.S[:, 1024:1536],
                                           ctx.pb[1][:, 0:512],
                                           ctx.W[:, 512:1024], ALU.mult)
                    ctx.fence(v, "f_dve", inst)
                    inst = v.tensor_tensor(ctx.S[:, 512:1024],
                                           ctx.S[:, 512:1024],
                                           ctx.S[:, 1024:1536], ALU.add)
                    ctx.fence(v, "f_dve", inst)
                    inst = v.tensor_tensor(ctx.S[:, 512:768],
                                           ctx.S[:, 512:768],
                                           ctx.S[:, 768:1024], ALU.add)
                    ctx.fence(v, "f_dve", inst)
                    inst = v.tensor_tensor(ctx.W[:, 3138:3266],
                                           ctx.S[:, 512:640],
                                           ctx.S[:, 640:768], ALU.add)
                    la.inc(inst)
                    v.wait_ge(ym.h, ym0 + c + 1)
                    yev.inc(v.tensor_copy(
                        ctx.B[(h % 2) * 64:(h % 2) * 64 + 64,
                              (h // 2) * T + c * 128:(h // 2) * T + (c + 1) * 128],
                        ctx.pb[2][0:64, 0:128]))
                    v.wait_ge(smm.h, sm0 + c + 1)
                    inst = v.tensor_scalar(ctx.S[0:64, 1536:1600],
                                           ctx.W[0:64, 3266 + (c % 2) * 64:
                                                 3330 + (c % 2) * 64],
                                           dp(128, 129), None, ALU.mult)
                    ctx.fence(v, "f_dve", inst)
                    inst = v.tensor_tensor(ctx.W[0:64, 3266 + ((c + 1) % 2) * 64:
                                                 3330 + ((c + 1) % 2) * 64],
                                           ctx.pb[5][0:64, 0:64],
                                           ctx.S[0:64, 1536:1600], ALU.add)
                    sev.inc(inst)
                    v.wait_ge(sev.h, sev.n)


def sub_l1_attn_b(ctx, x_in, x_out, b, w):
    emit_in_ln_tp(ctx, x_in, b)
    emit_l1_attn_b_mix(ctx, b, w)
    proj = dict(
        kchunks=8, kpart=128, transpose=True,
        w_rows=lambda of: ap_off(w["Wo"], of * 128 * D, [[D, 128], [1, D]]),
        src=lambda k, t0, n: ctx.b_fm(k, t0, n),
        bias=None,
    )
    emit_out_phase(ctx, x_in, x_out, b, proj=proj)


SUBLAYERS["l1_attn_b"] = sub_l1_attn_b





CHAIN = [
    ("l0_attn_a", "L0_attn_a", ["w_qkv", "wo", "lambda_q1", "lambda_q2",
                                "lambda_k1", "lambda_k2", "norm_w"]),
    ("l0_ffn", "L0_ffn", ["gate_w", "gate_b", "sw1", "sb1", "sw2", "sb2",
                          "rw1", "rb1", "rw2", "rb2"]),
    ("l0_attn_b", "L0_attn_b", ["w_qkv", "w_proj", "b_proj"]),
    ("l0_ffn", "L0_ffn", ["gate_w", "gate_b", "sw1", "sb1", "sw2", "sb2",
                          "rw1", "rb1", "rw2", "rb2"]),
    ("l1_attn_a", "L1_attn_a", ["Wq", "bq", "Wk", "bk", "Wv", "bv", "Wo",
                                "bo", "q_phase", "k_phase", "temperature"]),
    ("l1_ffn", "L1_ffn", ["Win", "bin_", "W1", "b1", "W2", "b2", "Wout",
                          "bout"]),
    ("l1_attn_b", "L1_attn_b", ["Wk", "Wv", "Wr", "Wo", "time_decay",
                                "time_first"]),
    ("l1_ffn", "L1_ffn", ["Win", "bin_", "W1", "b1", "W2", "b2", "Wout",
                          "bout"]),
]



def build_fused(wshapes_by_stage):
    nc = bass.Bass("TRN2", target_bir_lowering=False)
    x_in = nc.dram_tensor("x", [2, T, D], F32, kind="ExternalInput").ap()
    y_out = nc.dram_tensor("y", [2, T, D], F32, kind="ExternalOutput").ap()
    xa = nc.dram_tensor("xa", [2, T, D], F32, kind="Internal").ap()
    xb = nc.dram_tensor("xb", [2, T, D], F32, kind="Internal").ap()
    wsc = nc.dram_tensor("wsc", [192 * 128, 2048], BF, kind="Internal").ap()
    waps = []
    for si, (which, prefix, wnames) in enumerate(CHAIN):
        waps.append({nm: nc.dram_tensor(f"s{si}_{nm}",
                                        list(wshapes_by_stage[si][nm]), F32,
                                        kind="ExternalInput").ap()
                     for nm in wnames})
    with contextlib.ExitStack() as es:
        ctx = Ctx(nc, es)
        ctx.wsc = wsc
        emit_prelude(ctx)
        cur = x_in
        ping = [xa, xb]
        for si, (which, prefix, wnames) in enumerate(CHAIN):
            nxt = y_out if si == len(CHAIN) - 1 else ping[si % 2]
            for b in range(2):
                SUBLAYERS[which](ctx, cur, nxt, b, waps[si])
            cur = nxt
    return nc


_fused_cache = {}


def kernel(**inputs):
    x = np.ascontiguousarray(inputs["L0_attn_a_x"], dtype=np.float32)
    shards = [x[2 * c:2 * c + 2] for c in range(NCORES)]
    wstage = []
    flat = {}
    for si, (which, prefix, wnames) in enumerate(CHAIN):
        w = {nm: np.ascontiguousarray(inputs[prefix + "_" + nm],
                                      dtype=np.float32) for nm in wnames}
        wstage.append({nm: tuple(w[nm].shape) for nm in wnames})
        for nm in wnames:
            flat[f"s{si}_{nm}"] = w[nm]
    key = "fused"
    if key not in _fused_cache:
        _fused_cache[key] = build_fused(wstage)
    nc = _fused_cache[key]
    in_maps = [{"x": shards[c], **flat} for c in range(NCORES)]
    res = run_bass_kernel_spmd(nc, in_maps, core_ids=list(range(NCORES)))
    return np.concatenate([np.asarray(res.results[c]["y"])
                           for c in range(NCORES)], axis=0)


# version 17
# speedup vs baseline: 1.5010x; 1.0466x over previous
import contextlib
import numpy as np

import concourse.bass as bass
import concourse.mybir as mybir
from concourse.bass_utils import run_bass_kernel_spmd

F32 = mybir.dt.float32
BF = mybir.dt.bfloat16
AF = mybir.ActivationFunctionType
ALU = mybir.AluOpType
AX = mybir.AxisListType

NCORES = 8
T = 2048
D = 1024
KC = D // 128
NT = T // 128


def ap_off(t_ap, extra_off, ap):
    return bass.AP(t_ap.tensor, t_ap.offset + extra_off, ap)


class Sem:
    def __init__(self, h):
        self.h = h
        self.n = 0

    def inc(self, inst, k=1):
        inst.then_inc(self.h, k)
        self.n += k
        return self.n


class Ctx:
    def __init__(self, nc, es):
        self.nc = nc
        self.es = es
        E = es.enter_context
        self.A = E(nc.sbuf_tensor("arena_a", [128, KC * T], BF))
        self.B = E(nc.sbuf_tensor("arena_b", [128, KC * T], F32))
        self.C = E(nc.sbuf_tensor("arena_c", [128, 8192], F32))
        self.W = E(nc.sbuf_tensor("arena_w", [128, 8192], F32))
        self.Wb = E(nc.sbuf_tensor("arena_wb", [128, 7968], BF))
        self.XS = E(nc.sbuf_tensor("arena_xs", [128, 4096], F32))
        self.SW = E(nc.sbuf_tensor("arena_sw", [128, 2048], BF))
        self.S = E(nc.sbuf_tensor("arena_s", [128, 1792], F32))
        self.ident = E(nc.sbuf_tensor("ident", [128, 128], F32))
        self.ones_row = E(nc.sbuf_tensor("ones_row", [1, 128], F32))
        self.R = E(nc.sbuf_tensor("row_scratch", [1, 1032], F32))
        self.ceps = E(nc.sbuf_tensor("c_eps6", [128, 1], F32))
        self.pb = [E(nc.psum_tensor(f"pb{i}", [128, 512], F32)) for i in range(8)]
        self._sems = {}

    def sem(self, role):
        DMA_ROLES = {"in_ld", "out_ldx", "out_ldw", "out_ldw0", "out_ldw1",
                     "ffn1_ld", "fp_ld", "fx_ldw", "fx_ldw0", "fx_ldw1",
                     "fx_ldw2", "fx_ldw3", "wc_l2",
                     "ab_ld", "ab_ld0", "ab_ld1", "aa_pld", "aa_ldw",
                     "aa_vld", "da_pld", "da_nwl", "da_ldw", "rk_ld"}
        if role in ("f_dve", "f_act", "out_st", "wc_ld", "wc_ct", "wc_st"):
            key = role
        else:
            fam = role.split("_")[0]
            if role in DMA_ROLES:
                fam += "D"
            if not hasattr(self, "_fam_slots"):
                self._fam_slots = {}
                self._fam_counts = {}
            if role not in self._fam_slots:
                idx = self._fam_counts.get(fam, 0)
                self._fam_counts[fam] = idx + 1
                pfx = "dpool" if fam.endswith("D") else "pool"
                self._fam_slots[role] = f"{pfx}{idx}"
            key = self._fam_slots[role]
        if key not in self._sems:
            self._sems[key] = Sem(self.es.enter_context(self.nc.semaphore(key)))
        s = self._sems[key]
        return s

    def fence(self, eng, role, inst=None):
        s = self.sem(role)
        if inst is None:
            inst = eng.engine_nop() if hasattr(eng, "engine_nop") else eng.nop()
        s.inc(inst)
        eng.wait_ge(s.h, s.n)

    def a_fm(self, k, t0=0, n=T):
        return self.A[:, k * T + t0: k * T + t0 + n]

    def b_fm(self, k, t0=0, n=T):
        return self.B[:, k * T + t0: k * T + t0 + n]

    def a_tok(self, i):
        return self.A[:, i * 1024:(i + 1) * 1024]

    def b_tok(self, i):
        return self.B[:, i * 1024:(i + 1) * 1024]


def emit_prelude(ctx):
    nc = ctx.nc
    with nc.Block() as blk:
        @blk.gpsimd
        def _(g):
            g.iota(ctx.C[:, 0:128], [[-1, 128]], channel_multiplier=1,
                   allow_small_or_imprecise_dtypes=True)
            g.memset(ctx.S[:, 0:1], 1.0)
            g.memset(ctx.S[:, 1:2], 0.0)
            g.memset(ctx.ones_row[:, :], 1.0)
            g.memset(ctx.ceps[:, :], 1e-6)
    with nc.Block() as blk:
        @blk.vector
        def _(v):
            v.tensor_scalar(ctx.ident[:, :], ctx.C[:, 0:128], 0.0, None,
                            ALU.is_equal)



def emit_in_ln_tp(ctx, x_in, b, ln=True):
    nc = ctx.nc
    ld = ctx.sem("in_ld")
    with nc.Block() as blk:
        @blk.sync
        def _(sync):
            dst = bass.AP(ctx.B, 0, [[KC * T, 128], [1024, NT], [1, 1024]])
            src = ap_off(x_in, b * T * D,
                         [[1024, 128], [128 * 1024, NT], [1, 1024]])
            ld.inc(sync.dma_start(out=dst, in_=src), 16)
            sync.wait_ge(ld.h, ld.n)

    if ln:
        sq = ctx.sem("ln_sq")
        var = ctx.sem("ln_var")
        rst = ctx.sem("ln_rst")
        with nc.Block() as blk:
            @blk.scalar
            def _(s):
                for i in range(NT):
                    sq.inc(s.activation(ctx.XS[:, 0:1024], ctx.b_tok(i), AF.Square,
                                        accum_out=ctx.S[:, 32 + i:33 + i]))
                s.wait_ge(var.h, var.n + 1)
                inst = s.activation(ctx.S[:, 80:80 + NT], ctx.S[:, 64:64 + NT],
                                    AF.Ln)
                ctx.fence(s, "f_act", inst)
                rst.inc(s.activation(ctx.S[:, 96:96 + NT], ctx.S[:, 80:80 + NT],
                                     AF.Exp, scale=-0.5))

            @blk.vector
            def _(v):
                sq0 = sq.n - NT
                for i in range(NT):
                    inst = v.tensor_reduce(ctx.S[:, 48 + i:49 + i], ctx.b_tok(i),
                                           axis=AX.X, op=ALU.add)
                v.wait_ge(sq.h, sq0 + NT)
                ctx.fence(v, "f_dve", inst)
                inst = v.tensor_scalar(ctx.S[:, 112:112 + NT], ctx.S[:, 48:48 + NT],
                                       1.0 / D, None, ALU.mult)
                ctx.fence(v, "f_dve", inst)
                v.tensor_scalar(ctx.S[:, 192:192 + NT], ctx.S[:, 32:32 + NT],
                                1.0 / D, 1e-5, ALU.mult, ALU.add)
                inst = v.tensor_tensor(ctx.S[:, 208:208 + NT], ctx.S[:, 112:112 + NT],
                                       ctx.S[:, 112:112 + NT], ALU.mult)
                ctx.fence(v, "f_dve", inst)
                inst = v.tensor_tensor(ctx.S[:, 64:64 + NT], ctx.S[:, 192:192 + NT],
                                       ctx.S[:, 208:208 + NT], ALU.subtract)
                var.inc(inst)
                v.wait_ge(rst.h, rst.n)
                for i in range(NT):
                    v.tensor_scalar(ctx.b_tok(i), ctx.b_tok(i),
                                    ctx.S[:, 112 + i:113 + i],
                                    ctx.S[:, 96 + i:97 + i],
                                    ALU.subtract, ALU.mult)

    tp = ctx.sem("tp")
    ev = ctx.sem("tp_ev")
    with nc.Block() as blk:
        @blk.tensor
        def _(t):
            ev0, tp0 = ev.n, tp.n
            g = 0
            for i in range(NT):
                for half in range(2):
                    if g >= 4:
                        t.wait_ge(ev.h, ev0 + g - 3)
                    bank = ctx.pb[g % 4]
                    for q in range(4):
                        k = half * 4 + q
                        inst = t.transpose(bank[:, q * 128:(q + 1) * 128],
                                           ctx.b_tok(i)[:, k * 128:(k + 1) * 128],
                                           ctx.ident[:, :])
                        if q == 3:
                            tp.inc(inst)
                    g += 1

        @blk.vector
        def _(v):
            ev0, tp0 = ev.n, tp.n - 2 * NT
            g = 0
            for i in range(NT):
                for half in range(2):
                    v.wait_ge(tp.h, tp0 + g + 1)
                    bank = ctx.pb[g % 4]
                    for q in range(4):
                        k = half * 4 + q
                        inst = v.tensor_copy(ctx.a_fm(k, i * 128, 128),
                                             bank[:, q * 128:(q + 1) * 128])
                        if q == 3:
                            ev.inc(inst)
                    g += 1



def emit_out_phase(ctx, x_in, x_out, b, proj=None):
    nc = ctx.nc
    ldx = ctx.sem("out_ldx")
    ldw = ctx.sem("out_ldw")
    mm = ctx.sem("out_mm")
    evs = ctx.sem("out_ev")
    tps = ctx.sem("out_tp")
    add = ctx.sem("out_add")
    stw = ctx.sem("out_st")
    wtp = ctx.sem("out_wtp")
    wte = ctx.sem("out_wte")

    has_bias = proj is not None and proj.get("bias") is not None
    kchunks = proj["kchunks"] if proj else 0
    kpart = proj.get("kpart", 128) if proj else 128
    wtrans = proj.get("transpose", False) if proj else False

    with nc.Block() as blk:
        @blk.sync
        def _(sync):
            if has_bias:
                bsrc = proj["bias"].rearrange("(a c) -> a c", c=128)
                ldx.inc(sync.dma_start(out=ctx.S[0:8, 128:256], in_=bsrc), 16)
            if proj and not wtrans:
                for of in range(8):
                    wdst = ctx.W[0:kpart,
                                 of * kchunks * 128:(of + 1) * kchunks * 128]
                    ldw.inc(sync.dma_start(out=wdst, in_=proj["w_src"](of)), 16)
            if proj and wtrans:
                wtp0 = wtp.n
                ngrp = (kchunks + 3) // 4
                for of in range(8):
                    if of >= 2:
                        sync.wait_ge(wtp.h, wtp0 + (of - 1) * ngrp)
                    sl = ctx.sem(f"out_ldw{of % 2}")
                    sl.inc(sync.dma_start(
                        out=ctx.C[:, (of % 2) * 1024:(of % 2) * 1024 + 1024],
                        in_=proj["w_rows"](of)), 16)
            sync.wait_ge(ldx.h, ldx.n)

        if has_bias:
            @blk.tensor
            def _(t):
                t.wait_ge(ldx.h, ldx.n)
                tps.inc(t.transpose(ctx.pb[6][0:128, 0:8], ctx.S[0:8, 128:256],
                                    ctx.ident[0:8, 0:8]))

            @blk.vector
            def _(v):
                v.wait_ge(tps.h, tps.n)
                v.tensor_copy(ctx.S[:, 256:264], ctx.pb[6][:, 0:8])

        if proj and wtrans:
            ngrp = (kchunks + 3) // 4

            @blk.tensor
            def _(t):
                slb = [ctx.sem("out_ldw0").n - 4 * 16, ctx.sem("out_ldw1").n - 4 * 16]
                wte0 = wte.n
                gcnt = 0
                for of in range(8):
                    sl = ctx.sem(f"out_ldw{of % 2}")
                    t.wait_ge(sl.h, slb[of % 2] + 16 * (of // 2 + 1))
                    for g in range(ngrp):
                        if gcnt >= 2:
                            t.wait_ge(wte.h, wte0 + gcnt - 1)
                        bank = ctx.pb[4 + gcnt % 2]
                        for k in range(g * 4, min(g * 4 + 4, kchunks)):
                            inst = t.transpose(
                                bank[:, (k % 4) * 128:(k % 4) * 128 + 128],
                                ctx.C[:, (of % 2) * 1024 + k * 128:
                                      (of % 2) * 1024 + (k + 1) * 128],
                                ctx.ident[:, :])
                        wtp.inc(inst)
                        gcnt += 1

            @blk.vector
            def _(v):
                wtp0 = wtp.n - 8 * ngrp
                gcnt = 0
                for of in range(8):
                    for g in range(ngrp):
                        v.wait_ge(wtp.h, wtp0 + gcnt + 1)
                        bank = ctx.pb[4 + gcnt % 2]
                        for k in range(g * 4, min(g * 4 + 4, kchunks)):
                            inst = v.tensor_copy(
                                ctx.W[:, of * kchunks * 128 + k * 128:
                                      of * kchunks * 128 + (k + 1) * 128],
                                bank[:, (k % 4) * 128:(k % 4) * 128 + 128])
                        wte.inc(inst)
                        gcnt += 1

    ldw_needed = (ldw.n if proj and not wtrans else 0)
    ldx_base = ldx.n
    add_base = add.n
    with nc.Block() as blk:
        @blk.sync
        def _(sync):
            for n in range(4):
                if n >= 1:
                    sync.wait_ge(add.h, add_base + n * 8)
                dst = bass.AP(ctx.XS, 0, [[4096, 128], [1024, 4], [1, 1024]])
                src = ap_off(x_in, (b * T + n * 512) * D,
                             [[1024, 128], [128 * 1024, 4], [1, 1024]])
                ldx.inc(sync.dma_start(out=dst, in_=src), 16)

        @blk.tensor
        def _(t):
            mm0, ev0, add0 = mm.n, evs.n, add.n
            if proj and not wtrans:
                t.wait_ge(ldw.h, ldw_needed)
            for i in range(32):
                n, of = divmod(i, 8)
                if proj:
                    if i >= 2:
                        t.wait_ge(evs.h, ev0 + i - 1)
                    wbase = of * kchunks * 128
                    for k in range(kchunks):
                        inst = t.matmul(ctx.pb[i % 2][:, :],
                                        ctx.W[0:kpart,
                                              wbase + k * 128:wbase + (k + 1) * 128],
                                        proj["src"](k, n * 512, 512),
                                        start=(k == 0), stop=(k == kchunks - 1))
                    mm.inc(inst)
                if proj:
                    t.wait_ge(evs.h, ev0 + i + 1)
                    srct = ctx.C[:, (i % 4) * 512:(i % 4) * 512 + 512]
                else:
                    srct = ctx.b_fm(of, n * 512, 512)
                if i >= 2:
                    t.wait_ge(add.h, add0 + i - 1)
                bank = ctx.pb[2 + i % 2]
                for q in range(4):
                    inst = t.transpose(bank[:, q * 128:(q + 1) * 128],
                                       srct[:, q * 128:(q + 1) * 128],
                                       ctx.ident[:, :])
                tps.inc(inst)

        @blk.vector
        def _(v):
            mm0 = mm.n - (32 if proj else 0)
            tp0 = tps.n - 32
            st0 = stw.n
            for i in range(32):
                n, of = divmod(i, 8)
                if proj:
                    v.wait_ge(mm.h, mm0 + i + 1)
                    dstc = ctx.C[:, (i % 4) * 512:(i % 4) * 512 + 512]
                    if has_bias:
                        inst = v.tensor_scalar(dstc, ctx.pb[i % 2][:, :],
                                               ctx.S[:, 256 + of:257 + of],
                                               None, ALU.add)
                    else:
                        inst = v.tensor_copy(dstc, ctx.pb[i % 2][:, :])
                    evs.inc(inst)
                v.wait_ge(tps.h, tp0 + i + 1)
                if of == 0 and n >= 1:
                    v.wait_ge(stw.h, st0 + 16 * n)
                if of == 0:
                    v.wait_ge(ldx.h, ldx_base + 16 * (n + 1))
                bank = ctx.pb[2 + i % 2]
                for q in range(4):
                    inst = v.tensor_tensor(
                        ctx.C[:, 2048 + q * 1024 + of * 128:
                              2048 + q * 1024 + (of + 1) * 128],
                        bank[:, q * 128:(q + 1) * 128],
                        ctx.XS[:, q * 1024 + of * 128:q * 1024 + (of + 1) * 128],
                        ALU.add)
                add.inc(inst)

        @blk.gpsimd
        def _(g):
            add0 = add.n - 32
            for n in range(4):
                g.wait_ge(add.h, add0 + (n + 1) * 8)
                src = bass.AP(ctx.C, 2048, [[8192, 128], [1024, 4], [1, 1024]])
                dst = ap_off(x_out, (b * T + n * 512) * D,
                             [[1024, 128], [128 * 1024, 4], [1, 1024]])
                stw.inc(g.dma_start(out=dst, in_=src), 16)
            g.wait_ge(stw.h, stw.n)



def emit_l1_ffn_mix(ctx, b, w):
    nc = ctx.nc
    ld = ctx.sem("ffn1_ld")
    mm = ctx.sem("ffn1_mm")
    evs = ctx.sem("ffn1_ev")

    with nc.Block() as blk:
        @blk.sync
        def _(sync):
            wsrc = ap_off(w["Win"], 0, [[64, 128], [128 * 64, KC], [1, 64]])
            ld.inc(sync.dma_start(out=ctx.W[:, 4096:4096 + KC * 64], in_=wsrc), 16)
            ld.inc(sync.dma_start(out=ctx.W[0:64, 4608:4624], in_=w["W1"]), 16)
            ld.inc(sync.dma_start(out=ctx.W[0:64, 4624:4640], in_=w["b1"]), 16)
            ld.inc(sync.dma_start(out=ctx.W[0:64, 4640:4656], in_=w["W2"]), 16)
            ld.inc(sync.dma_start(out=ctx.W[0:64, 4656:4657],
                                  in_=w["bin_"].rearrange("(m o) -> m o", o=1)), 16)
            ld.inc(sync.dma_start(out=ctx.W[0:64, 4657:4658],
                                  in_=w["b2"].rearrange("(m o) -> m o", o=1)), 16)
            sync.wait_ge(ld.h, ld.n)

        @blk.gpsimd
        def _(g):
            g.wait_ge(ld.h, ld.n)
            g.tensor_copy(ctx.Wb[:, 0:KC * 64], ctx.W[:, 4096:4096 + KC * 64])

    with nc.Block() as blk:
        @blk.tensor
        def _(t):
            ev0 = evs.n
            for n in range(4):
                if n >= 2:
                    t.wait_ge(evs.h, ev0 + n - 1)
                for k in range(KC):
                    inst = t.matmul(ctx.pb[4 + n % 2][0:64, :],
                                    ctx.Wb[:, k * 64:(k + 1) * 64],
                                    ctx.a_fm(k, n * 512, 512),
                                    start=(k == 0), stop=(k == KC - 1))
                mm.inc(inst)

        @blk.scalar
        def _(s):
            mm0 = mm.n - 4
            for n in range(4):
                s.wait_ge(mm.h, mm0 + n + 1)
                inst = s.activation(ctx.C[0:64, 0 + n * 512: 512 + n * 512],
                                    ctx.pb[4 + n % 2][0:64, :], AF.Identity,
                                    bias=ctx.W[0:64, 4656:4657])
                evs.inc(inst)

    hsem = ctx.sem("ffn1_h")
    gsem = ctx.sem("ffn1_g")
    with nc.Block() as blk:
        @blk.scalar
        def _(s):
            g0 = gsem.n
            for i in range(16):
                if i >= 2:
                    s.wait_ge(gsem.h, g0 + i - 1)
                hb = ctx.C[0:64, 2048 + (i % 2) * T:2048 + (i % 2) * T + T]
                hsem.inc(s.activation(hb, ctx.C[0:64, 0:T], AF.Relu,
                                      bias=ctx.W[0:64, 4624 + i:4625 + i],
                                      scale=ctx.W[0:64, 4608 + i:4609 + i]))

        @blk.vector
        def _(v):
            h0 = hsem.n - 16
            g = ctx.B[0:64, 0:T]
            for i in range(16):
                v.wait_ge(hsem.h, h0 + i + 1)
                hb = ctx.C[0:64, 2048 + (i % 2) * T:2048 + (i % 2) * T + T]
                if i == 0:
                    inst = v.tensor_scalar(g, hb, ctx.W[0:64, 4640:4641],
                                           None, ALU.mult)
                else:
                    tg = ctx.C[0:64, 6144:6144 + T]
                    v.tensor_scalar(tg, hb, ctx.W[0:64, 4640 + i:4641 + i],
                                    None, ALU.mult)
                    inst = v.tensor_tensor(g, g, tg, ALU.add)
                gsem.inc(inst)
            v.tensor_scalar(g, g, ctx.W[0:64, 4657:4658], None, ALU.add)
    return


SUBLAYERS = {}


def build_test_program(which, weights_spec):
    nc = bass.Bass("TRN2", target_bir_lowering=False)
    x_in = nc.dram_tensor("x", [2, T, D], F32, kind="ExternalInput").ap()
    x_out = nc.dram_tensor("y", [2, T, D], F32, kind="ExternalOutput").ap()
    waps = {k: nc.dram_tensor(k, list(v), F32, kind="ExternalInput").ap()
            for k, v in weights_spec.items()}
    wsc = nc.dram_tensor("wsc", [192 * 128, 2048], BF, kind="Internal").ap()
    with contextlib.ExitStack() as es:
        ctx = Ctx(nc, es)
        ctx.wsc = wsc
        emit_prelude(ctx)
        for b in range(2):
            SUBLAYERS[which](ctx, x_in, x_out, b, waps)
    return nc


def sub_l1_ffn(ctx, x_in, x_out, b, w):
    emit_in_ln_tp(ctx, x_in, b)
    emit_l1_ffn_mix(ctx, b, w)
    proj = dict(
        kchunks=1, kpart=64,
        w_src=lambda of: ap_off(w["Wout"], of * 128, [[1024, 64], [1, 128]]),
        src=lambda k, t0, n: ctx.B[0:64, t0:t0 + n],
        bias=w["bout"],
    )
    emit_out_phase(ctx, x_in, x_out, b, proj=proj)


SUBLAYERS["l1_ffn"] = sub_l1_ffn





def emit_l0_ffn_wcast(ctx, w):
    nc = ctx.nc
    wcl = ctx.sem("wc_ld")
    wc2 = ctx.sem("wc_l2")
    wct = ctx.sem("wc_ct")
    wcs = ctx.sem("wc_st")
    wcs_base = wcs.n
    wcl_base, wc2_base = wcl.n, wc2.n
    with nc.Block() as blk:
        @blk.sync
        def _(sync):
            idx = 0
            for e in range(6):
                w1 = w["sw1"] if e < 2 else w["rw1"]
                ei = e if e < 2 else e - 2
                for hc in range(32):
                    slot = idx % 2
                    if idx >= 2:
                        sync.wait_ge(wcs.h, wcs_base + 16 * (idx - 1))
                    src1 = ap_off(w1, ei * D * 4096 + hc * 128,
                                  [[4096, 128], [128 * 4096, KC], [1, 128]])
                    wcl.inc(sync.dma_start(
                        out=ctx.C[:, slot * 2048:slot * 2048 + 1024],
                        in_=src1), 16)
                    idx += 1

        @blk.scalar
        def _(s):
            idx = 0
            for e in range(6):
                w2 = w["sw2"] if e < 2 else w["rw2"]
                ei = e if e < 2 else e - 2
                for hc in range(32):
                    slot = idx % 2
                    if idx >= 2:
                        s.wait_ge(wcs.h, wcs_base + 16 * (idx - 1))
                    wc2.inc(s.dma_start(
                        out=ctx.C[:, slot * 2048 + 1024:slot * 2048 + 2048],
                        in_=ap_off(w2, ei * 4096 * D + hc * 128 * D,
                                   [[D, 128], [1, D]])), 16)
                    idx += 1

        @blk.gpsimd
        def _(g):
            for idx in range(192):
                slot = idx % 2
                g.wait_ge(wcl.h, wcl_base + 16 * (idx + 1))
                g.wait_ge(wc2.h, wc2_base + 16 * (idx + 1))
                if idx >= 2:
                    g.wait_ge(wcs.h, wcs_base + 16 * (idx - 1))
                wct.inc(g.tensor_copy(
                    ctx.Wb[:, slot * 2048:(slot + 1) * 2048],
                    ctx.C[:, slot * 2048:(slot + 1) * 2048]))
                dstw = ap_off(ctx.wsc, idx * 128 * 2048,
                              [[2048, 128], [1, 2048]])
                wcs.inc(g.dma_start(
                    out=dstw, in_=ctx.Wb[:, slot * 2048:(slot + 1) * 2048]), 16)
            g.wait_ge(wcs.h, wcs.n)


def emit_l0_ffn_prep(ctx, w):
    nc = ctx.nc
    ld = ctx.sem("fp_ld")
    tp = ctx.sem("fp_tp")
    with nc.Block() as blk:
        @blk.sync
        def _(sync):
            for e in range(6):
                wsrc = w["sw1"] if e < 2 else w["rw1"]
                b1 = w["sb1"] if e < 2 else w["rb1"]
                b2 = w["sb2"] if e < 2 else w["rb2"]
                ei = e if e < 2 else e - 2
                ld.inc(sync.dma_start(
                    out=ctx.C[0:32, e * 128:(e + 1) * 128],
                    in_=ap_off(b1, ei * 4096, [[128, 32], [1, 128]])), 16)
                ld.inc(sync.dma_start(
                    out=ctx.C[0:8, 768 + e * 128:768 + e * 128 + 128],
                    in_=ap_off(b2, ei * 1024, [[128, 8], [1, 128]])), 16)
            for k in range(KC):
                ld.inc(sync.dma_start(
                    out=ctx.W[:, 6424 + 4 * k:6428 + 4 * k],
                    in_=ap_off(w["gate_w"], k * 128 * 4, [[4, 128], [1, 4]])), 16)
            ld.inc(sync.dma_start(out=ctx.S[0:1, 130:134],
                                  in_=w["gate_b"].rearrange("(o e) -> o e", o=1)), 16)
            ld.inc(sync.dma_start(out=ctx.W[0:4, 4096:5120],
                                  in_=w["rb2"]), 16)
            ld.inc(sync.dma_start(out=ctx.W[4:6, 4096:5120],
                                  in_=w["sb2"]), 16)
            sync.wait_ge(ld.h, ld.n)

        @blk.gpsimd
        def _(g):
            g.iota(ctx.W[0:4, 6460:6460 + 128], [[0, 128]], channel_multiplier=1,
                   allow_small_or_imprecise_dtypes=True)
            g.wait_ge(ld.h, ld.n)
            g.tensor_copy(ctx.Wb[:, 4864:4896], ctx.W[:, 6424:6456])
            g.tensor_copy(ctx.Wb[0:6, 4896:5920], ctx.W[0:6, 4096:5120])

    with nc.Block() as blk:
        @blk.tensor
        def _(t):
            feb = ctx.sem("fp_ev").n
            for e in range(6):
                if e >= 1:
                    t.wait_ge(ctx.sem("fp_ev").h, feb + e)
                t.transpose(ctx.pb[7][:, 0:32], ctx.C[0:32, e * 128:(e + 1) * 128],
                            ctx.ident[0:32, 0:32])
                tp.inc(t.transpose(ctx.pb[7][:, 32:40],
                                   ctx.C[0:8, 768 + e * 128:768 + e * 128 + 128],
                                   ctx.ident[0:8, 0:8]))
            tp.inc(t.matmul(ctx.pb[6][:, 0:4], ctx.ones_row[0:1, 0:128],
                            ctx.S[0:1, 130:134], start=True, stop=True))

        @blk.vector
        def _(v):
            fe = ctx.sem("fp_ev")
            tp0 = tp.n - 7
            for e in range(6):
                v.wait_ge(tp.h, tp0 + e + 1)
                v.tensor_copy(ctx.W[:, 6176 + 32 * e:6176 + 32 * e + 32],
                              ctx.pb[7][:, 0:32])
                fe.inc(v.tensor_copy(ctx.W[:, 6368 + 8 * e:6368 + 8 * e + 8],
                                     ctx.pb[7][:, 32:40]))
            v.wait_ge(tp.h, tp0 + 7)
            v.tensor_copy(ctx.W[:, 6456:6460], ctx.pb[6][:, 0:4])
            v.tensor_scalar(ctx.W[:, 6176:6240], ctx.W[:, 6176:6240],
                            0.5, None, ALU.mult)
            v.tensor_tensor(ctx.W[:, 6416:6424], ctx.W[:, 6368:6376],
                            ctx.W[:, 6376:6384], ALU.add)
            for e in range(4):
                v.tensor_scalar(ctx.W[0:4, 6460 + 128 * (e + 1):6460 + 128 * (e + 2)]
                                if False else
                                ctx.W[0:4, 6588 + 128 * e:6588 + 128 * e + 128],
                                ctx.W[0:4, 6460:6588][0:4, 0:128], float(e), None,
                                ALU.is_equal)


def emit_l0_ffn_gate(ctx, b):
    nc = ctx.nc
    gm = ctx.sem("fg_mm")
    ge = ctx.sem("fg_ev")
    with nc.Block() as blk:
        @blk.tensor
        def _(t):
            ge0 = ge.n
            for i in range(NT):
                if i >= 2:
                    t.wait_ge(ge.h, ge0 + i - 1)
                for k in range(KC):
                    inst = t.matmul(ctx.pb[6 + i % 2][0:128, 0:4],
                                    ctx.a_fm(k, i * 128, 128),
                                    ctx.Wb[:, 4864 + 4 * k:4868 + 4 * k],
                                    start=(k == 0), stop=(k == KC - 1))
                gm.inc(inst)

        @blk.vector
        def _(v):
            gm0 = gm.n - NT
            for i in range(NT):
                v.wait_ge(gm.h, gm0 + i + 1)
                ge.inc(v.tensor_tensor(ctx.C[:, 6144 + 4 * i:6148 + 4 * i],
                                       ctx.pb[6 + i % 2][:, 0:4],
                                       ctx.W[:, 6456:6460], ALU.add))

    sm = ctx.sem("fg_sm")
    with nc.Block() as blk:
        @blk.vector
        def _(v):
            L = ctx.C[:, 6144:6208]
            E3 = L.rearrange("p (g e) -> p g e", e=4)
            MX = ctx.C[:, 6208:6224]
            SM = ctx.C[:, 6224:6240]
            W4 = ctx.C[:, 6240:6304]
            M1 = ctx.C[:, 6304:6320]
            M2 = ctx.C[:, 6320:6336]
            TMP = ctx.C[:, 6336:6400]
            def bc16(x):
                return bass.AP(x.tensor, x.offset, [list(x.ap[0]), list(x.ap[1]), [0, 4]])
            smb = sm.n
            inst = v.tensor_reduce(MX, E3, axis=AX.X, op=ALU.max)
            ctx.fence(v, "f_dve", inst)
            inst = v.tensor_tensor(W4.rearrange("p (g e) -> p g e", e=4), E3,
                                   bc16(MX), ALU.subtract)
            sm.inc(inst)
            v.wait_ge(sm.h, smb + 2)
            inst = v.tensor_reduce(SM, W4.rearrange("p (g e) -> p g e", e=4),
                                   axis=AX.X, op=ALU.add)
            ctx.fence(v, "f_dve", inst)
            inst = v.reciprocal(SM, SM)
            ctx.fence(v, "f_dve", inst)
            inst = v.tensor_tensor(W4.rearrange("p (g e) -> p g e", e=4),
                                   W4.rearrange("p (g e) -> p g e", e=4),
                                   bc16(SM), ALU.mult)
            ctx.fence(v, "f_dve", inst)
            inst = v.tensor_reduce(M1, W4.rearrange("p (g e) -> p g e", e=4),
                                   axis=AX.X, op=ALU.max)
            ctx.fence(v, "f_dve", inst)
            inst = v.tensor_tensor(TMP.rearrange("p (g e) -> p g e", e=4),
                                   W4.rearrange("p (g e) -> p g e", e=4),
                                   bc16(M1), ALU.subtract)
            ctx.fence(v, "f_dve", inst)
            inst = v.tensor_scalar(TMP, TMP, 0.0, None, ALU.is_ge)
            ctx.fence(v, "f_dve", inst)
            inst = v.scalar_tensor_tensor(TMP, TMP, -1e30, W4, ALU.mult, ALU.add)
            ctx.fence(v, "f_dve", inst)
            inst = v.tensor_reduce(M2, TMP.rearrange("p (g e) -> p g e", e=4),
                                   axis=AX.X, op=ALU.max)
            ctx.fence(v, "f_dve", inst)
            inst = v.tensor_tensor(TMP.rearrange("p (g e) -> p g e", e=4),
                                   W4.rearrange("p (g e) -> p g e", e=4),
                                   bc16(M2), ALU.subtract)
            ctx.fence(v, "f_dve", inst)
            inst = v.tensor_scalar(TMP, TMP, 0.0, None, ALU.is_ge)
            ctx.fence(v, "f_dve", inst)
            v.tensor_tensor(W4, W4, TMP, ALU.mult)

        @blk.scalar
        def _(s):
            s.wait_ge(sm.h, sm.n - 1 + 1)
            sm.inc(s.activation(ctx.C[:, 6240:6304], ctx.C[:, 6240:6304], AF.Exp))

        @blk.gpsimd
        def _(g):
            g.memset(ctx.SW[0:6, 0:2048], 0.5)

    tp = ctx.sem("fg_tp")
    ev = ctx.sem("fg_tpe")
    with nc.Block() as blk:
        @blk.tensor
        def _(t):
            ev0 = ev.n
            for i in range(NT):
                if i >= 8:
                    t.wait_ge(ev.h, ev0 + (i - 8) // 4 + 1)
                tp.inc(t.transpose(ctx.pb[6 + (i // 4) % 2][0:4, (i % 4) * 128:(i % 4) * 128 + 128],
                                   ctx.C[:, 6240 + 4 * i:6244 + 4 * i],
                                   ctx.ident[:, :]))

        @blk.vector
        def _(v):
            tp0 = tp.n - NT
            for gsel in range(4):
                v.wait_ge(tp.h, tp0 + gsel * 4 + 4)
                v.tensor_copy(
                    ctx.C[0:4, 2048 + gsel * 512:2048 + gsel * 512 + 512],
                    ctx.pb[6 + gsel % 2][0:4, 0:512])
                ev.inc(v.tensor_copy(
                    ctx.SW[0:4, gsel * 512:gsel * 512 + 512],
                    ctx.pb[6 + gsel % 2][0:4, 0:512]))



def emit_l0_ffn_mix(ctx, b, w):
    nc = ctx.nc
    hmm = ctx.sem("fx_hmm")
    hrl = ctx.sem("fx_hrl")
    hsc = ctx.sem("fx_hsc")
    omm = ctx.sem("fx_omm")
    oev = ctx.sem("fx_oev")
    wbm = ctx.sem("fx_wbm")
    wbe = ctx.sem("fx_wbe")

    TILES = [(0, 384), (384, 384), (768, 384), (1152, 384), (1536, 384),
             (1920, 128)]
    NIDX = len(TILES) * 6 * 32
    omm_base = omm.n
    ldw_base = [ctx.sem(f"fx_ldw{q}").n for q in range(2)]
    SLOT = [0, 2048, 5920]

    def emit_load(eng, idx, e, hc):
        sb = SLOT[idx % 3]
        if idx >= 3:
            eng.wait_ge(omm.h, omm_base + idx - 2)
        sl = ctx.sem(f"fx_ldw{idx % 2}")
        ti = e * 32 + hc
        srcw = ap_off(ctx.wsc, ti * 128 * 2048, [[2048, 128], [1, 2048]])
        sl.inc(eng.dma_start(out=ctx.Wb[:, sb:sb + 2048], in_=srcw), 16)

    wcs_total = ctx.sem("wc_st").n
    with nc.Block() as blk:
        def emit_loads(eng, qid):
            eng.wait_ge(ctx.sem("wc_st").h, wcs_total)
            idx = 0
            for j in range(len(TILES)):
                for e in range(6):
                    for hc in range(32):
                        if idx % 2 == qid:
                            emit_load(eng, idx, e, hc)
                        idx += 1

        @blk.sync
        def _(sync):
            emit_loads(sync, 0)

        @blk.gpsimd
        def _(g):
            emit_loads(g, 1)

        @blk.tensor
        def _(t):
            hrl0, oev0, wbe0, hsc0 = hrl.n, oev.n, wbe.n, hsc.n
            idx = 0
            rcnt = 0
            ne_done = 0
            for j, (ts0, tlen) in enumerate(TILES):
                ntc = tlen // 128
                t.wait_ge(hrl.h, hrl0 + idx)
                for e4 in range(4):
                    if e4 >= 2:
                        t.wait_ge(wbe.h, wbe0 + j * 4 + e4 - 1)
                    wbm.inc(t.matmul(
                        ctx.pb[6 + e4 % 2][:, 0:tlen],
                        ctx.W[0:4, 6588 + 128 * e4:6588 + 128 * e4 + 128],
                        ctx.C[0:4, 2048 + ts0:2048 + ts0 + tlen],
                        start=True, stop=True))
                if j >= 1:
                    t.wait_ge(oev.h, oev0 + ne_done)
                for tc in range(ntc):
                    for oh in range(2):
                        t.matmul(ctx.pb[tc * 2 + oh][:, 0:512],
                                 ctx.SW[0:6, ts0 + tc * 128:
                                        ts0 + tc * 128 + 128],
                                 ctx.Wb[0:6, 4896 + oh * 512:5408 + oh * 512],
                                 start=True, stop=False, skip_group_check=True)
                t.wait_ge(wbe.h, wbe0 + (j + 1) * 4)
                for e in range(6):
                    for hc in range(32):
                        sb = SLOT[idx % 3]
                        t.wait_ge(ctx.sem(f"fx_ldw{idx % 2}").h,
                                  ldw_base[idx % 2] + 16 * (idx // 2 + 1))
                        if idx >= 2:
                            t.wait_ge(hrl.h, hrl0 + idx - 1)
                        for k in range(KC):
                            inst = t.matmul(
                                ctx.pb[6 + idx % 2][:, 0:tlen],
                                ctx.Wb[:, sb + k * 128:sb + (k + 1) * 128],
                                ctx.a_fm(k, ts0, tlen),
                                start=(k == 0), stop=(k == KC - 1))
                        hmm.inc(inst)
                        if e < 2:
                            t.wait_ge(hrl.h, hrl0 + idx + 1)
                        else:
                            rcnt += 1
                            t.wait_ge(hsc.h, hsc0 + rcnt)
                        hs0 = 4096 + (idx % 2) * 384
                        last = (e == 5 and hc == 31)
                        for tc in range(ntc):
                            for oh in range(2):
                                inst = t.matmul(
                                    ctx.pb[tc * 2 + oh][:, 0:512],
                                    ctx.Wb[:, hs0 + tc * 128:hs0 + tc * 128 + 128],
                                    ctx.Wb[:, sb + 1024 + oh * 512:
                                           sb + 1536 + oh * 512],
                                    start=False, stop=(last and tc == ntc - 1),
                                    skip_group_check=True)
                        omm.inc(inst)
                        idx += 1
                ne_done += ntc * 2

        @blk.scalar
        def _(s):
            hmm0 = hmm.n - NIDX
            idx = 0
            for j, (ts0, tlen) in enumerate(TILES):
                for e in range(6):
                    for hc in range(32):
                        s.wait_ge(hmm.h, hmm0 + idx + 1)
                        dsth = (ctx.Wb[:, 4096 + (idx % 2) * 384:
                                       4096 + (idx % 2) * 384 + tlen]
                                if e < 2 else
                                ctx.C[:, (idx % 2) * 384:(idx % 2) * 384 + tlen])
                        hrl.inc(s.activation(
                            dsth,
                            ctx.pb[6 + idx % 2][:, 0:tlen], AF.Relu,
                            scale=(0.5 if e < 2 else 1.0),
                            bias=ctx.W[:, 6176 + 32 * e + hc:
                                       6176 + 32 * e + hc + 1]))
                        idx += 1

        @blk.vector
        def _(v):
            omm0 = omm.n - NIDX
            hrl0 = hrl.n - NIDX
            wbm0 = wbm.n - len(TILES) * 4
            idx = 0
            rcnt = 0
            for j, (ts0, tlen) in enumerate(TILES):
                ntc = tlen // 128
                for e4 in range(4):
                    v.wait_ge(wbm.h, wbm0 + j * 4 + e4 + 1)
                    inst = v.tensor_copy(
                        ctx.C[:, 5120 + e4 * 384:5120 + e4 * 384 + tlen],
                        ctx.pb[6 + e4 % 2][:, 0:tlen])
                    wbe.inc(inst)
                    v.wait_ge(wbe.h, wbe.n)
                for e in range(6):
                    for hc in range(32):
                        if e >= 2:
                            v.wait_ge(hrl.h, hrl0 + idx + 1)
                            hs = ctx.C[:, (idx % 2) * 384:
                                       (idx % 2) * 384 + tlen]
                            hsc.inc(v.tensor_tensor(
                                ctx.Wb[:, 4096 + (idx % 2) * 384:
                                       4096 + (idx % 2) * 384 + tlen], hs,
                                ctx.C[:, 5120 + (e - 2) * 384:
                                      5120 + (e - 2) * 384 + tlen], ALU.mult))
                            rcnt += 1
                        idx += 1
                v.wait_ge(omm.h, omm0 + idx)
                for tc in range(ntc):
                    for oh in range(2):
                        nchunk = j * 3 + tc
                        oev.inc(v.tensor_copy(
                            ctx.B[:, nchunk * 1024 + oh * 512:
                                  nchunk * 1024 + oh * 512 + 512],
                            ctx.pb[tc * 2 + oh][:, 0:512]))


def emit_out_resid_tm(ctx, x_in, x_out, b):
    nc = ctx.nc
    ldx = ctx.sem("out_ldx")
    add = ctx.sem("out_add")
    stw = ctx.sem("out_st")
    ldx_base = ldx.n
    add_base = add.n
    with nc.Block() as blk:
        @blk.sync
        def _(sync):
            for g in range(4):
                if g >= 1:
                    sync.wait_ge(add.h, add_base + 4 * g)
                dst = bass.AP(ctx.XS, 0, [[4096, 128], [1024, 4], [1, 1024]])
                srcx = ap_off(x_in, (b * T + g * 512) * D,
                              [[1024, 128], [128 * 1024, 4], [1, 1024]])
                ldx.inc(sync.dma_start(out=dst, in_=srcx), 16)

        @blk.vector
        def _(v):
            for n in range(NT):
                if n % 4 == 0:
                    v.wait_ge(ldx.h, ldx_base + 16 * (n // 4 + 1))
                add.inc(v.tensor_tensor(
                    ctx.B[:, n * 1024:(n + 1) * 1024],
                    ctx.B[:, n * 1024:(n + 1) * 1024],
                    ctx.XS[:, (n % 4) * 1024:(n % 4 + 1) * 1024], ALU.add))

        @blk.gpsimd
        def _(g):
            add0 = add.n - NT
            st0 = stw.n
            for n in range(4):
                g.wait_ge(add.h, add0 + (n + 1) * 4)
                srcb = bass.AP(ctx.B, n * 4096,
                               [[KC * T, 128], [1024, 4], [1, 1024]])
                dst = ap_off(x_out, (b * T + n * 512) * D,
                             [[1024, 128], [128 * 1024, 4], [1, 1024]])
                stw.inc(g.dma_start(out=dst, in_=srcb), 16)
            g.wait_ge(stw.h, stw.n)


def sub_l0_ffn(ctx, x_in, x_out, b, w):
    if (b == 0 and not getattr(ctx, "wsc_cast_done", False)
            and getattr(ctx, "wcast_cursor", 0) < 192):
        emit_l0_ffn_wcast(ctx, w)
        ctx.wsc_cast_done = True
    if b == 0:
        emit_l0_ffn_prep(ctx, w)
    emit_in_ln_tp(ctx, x_in, b)
    emit_l0_ffn_gate(ctx, b)
    emit_l0_ffn_mix(ctx, b, w)
    emit_out_resid_tm(ctx, x_in, x_out, b)


SUBLAYERS["l0_ffn"] = sub_l0_ffn





def emit_l0_attn_b_mix(ctx, b, w):
    nc = ctx.nc
    EPS = float(np.finfo(np.float32).eps)
    ld = ctx.sem("ab_ld")
    wtp = ctx.sem("ab_wtp")
    wte = ctx.sem("ab_wte")
    mm = ctx.sem("ab_mm")
    aev = ctx.sem("ab_ev")
    tp = ctx.sem("ab_tp")
    te = ctx.sem("ab_te")
    ks = ctx.sem("ab_ks")
    cx = ctx.sem("ab_cx")
    cwe = ctx.sem("ab_cwe")
    om = ctx.sem("ab_om")
    zr = ctx.sem("ab_zr")
    zb = ctx.sem("ab_zb")
    ze = ctx.sem("ab_ze")
    fin = ctx.sem("ab_fin")

    for hp in range(8):
        with nc.Block() as blk:
            @blk.sync
            def _(sync):
                wtp0 = wtp.n
                for wi in range(3):
                    if wi >= 2:
                        sync.wait_ge(wtp.h, wtp0 + (wi - 1) * 2)
                    off = (wi * 1024 + hp * 128) * D
                    ctx.sem(f"ab_ld{wi % 2}").inc(sync.dma_start(
                        out=ctx.C[:, (wi % 2) * 1024:(wi % 2) * 1024 + 1024],
                        in_=ap_off(w["w_qkv"], off, [[D, 128], [1, D]])), 16)

            @blk.tensor
            def _(t):
                slb = [ctx.sem("ab_ld0").n - 2 * 16, ctx.sem("ab_ld1").n - 16]
                wte0 = wte.n
                for wi in range(3):
                    t.wait_ge(ctx.sem(f"ab_ld{wi % 2}").h,
                              slb[wi % 2] + 16 * (wi // 2 + 1))
                    if wi >= 1:
                        t.wait_ge(wte.h, wte0 + wi)
                    for k in range(KC):
                        inst = t.transpose(
                            ctx.pb[2 + (k // 4) % 2][:, (k % 4) * 128:(k % 4) * 128 + 128],
                            ctx.C[:, (wi % 2) * 1024 + k * 128:
                                  (wi % 2) * 1024 + (k + 1) * 128],
                            ctx.ident[:, :])
                        if k == 3 or k == 7:
                            wtp.inc(inst)

            @blk.vector
            def _(v):
                wtp0 = wtp.n - 6
                wte0 = wte.n
                for wi in range(3):
                    for g in range(2):
                        v.wait_ge(wtp.h, wtp0 + wi * 2 + g + 1)
                        for k in range(g * 4, g * 4 + 4):
                            inst = v.tensor_copy(
                                ctx.Wb[:, wi * 1024 + k * 128:wi * 1024 + (k + 1) * 128],
                                ctx.pb[2 + g][:, (k % 4) * 128:(k % 4) * 128 + 128])
                    wte.inc(inst)

        with nc.Block() as blk:
            @blk.tensor
            def _(t):
                aev0, te0, cwe0, zr0, ze0, fin0 = aev.n, te.n, cwe.n, zr.n, ze.n, fin.n
                for gi in range(12):
                    wi, ncn = divmod(gi, 4)
                    if gi >= 2:
                        t.wait_ge(aev.h, aev0 + gi - 1)
                    for k in range(KC):
                        inst = t.matmul(ctx.pb[gi % 2][:, :],
                                        ctx.Wb[:, wi * 1024 + k * 128:wi * 1024 + (k + 1) * 128],
                                        ctx.a_fm(k, ncn * 512, 512),
                                        start=(k == 0), stop=(k == KC - 1))
                    mm.inc(inst)
                    if gi == 7:
                        t.wait_ge(aev.h, aev0 + 8)
                        for tc in range(16):
                            g = tc // 4
                            if g >= 2:
                                t.wait_ge(te.h, te0 + g - 1)
                            inst = t.transpose(
                                ctx.pb[2 + g % 2][:, (tc % 4) * 128:(tc % 4) * 128 + 128],
                                ctx.C[:, 2048 + tc * 128:2048 + (tc + 1) * 128],
                                ctx.ident[:, :])
                            if tc % 4 == 3:
                                tp.inc(inst)
                t.wait_ge(aev.h, aev0 + 12)
                for tc in range(16):
                    g = tc // 4
                    t.wait_ge(te.h, te0 + 3 + g)
                    inst = t.transpose(
                        ctx.pb[2 + g % 2][:, (tc % 4) * 128:(tc % 4) * 128 + 128],
                        ctx.C[:, 2048 + tc * 128:2048 + (tc + 1) * 128],
                        ctx.ident[:, :])
                    if tc % 4 == 3:
                        tp.inc(inst)
                if hp > 0:
                    t.wait_ge(cwe.h, cwe0)
                for tc in range(16):
                    t.wait_ge(te.h, te0 + 4 + tc // 4 + 1)
                    for h in range(2):
                        inst = t.matmul(
                            ctx.pb[4][0:128, h * 64:h * 64 + 64],
                            ctx.C[:, 4096 + tc * 128:4096 + (tc + 1) * 128],
                            ctx.C[:, 6144 + tc * 128 + h * 64:6144 + tc * 128 + h * 64 + 64],
                            start=(tc == 0 and h == 0), stop=(tc == 15),
                            skip_group_check=True)
                cx.inc(inst)
                t.wait_ge(cwe.h, cwe0 + 1)
                for g in range(8):
                    h, ncn = divmod(g, 4)
                    if g >= 2:
                        t.wait_ge(fin.h, fin0 + g - 1)
                    om.inc(t.matmul(ctx.pb[5 + g % 2][0:65, :],
                                    ctx.S[h * 64:h * 64 + 64, 300:365],
                                    ctx.C[h * 64:h * 64 + 64,
                                          ncn * 512:ncn * 512 + 512],
                                    start=True, stop=True))
                    t.wait_ge(zr.h, zr0 + g + 1)
                    if g >= 1:
                        t.wait_ge(ze.h, ze0 + g)
                    zb.inc(t.matmul(ctx.pb[7][:, :], ctx.ones_row[0:1, 0:128],
                                    ctx.S[0:1, 365:877], start=True, stop=True))

            @blk.scalar
            def _(s):
                mm0 = mm.n - 12
                tp0 = tp.n - 8
                ks0 = ks.n
                for gi in range(12):
                    wi, ncn = divmod(gi, 4)
                    s.wait_ge(mm.h, mm0 + gi + 1)
                    if wi == 2 and ncn == 0:
                        s.wait_ge(tp.h, tp0 + 4)
                        s.wait_ge(ks.h, ks0 + 1)
                    dst = ctx.C[:, wi * 2048 + ncn * 512:wi * 2048 + ncn * 512 + 512] \
                        if wi < 2 else \
                        ctx.C[:, 2048 + ncn * 512:2048 + ncn * 512 + 512]
                    aev.inc(s.activation(dst, ctx.pb[gi % 2][:, :],
                                         AF.Relu if wi < 2 else AF.Identity))

            @blk.vector
            def _(v):
                aev0 = aev.n - 12
                tp0 = tp.n - 8
                cx0 = cx.n - 1
                om0 = om.n - 8
                zb0 = zb.n - 8
                v.wait_ge(aev.h, aev0 + 8)
                ks.inc(v.tensor_reduce(ctx.S[:, 364:365], ctx.C[:, 2048:4096],
                                       axis=AX.X, op=ALU.add))
                for g in range(8):
                    v.wait_ge(tp.h, tp0 + g + 1)
                    base = 4096 if g < 4 else 6144
                    gg = g % 4
                    for tc in range(gg * 4, gg * 4 + 4):
                        inst = v.tensor_copy(
                            ctx.C[:, base + tc * 128:base + (tc + 1) * 128],
                            ctx.pb[2 + g % 2][:, (tc % 4) * 128:(tc % 4) * 128 + 128])
                    te.inc(inst)
                v.wait_ge(cx.h, cx0 + 1)
                v.tensor_copy(ctx.S[0:64, 300:364], ctx.pb[4][0:64, 0:64])
                cwe.inc(v.tensor_copy(ctx.S[64:128, 300:364],
                                      ctx.pb[4][64:128, 64:128]))
                for g in range(8):
                    h, ncn = divmod(g, 4)
                    v.wait_ge(om.h, om0 + g + 1)
                    inst = v.tensor_scalar(ctx.S[0:1, 365:877],
                                           ctx.pb[5 + g % 2][64:65, :], EPS,
                                           None, ALU.add)
                    ctx.fence(v, "f_dve", inst)
                    zr.inc(v.reciprocal(ctx.S[0:1, 365:877],
                                        ctx.S[0:1, 365:877]))
                    v.wait_ge(zb.h, zb0 + g + 1)
                    inst = v.tensor_copy(ctx.S[:, 896:1408], ctx.pb[7][:, :])
                    ze.inc(inst)
                    v.wait_ge(ze.h, ze.n)
                    inst = v.tensor_tensor(
                        ctx.B[h * 64:h * 64 + 64,
                              hp * T + ncn * 512:hp * T + ncn * 512 + 512],
                        ctx.pb[5 + g % 2][0:64, :],
                        ctx.S[0:64, 896:1408], ALU.mult)
                    fin.inc(inst)


def sub_l0_attn_b(ctx, x_in, x_out, b, w):
    emit_in_ln_tp(ctx, x_in, b)
    emit_l0_attn_b_mix(ctx, b, w)
    proj = dict(
        kchunks=8, kpart=128, transpose=True,
        w_rows=lambda of: ap_off(w["w_proj"], of * 128 * D, [[D, 128], [1, D]]),
        src=lambda k, t0, n: ctx.b_fm(k, t0, n),
        bias=w["b_proj"],
    )
    emit_out_phase(ctx, x_in, x_out, b, proj=proj)


SUBLAYERS["l0_attn_b"] = sub_l0_attn_b





def emit_l1_attn_a_prep(ctx, w):
    nc = ctx.nc
    ld = ctx.sem("aa_pld")
    tp = ctx.sem("aa_ptp")
    rc = ctx.sem("aa_prc")
    with nc.Block() as blk:
        @blk.sync
        def _(sync):
            ld.inc(sync.dma_start(out=ctx.R[0:1, 0:256],
                                  in_=w["bq"].rearrange("(o c) -> o c", o=1)), 16)
            ld.inc(sync.dma_start(out=ctx.R[0:1, 256:512],
                                  in_=w["bk"].rearrange("(o c) -> o c", o=1)), 16)
            ld.inc(sync.dma_start(
                out=ctx.R[0:1, 512:768],
                in_=ap_off(w["q_phase"], 0, [[256, 1], [1, 256]])), 16)
            ld.inc(sync.dma_start(
                out=ctx.R[0:1, 768:1024],
                in_=ap_off(w["k_phase"], 0, [[256, 1], [1, 256]])), 16)
            ld.inc(sync.dma_start(out=ctx.R[0:1, 1024:1025],
                                  in_=w["temperature"].rearrange("(o c) -> o c", o=1)), 16)
            ld.inc(sync.dma_start(out=ctx.S[0:8, 0:128],
                                  in_=w["bv"].rearrange("(a c) -> a c", c=128)), 16)
            sync.wait_ge(ld.h, ld.n)
        @blk.gpsimd
        def _(g):
            g.iota(ctx.C[:, 0:896], [[1, 896]], channel_multiplier=-1,
                   allow_small_or_imprecise_dtypes=True)
    with nc.Block() as blk:
        @blk.vector
        def _(v):
            v.tensor_scalar(ctx.C[:, 0:896], ctx.C[:, 0:896], 383.5,
                            None, ALU.is_ge)
            v.tensor_tensor(ctx.R[0:1, 0:256], ctx.R[0:1, 0:256],
                            ctx.R[0:1, 512:768], ALU.add)
            v.tensor_tensor(ctx.R[0:1, 256:512], ctx.R[0:1, 256:512],
                            ctx.R[0:1, 768:1024], ALU.add)
            inst = v.reciprocal(ctx.R[0:1, 1025:1026], ctx.R[0:1, 1024:1025])
            rc.inc(inst)
    with nc.Block() as blk:
        @blk.gpsimd
        def _(g):
            g.tensor_copy(ctx.Wb[:, 5632:6528], ctx.C[:, 0:896])

        @blk.tensor
        def _(t):
            t.wait_ge(rc.h, rc.n)
            t.matmul(ctx.pb[6][:, 0:1], ctx.ones_row[0:1, 0:128],
                     ctx.R[0:1, 1025:1026], start=True, stop=True)
            tp.inc(t.transpose(ctx.pb[6][:, 8:16], ctx.S[0:8, 0:128],
                               ctx.ident[0:8, 0:8]))
        @blk.vector
        def _(v):
            v.wait_ge(tp.h, tp.n)
            inst = v.tensor_copy(ctx.W[:, 7294:7295], ctx.pb[6][:, 0:1])
            ctx.fence(v, "f_dve", inst)
            v.tensor_copy(ctx.W[:, 7286:7294], ctx.pb[6][:, 8:16])
            inst = v.tensor_scalar(ctx.W[:, 6803:6804], ctx.W[:, 7294:7295],
                                   -1.0, None, ALU.mult)
            ctx.fence(v, "f_dve", inst)
            v.tensor_tensor(ctx.W[:, 6802:6803], ctx.W[:, 7294:7295],
                            ctx.W[:, 6803:6804], ALU.min)


def emit_l1_attn_a_mix(ctx, b, w, first):
    nc = ctx.nc
    ldw = ctx.sem("aa_ldw")
    qk = ctx.sem("aa_qk")
    qe = ctx.sem("aa_qe")
    sqd = ctx.sem("aa_sq")
    red = ctx.sem("aa_rd")
    lne = ctx.sem("aa_le")
    nrm = ctx.sem("aa_nm")
    bcs = ctx.sem("aa_bc")

    with nc.Block() as blk:
        @blk.sync
        def _(sync):
            for wi, wname in ((0, "Wq"), (1, "Wk")):
                src = ap_off(w[wname], 0, [[256, 128], [128 * 256, KC], [1, 256]])
                ldw.inc(sync.dma_start(
                    out=ctx.W[:, wi * 2048:(wi + 1) * 2048], in_=src), 16)
            sync.wait_ge(ldw.h, ldw.n)

        @blk.gpsimd
        def _(g):
            g.wait_ge(ldw.h, ldw.n)
            g.tensor_copy(ctx.Wb[:, 0:4096], ctx.W[:, 0:4096])

    NG = 2 * NT
    with nc.Block() as blk:
        @blk.tensor
        def _(t):
            bcs.inc(t.matmul(ctx.pb[6][:, 0:256], ctx.ones_row[0:1, 0:128],
                             ctx.R[0:1, 0:256], start=True, stop=True))
            bcs.inc(t.matmul(ctx.pb[7][:, 0:256], ctx.ones_row[0:1, 0:128],
                             ctx.R[0:1, 256:512], start=True, stop=True))
            qe0 = qe.n
            for gi in range(NG):
                wi, i = divmod(gi, NT)
                if gi >= 2:
                    t.wait_ge(qe.h, qe0 + gi - 1)
                for k in range(KC):
                    inst = t.matmul(ctx.pb[gi % 2][:, 0:256],
                                    ctx.a_fm(k, i * 128, 128),
                                    ctx.Wb[:, wi * 2048 + k * 256:wi * 2048 + (k + 1) * 256],
                                    start=(k == 0), stop=(k == KC - 1))
                qk.inc(inst)

        @blk.scalar
        def _(s):
            qe0 = qe.n
            red0 = red.n
            nrm0 = nrm.n
            for gi in range(NG + 1):
                if gi < NG:
                    s.wait_ge(qe.h, qe0 + gi + 1)
                    if gi >= 2:
                        s.wait_ge(red.h, red0 + gi - 1)
                    wi, i = divmod(gi, NT)
                    dst = ctx.C[:, wi * 4096 + i * 256:wi * 4096 + (i + 1) * 256]
                    sqd.inc(s.activation(
                        ctx.W[:, 4624 + (gi % 2) * 256:4624 + (gi % 2) * 256 + 256],
                        dst, AF.Square))
                if gi >= 1:
                    gp = gi - 1
                    s.wait_ge(red.h, red0 + gp + 1)
                    if gp >= 2:
                        s.wait_ge(nrm.h, nrm0 + gp - 1)
                    nsq = ctx.S[:, 224 + (gp % 2) * 16:240 + (gp % 2) * 16]
                    lnt = ctx.S[:, 192 + (gp % 2) * 16:208 + (gp % 2) * 16]
                    nn = ctx.S[:, 160 + (gp % 2) * 16:176 + (gp % 2) * 16]
                    inst = s.activation(lnt, nsq, AF.Ln)
                    ctx.fence(s, "f_act", inst)
                    lne.inc(s.activation(nn, lnt, AF.Exp, scale=0.5))

        @blk.vector
        def _(v):
            qk0, bc0 = qk.n - NG, bcs.n - 2
            sq0 = sqd.n - NG
            lne0 = lne.n - NG
            for gi in range(NG + 2):
                if gi < NG:
                    wi, i = divmod(gi, NT)
                    v.wait_ge(qk.h, qk0 + gi + 1)
                    if gi == 0:
                        v.wait_ge(bcs.h, bc0 + 2)
                        inst = v.tensor_copy(ctx.W[:, 4096:4352],
                                             ctx.pb[6][:, 0:256])
                        inst = v.tensor_copy(ctx.W[:, 4352:4608],
                                             ctx.pb[7][:, 0:256])
                        ctx.fence(v, "f_dve", inst)
                    dst = ctx.C[:, wi * 4096 + i * 256:wi * 4096 + (i + 1) * 256]
                    qe.inc(v.tensor_tensor(dst, ctx.pb[gi % 2][:, 0:256],
                                           ctx.W[:, 4096 + wi * 256:4352 + wi * 256],
                                           ALU.add))
                if gi >= 1 and gi - 1 < NG:
                    gp = gi - 1
                    v.wait_ge(sqd.h, sq0 + gp + 1)
                    sq3 = ctx.W[:, 4624 + (gp % 2) * 256:4624 + (gp % 2) * 256 + 256]
                    red.inc(v.tensor_reduce(
                        ctx.S[:, 224 + (gp % 2) * 16:240 + (gp % 2) * 16],
                        sq3.rearrange("p (h w) -> p h w", w=16),
                        axis=AX.X, op=ALU.add))
                if gi >= 2:
                    gp = gi - 2
                    wi, i = divmod(gp, NT)
                    v.wait_ge(lne.h, lne0 + gp + 1)
                    nn = ctx.S[:, 160 + (gp % 2) * 16:176 + (gp % 2) * 16]
                    rn = ctx.S[:, 128 + (gp % 2) * 16:144 + (gp % 2) * 16]
                    inst = v.tensor_scalar(rn, nn, 1e-12, None, ALU.max)
                    ctx.fence(v, "f_dve", inst)
                    inst = v.reciprocal(rn, rn)
                    ctx.fence(v, "f_dve", inst)
                    if wi == 0:
                        inst = v.tensor_scalar(rn, rn, ctx.W[:, 7294:7295],
                                               None, ALU.mult)
                        ctx.fence(v, "f_dve", inst)
                    dst = ctx.C[:, wi * 4096 + i * 256:wi * 4096 + (i + 1) * 256]
                    rnb = bass.AP(rn.tensor, rn.offset,
                                  [[1792, 128], [1, 16], [0, 16]])
                    d3 = dst.rearrange("p (h w) -> p h w", w=16)
                    nrm.inc(v.tensor_tensor(d3, d3, rnb, ALU.mult))

    stg = ctx.sem("aa_st")
    ste = ctx.sem("aa_se")
    vld = ctx.sem("aa_vld")
    vm = ctx.sem("aa_vm")
    ve = ctx.sem("aa_ve")
    sc = ctx.sem("aa_sc")
    ex = ctx.sem("aa_ex")
    mk = ctx.sem("aa_mk")
    ym = ctx.sem("aa_ym")
    zr = ctx.sem("aa_zr")
    zb = ctx.sem("aa_zb")
    zbe = ctx.sem("aa_zbe")
    fin = ctx.sem("aa_fin")

    with nc.Block() as blk:
        @blk.gpsimd
        def _(g):
            g.memset(bass.AP(ctx.Wb, 6528 + 64, [[7968, 128], [65, 16]]), 1.0)

    CHUNKS = [(scn, tc) for scn in range(4) for tc in range(4 * scn + 4)]
    NCH = len(CHUNKS)
    for h in range(16):
        with nc.Block() as blk:
            @blk.sync
            def _(sync):
                src = ap_off(w["Wv"], h * 64,
                             [[D, 128], [128 * D, KC], [1, 64]])
                vld.inc(sync.dma_start(out=ctx.W[:, 0:512], in_=src), 16)
                sync.wait_ge(vld.h, vld.n)

            @blk.gpsimd
            def _(g):
                vct = ctx.sem("aa_vc")
                g.wait_ge(vld.h, vld.n)
                vct.inc(g.tensor_copy(ctx.Wb[:, 4096:4608], ctx.W[:, 0:512]))

            @blk.tensor
            def _(t):
                ste0, ve0, ex0, mk0 = ste.n, ve.n, ex.n, mk.n
                zr0, zbe0, fin0 = zr.n, zbe.n, fin.n
                gq = 0
                for wi in range(2):
                    for i in range(NT):
                        bank = ctx.pb[6 + gq % 2]
                        if gq >= 2:
                            t.wait_ge(ste.h, ste0 + gq - 1)
                        inst = t.transpose(
                            bank[0:16, (i % 4) * 128:(i % 4) * 128 + 128],
                            ctx.C[:, wi * 4096 + i * 256 + h * 16:
                                  wi * 4096 + i * 256 + h * 16 + 16],
                            ctx.ident[:, :])
                        if i % 4 == 3:
                            stg.inc(inst)
                            gq += 1
                t.wait_ge(ctx.sem("aa_vc").h, ctx.sem("aa_vc").n)
                for i in range(NT):
                    if i >= 2:
                        t.wait_ge(ve.h, ve0 + i - 1)
                    for k in range(KC):
                        inst = t.matmul(ctx.pb[i % 2][:, 0:64],
                                        ctx.a_fm(k, i * 128, 128),
                                        ctx.Wb[:, 4096 + k * 64:4096 + (k + 1) * 64],
                                        start=(k == 0), stop=(k == KC - 1))
                    vm.inc(inst)
                t.wait_ge(ve.h, ve0 + NT)
                t.wait_ge(ste.h, ste0 + 8)
                npart = 0
                for g, (scn, tc) in enumerate(CHUNKS):
                    partial = tc >= 4 * scn
                    if g >= 3:
                        t.wait_ge(ex.h, ex0 + g - 2)
                    sc.inc(t.matmul(
                        ctx.pb[[0, 1, 4][g % 3]][:, :],
                        ctx.Wb[0:16, 2048 + tc * 128:2048 + tc * 128 + 128],
                        ctx.Wb[0:16, scn * 512:(scn + 1) * 512],
                        start=True, stop=True))
                    if partial:
                        npart += 1
                        t.wait_ge(mk.h, mk0 + npart)
                    else:
                        t.wait_ge(ex.h, ex0 + g + 1)
                    if tc == 0 and scn >= 2:
                        t.wait_ge(fin.h, fin0 + scn - 1)
                    ym.inc(t.matmul(
                        ctx.pb[2 + scn % 2][0:65, :],
                        ctx.Wb[:, 6528 + tc * 65:6528 + tc * 65 + 65],
                        ctx.Wb[:, 4608 + (g % 2) * 512:4608 + (g % 2) * 512 + 512],
                        start=(tc == 0), stop=(tc == 4 * scn + 3),
                        skip_group_check=True))
                    if tc == 4 * scn + 3:
                        t.wait_ge(zr.h, zr0 + scn + 1)
                        if scn >= 2:
                            t.wait_ge(zbe.h, zbe0 + scn - 1)
                        zb.inc(t.matmul(ctx.pb[6 + scn % 2][0:64, :],
                                        ctx.ones_row[0:1, 0:64],
                                        ctx.W[0:1, 3600:4112],
                                        start=True, stop=True))

            @blk.scalar
            def _(s):
                sc0 = sc.n - NCH
                ym0s = ym.n - NCH
                for g, (scn, tc) in enumerate(CHUNKS):
                    s.wait_ge(sc.h, sc0 + g + 1)
                    if g >= 2:
                        s.wait_ge(ym.h, ym0s + g - 1)
                    ex.inc(s.activation(
                        ctx.Wb[:, 4608 + (g % 2) * 512:4608 + (g % 2) * 512 + 512],
                        ctx.pb[[0, 1, 4][g % 3]][:, :], AF.Exp,
                        bias=ctx.W[:, 6802:6803]))

            @blk.vector
            def _(v):
                stg0 = stg.n - 8
                vm0 = vm.n - NT
                ex0v = ex.n - NCH
                ym0v = ym.n - NCH
                zb0 = zb.n - 4
                for gq in range(8):
                    v.wait_ge(stg.h, stg0 + gq + 1)
                    bank = ctx.pb[6 + gq % 2]
                    wi, grp = divmod(gq, 4)
                    if wi == 0:
                        dst = ctx.Wb[0:16, grp * 512:grp * 512 + 512]
                    else:
                        dst = ctx.Wb[0:16, 2048 + grp * 512:2048 + grp * 512 + 512]
                    ste.inc(v.tensor_copy(dst, bank[0:16, 0:512]))
                for i in range(NT):
                    v.wait_ge(vm.h, vm0 + i + 1)
                    ve.inc(v.tensor_copy(ctx.Wb[:, 6528 + i * 65:6528 + i * 65 + 64],
                                         ctx.pb[i % 2][:, 0:64]))
                for g, (scn, tc) in enumerate(CHUNKS):
                    if tc >= 4 * scn:
                        v.wait_ge(ex.h, ex0v + g + 1)
                        esl = ctx.Wb[:, 4608 + (g % 2) * 512:
                                     4608 + (g % 2) * 512 + 512]
                        off = 5632 + 384 - 128 * (tc - 4 * scn)
                        mk.inc(v.tensor_tensor(esl, esl,
                                               ctx.Wb[:, off:off + 512],
                                               ALU.mult))
                    if tc == 4 * scn + 3:
                        v.wait_ge(ym.h, ym0v + g + 1)
                        ybank = ctx.pb[2 + scn % 2]
                        zr.inc(v.reciprocal(ctx.W[0:1, 3600:4112],
                                            ybank[64:65, :]))
                        v.wait_ge(zb.h, zb0 + scn + 1)
                        zbe.inc(v.tensor_copy(ctx.W[0:64, 2576:3088],
                                              ctx.pb[6 + scn % 2][0:64, :]))
                        v.wait_ge(zbe.h, zbe.n)
                        bslice = ctx.B[(h % 2) * 64:(h % 2) * 64 + 64,
                                       (h // 2) * T + scn * 512:
                                       (h // 2) * T + scn * 512 + 512]
                        inst = v.tensor_tensor(bslice, ybank[0:64, :],
                                               ctx.W[0:64, 2576:3088], ALU.mult)
                        ctx.fence(v, "f_dve", inst)
                        bvcol = ctx.W[(h % 2) * 64:(h % 2) * 64 + 64,
                                      7286 + h // 2:7287 + h // 2]
                        inst = v.tensor_scalar(bslice, bslice, bvcol, None,
                                               ALU.add)
                        fin.inc(inst)


def sub_l1_attn_a(ctx, x_in, x_out, b, w):
    emit_l1_attn_a_prep(ctx, w)
    emit_in_ln_tp(ctx, x_in, b)
    emit_l1_attn_a_mix(ctx, b, w, b == 0)
    proj = dict(
        kchunks=8, kpart=128,
        w_src=lambda of: ap_off(w["Wo"], of * 128,
                                [[D, 128], [128 * D, KC], [1, 128]]),
        src=lambda k, t0, n: ctx.b_fm(k, t0, n),
        bias=w["bo"],
    )
    emit_out_phase(ctx, x_in, x_out, b, proj=proj)


SUBLAYERS["l1_attn_a"] = sub_l1_attn_a






def _wcast_tiles(ctx, h, b):
    if b != 0 or getattr(ctx, "wcast_w", None) is None:
        return []
    return list(range(h * 12, h * 12 + 12))


def emit_wcast_sync(ctx, sync, tiles):
    w = ctx.wcast_w
    wcl = ctx.sem("wc_ld")
    wcs = ctx.sem("wc_st")
    for t in tiles:
        e, hc = divmod(t, 32)
        w1 = w["sw1"] if e < 2 else w["rw1"]
        w2 = w["sw2"] if e < 2 else w["rw2"]
        ei = e if e < 2 else e - 2
        cb = 4096 + (t % 2) * 2048
        if t >= 2:
            sync.wait_ge(wcs.h, 16 * (t - 1))
        src1 = ap_off(w1, ei * D * 4096 + hc * 128,
                      [[4096, 128], [128 * 4096, KC], [1, 128]])
        wcl.inc(sync.dma_start(out=ctx.C[:, cb:cb + 1024], in_=src1), 16)
        wcl.inc(sync.dma_start(
            out=ctx.C[:, cb + 1024:cb + 2048],
            in_=ap_off(w2, ei * 4096 * D + hc * 128 * D,
                       [[D, 128], [1, D]])), 16)


def emit_wcast_gp(ctx, g, tiles):
    wcl = ctx.sem("wc_ld")
    wct = ctx.sem("wc_ct")
    wcs = ctx.sem("wc_st")
    for t in tiles:
        cb = 4096 + (t % 2) * 2048
        g.wait_ge(wcl.h, 32 * (t + 1))
        if t >= 1:
            g.wait_ge(wcs.h, 16 * t)
        wct.inc(g.tensor_copy(ctx.SW[:, 0:2048], ctx.C[:, cb:cb + 2048]))
        dstw = ap_off(ctx.wsc, t * 128 * 2048, [[2048, 128], [1, 2048]])
        wcs.inc(g.dma_start(out=dstw, in_=ctx.SW[:, 0:2048]), 16)
        ctx.wcast_cursor = t + 1


def emit_l0_attn_a_prep(ctx, w):
    nc = ctx.nc
    ld = ctx.sem("da_pld")
    pr = ctx.sem("da_pr")
    with nc.Block() as blk:
        @blk.sync
        def _(sync):
            for i, nm in enumerate(["lambda_q1", "lambda_k1", "lambda_q2",
                                    "lambda_k2"]):
                ld.inc(sync.dma_start(out=ctx.R[0:1, i * 64:(i + 1) * 64],
                                      in_=w[nm].rearrange("(o c) -> o c", o=1)), 16)
            sync.wait_ge(ld.h, ld.n)
    with nc.Block() as blk:
        @blk.vector
        def _(v):
            prb = pr.n
            inst = v.tensor_tensor(ctx.R[0:1, 512:576], ctx.R[0:1, 0:64],
                                   ctx.R[0:1, 64:128], ALU.mult)
            ctx.fence(v, "f_dve", inst)
            v.tensor_reduce(ctx.R[0:1, 256:257], ctx.R[0:1, 512:576],
                            axis=AX.X, op=ALU.add)
            inst = v.tensor_tensor(ctx.R[0:1, 576:640], ctx.R[0:1, 128:192],
                                   ctx.R[0:1, 192:256], ALU.mult)
            ctx.fence(v, "f_dve", inst)
            inst = v.tensor_reduce(ctx.R[0:1, 257:258], ctx.R[0:1, 576:640],
                                   axis=AX.X, op=ALU.add)
            pr.inc(inst)
            v.wait_ge(pr.h, prb + 2)
            inst = v.tensor_tensor(ctx.R[0:1, 259:260], ctx.R[0:1, 259:260],
                                   ctx.R[0:1, 260:261], ALU.subtract)
            ctx.fence(v, "f_dve", inst)
            v.tensor_scalar(ctx.R[0:1, 258:259], ctx.R[0:1, 259:260],
                            0.8, None, ALU.add)
        @blk.scalar
        def _(s):
            s.wait_ge(pr.h, pr.n - 1 + 1)
            pr.inc(s.activation(ctx.R[0:1, 259:261], ctx.R[0:1, 256:258],
                                AF.Exp))


def emit_l0_attn_a_mix(ctx, b, w):
    nc = ctx.nc
    SCALE = 0.125
    SCB = [0, 1, 5]
    lam = ctx.sem("da_lam")
    lme = ctx.sem("da_lme")
    nwl = ctx.sem("da_nwl")
    ldw = ctx.sem("da_ldw")
    qm = ctx.sem("da_qm")
    qv = ctx.sem("da_qv")
    nwb = ctx.sem("da_nw")
    nwe = ctx.sem("da_nwe")
    vt = ctx.sem("da_vt")
    vte = ctx.sem("da_vte")
    sc = ctx.sem("da_sc")
    ex = ctx.sem("da_ex")
    ym = ctx.sem("da_ym")
    cb = ctx.sem("da_cb")
    cbe = ctx.sem("da_cbe")
    ytp = ctx.sem("da_ytp")
    rm1 = ctx.sem("da_rm1")
    rm2 = ctx.sem("da_rm2")
    rm3 = ctx.sem("da_rm3")

    with nc.Block() as blk:
        @blk.sync
        def _(sync):
            nwl.inc(sync.dma_start(out=ctx.W[0:1, 4692:5716],
                                   in_=ap_off(w["norm_w"], 0,
                                              [[1024, 1], [1, 1024]])), 16)
            sync.wait_ge(nwl.h, nwl.n)
        @blk.tensor
        def _(t):
            lam.inc(t.matmul(ctx.pb[7][:, 256:257], ctx.ones_row[0:1, 0:128],
                             ctx.R[0:1, 258:259], start=True, stop=True))
        @blk.vector
        def _(v):
            v.wait_ge(lam.h, lam.n)
            lme.inc(v.tensor_copy(ctx.S[:, 256:257], ctx.pb[7][:, 256:257]))
        @blk.gpsimd
        def _(g):
            g.memset(bass.AP(ctx.Wb, 2560 + 64, [[7968, 128], [65, 16]]), 1.0)
    with nc.Block() as blk:
        @blk.vector
        def _(v):
            v.tensor_scalar(ctx.W[0:1, 4692:5716], ctx.W[0:1, 4692:5716],
                            0.2, None, ALU.mult)

    for h in range(16):
        with nc.Block() as blk:
            @blk.sync
            def _(sync):
                for wi in range(3):
                    srcw = ap_off(w["w_qkv"], wi * 1024 + h * 64,
                                  [[3072, 128], [128 * 3072, KC], [1, 64]])
                    ldw.inc(sync.dma_start(
                        out=ctx.W[:, wi * 512:wi * 512 + 512], in_=srcw), 16)
                sync.wait_ge(ldw.h, ldw.n)
                emit_wcast_sync(ctx, sync, _wcast_tiles(ctx, h, b))

            @blk.gpsimd
            def _(g):
                dct = ctx.sem("da_ct")
                g.wait_ge(ldw.h, ldw.n)
                dct.inc(g.tensor_copy(ctx.Wb[:, 0:1536], ctx.W[:, 0:1536]))
                emit_wcast_gp(ctx, g, _wcast_tiles(ctx, h, b))

            @blk.tensor
            def _(t):
                qv0, vte0, nwe0 = qv.n, vte.n, nwe.n
                ex0, ym0 = ex.n, ym.n
                cbe0, ytp0, rm10, rm30 = cbe.n, ytp.n, rm1.n, rm3.n
                rm20t = ctx.sem("da_rm2").n
                t.wait_ge(ctx.sem("da_ct").h, ctx.sem("da_ct").n)
                nwb.inc(t.matmul(ctx.pb[7][:, 320:384],
                                 ctx.ones_row[0:1, 0:128],
                                 ctx.W[0:1, 4692 + h * 64:4692 + h * 64 + 64],
                                 start=True, stop=True))
                for g in range(3):
                    for ncn in range(4):
                        gi = g * 4 + ncn
                        if gi >= 2:
                            t.wait_ge(qv.h, qv0 + gi - 1)
                        pslice = ctx.pb[gi % 2][0:64, :] if g < 2 else \
                            ctx.pb[gi % 2][64:128, :]
                        for k in range(KC):
                            inst = t.matmul(
                                pslice,
                                ctx.Wb[:, g * 512 + k * 64:g * 512 + (k + 1) * 64],
                                ctx.a_fm(k, ncn * 512, 512),
                                start=(k == 0), stop=(k == KC - 1))
                        qm.inc(inst)
                t.wait_ge(qv.h, qv0 + 12)
                for tc in range(NT):
                    g2 = tc // 4
                    if g2 >= 2:
                        t.wait_ge(vte.h, vte0 + g2 - 1)
                    inst = t.transpose(
                        ctx.pb[4 + g2 % 2][:, (tc % 4) * 128:(tc % 4) * 128 + 64],
                        ctx.C[64:128, tc * 128:(tc + 1) * 128],
                        ctx.ident[64:128, 64:128])
                    if tc % 4 == 3:
                        vt.inc(inst)
                t.wait_ge(vte.h, vte0 + 4)
                t.wait_ge(nwe.h, nwe0 + 1)
                nit = 0
                for scn in range(4):
                    for mat in range(2):
                        ybank = 2 + mat
                        for tc in range(NT):
                            git = nit * NT + tc
                            if git >= 3:
                                t.wait_ge(ex.h, ex0 + git - 2)
                            inst = t.matmul(
                                ctx.pb[SCB[git % 3]][:, :],
                                ctx.Wb[mat * 32:mat * 32 + 32,
                                       5648 + tc * 128:5648 + (tc + 1) * 128],
                                ctx.Wb[mat * 32:mat * 32 + 32,
                                       3600 + scn * 512:3600 + (scn + 1) * 512],
                                start=True, stop=True)
                            sc.inc(inst)
                            if tc >= 1:
                                t.wait_ge(ex.h, ex0 + git)
                                ti = tc - 1
                                inst = t.matmul(
                                    ctx.pb[ybank][0:65, :],
                                    ctx.Wb[:, 2560 + ti * 65:2560 + ti * 65 + 65],
                                    ctx.Wb[:, 1536 + (ti % 2) * 512:
                                           1536 + (ti % 2) * 512 + 512],
                                    start=(ti == 0), stop=False,
                                    skip_group_check=True)
                        t.wait_ge(ex.h, ex0 + nit * NT + NT)
                        inst = t.matmul(
                            ctx.pb[ybank][0:65, :],
                            ctx.Wb[:, 2560 + 15 * 65:2560 + 15 * 65 + 65],
                            ctx.Wb[:, 1536 + (15 % 2) * 512:
                                   1536 + (15 % 2) * 512 + 512],
                            start=False, stop=True, skip_group_check=True)
                        ym.inc(inst)
                        nit += 1
                    t.wait_ge(rm1.h, rm10 + scn + 1)
                    t.wait_ge(cbe.h, cbe0 + max(0, (scn - 1) * 2))
                    cb.inc(t.matmul(ctx.pb[6][0:64, :], ctx.ones_row[0:1, 0:64],
                                    ctx.S[0:1, 1280:1792], start=True,
                                    stop=True))
                    cb.inc(t.matmul(ctx.pb[7][0:64, 0:512],
                                    ctx.ones_row[0:1, 0:64],
                                    ctx.W[0:1, 4112:4624], start=True,
                                    stop=True))
                    t.wait_ge(cbe.h, cbe0 + scn * 2 + 2)
                    if scn >= 1:
                        t.wait_ge(rm2.h, rm20t + (scn - 1) * 4 + 4)
                    for q4 in range(4):
                        inst = t.transpose(
                            ctx.pb[4][:, q4 * 128:q4 * 128 + 64],
                            ctx.W[0:64, 3600 + q4 * 128:3600 + (q4 + 1) * 128],
                            ctx.ident[0:64, 0:64])
                    ytp.inc(inst)

            @blk.scalar
            def _(s):
                sc0 = sc.n - 4 * 2 * NT
                rm20 = rm2.n
                rstd = ctx.sem("da_rstd")

                def rms_chain(scn2):
                    for q4 in range(4):
                        s.wait_ge(rm2.h, rm20 + scn2 * 4 + q4 + 1)
                        buf = ctx.W[:, 5716 + (scn2 % 2) * 256 + q4 * 64:
                                    5716 + (scn2 % 2) * 256 + q4 * 64 + 64]
                        inst = s.activation(ctx.W[:, 6228:6292], buf,
                                            AF.Square,
                                            accum_out=ctx.W[:, 6292:6293])
                        ctx.fence(s, "f_act", inst)
                        inst = s.activation(ctx.W[:, 6293:6294],
                                            ctx.W[:, 6292:6293], AF.Ln,
                                            scale=1.0 / 64.0,
                                            bias=ctx.ceps[:, :])
                        ctx.fence(s, "f_act", inst)
                        rstd.inc(s.activation(
                            ctx.W[:, 6294 + (scn2 % 2) * 4 + q4:
                                  6295 + (scn2 % 2) * 4 + q4],
                            ctx.W[:, 6293:6294], AF.Exp, scale=-0.5))

                nit = 0
                for scn in range(4):
                    for mat in range(2):
                        for tc in range(NT):
                            git = nit * NT + tc
                            s.wait_ge(sc.h, sc0 + git + 1)
                            ex.inc(s.activation(
                                ctx.Wb[:, 1536 + (tc % 2) * 512:
                                       1536 + (tc % 2) * 512 + 512],
                                ctx.pb[SCB[git % 3]][:, :], AF.Exp, scale=SCALE))
                        nit += 1
                    if scn >= 1:
                        rms_chain(scn - 1)
                rms_chain(3)

            @blk.vector
            def _(v):
                qm0 = qm.n - 12
                vt0 = vt.n - 4
                nwb0 = nwb.n - 1
                ym0 = ym.n - 8
                cb0 = cb.n - 8
                ytp0 = ytp.n - 4
                rm20 = rm2.n
                for gi in range(12):
                    g, ncn = divmod(gi, 4)
                    v.wait_ge(qm.h, qm0 + gi + 1)
                    if g == 0:
                        dst = ctx.Wb[0:64, 3600 + ncn * 512:3600 + ncn * 512 + 512]
                        srcp = ctx.pb[gi % 2][0:64, :]
                    elif g == 1:
                        dst = ctx.Wb[0:64, 5648 + ncn * 512:5648 + ncn * 512 + 512]
                        srcp = ctx.pb[gi % 2][0:64, :]
                    else:
                        dst = ctx.C[64:128, ncn * 512:ncn * 512 + 512]
                        srcp = ctx.pb[gi % 2][64:128, :]
                    qv.inc(v.tensor_copy(dst, srcp))
                v.wait_ge(nwb.h, nwb0 + 1)
                nwe.inc(v.tensor_copy(ctx.W[:, 4624:4688], ctx.pb[7][:, 320:384]))
                for g2 in range(4):
                    v.wait_ge(vt.h, vt0 + g2 + 1)
                    bank = ctx.pb[4 + g2 % 2]
                    srcv = bass.AP(bank, 0, [[512, 128], [128, 4], [1, 64]])
                    dstv = bass.AP(ctx.Wb, 2560 + g2 * 4 * 65,
                                   [[7968, 128], [65, 4], [1, 64]])
                    vte.inc(v.tensor_copy(dstv, srcv))
                rstd = ctx.sem("da_rstd")
                rstd0 = rstd.n - 16

                def rstd_mults(scn2):
                    for q4 in range(4):
                        si = scn2 * 4 + q4
                        v.wait_ge(rstd.h, rstd0 + scn2 * 4 + q4 + 1)
                        buf = ctx.W[:, 5716 + (scn2 % 2) * 256 + q4 * 64:
                                    5716 + (scn2 % 2) * 256 + q4 * 64 + 64]
                        inst = v.tensor_scalar(buf, buf,
                                               ctx.W[:, 6294 + (scn2 % 2) * 4 + q4:
                                                     6295 + (scn2 % 2) * 4 + q4],
                                               None, ALU.mult)
                        ctx.fence(v, "f_dve", inst)
                        dstB = bass.AP(ctx.B,
                                       (si % 8) * T + 2 * h + (si // 8),
                                       [[KC * T, 128], [32, 64]])
                        inst = v.tensor_tensor(dstB, buf,
                                               ctx.W[:, 4624:4688], ALU.mult)
                        rm3.inc(inst)

                for scn in range(4):
                    v.wait_ge(ym.h, ym0 + scn * 2 + 2)
                    inst = v.reciprocal(ctx.S[0:1, 1280:1792],
                                        ctx.pb[2][64:65, :])
                    ctx.fence(v, "f_dve", inst)
                    inst = v.reciprocal(ctx.W[0:1, 4112:4624],
                                        ctx.pb[3][64:65, :])
                    ctx.fence(v, "f_dve", inst)
                    inst = v.tensor_scalar(ctx.W[0:1, 4112:4624],
                                           ctx.W[0:1, 4112:4624],
                                           ctx.S[0:1, 256:257], None,
                                           ALU.mult)
                    rm1.inc(inst)
                    v.wait_ge(cb.h, cb0 + scn * 2 + 2)
                    inst = v.tensor_copy(ctx.W[0:64, 2576:3088],
                                         ctx.pb[6][0:64, :])
                    cbe.inc(inst)
                    inst = v.tensor_copy(ctx.W[0:64, 3088:3600],
                                         ctx.pb[7][0:64, 0:512])
                    ctx.fence(v, "f_dve", inst)
                    inst = v.tensor_tensor(ctx.W[0:64, 3600:4112],
                                           ctx.pb[2][0:64, :],
                                           ctx.W[0:64, 2576:3088], ALU.mult)
                    ctx.fence(v, "f_dve", inst)
                    inst = v.tensor_tensor(ctx.W[0:64, 2576:3088],
                                           ctx.pb[3][0:64, :],
                                           ctx.W[0:64, 3088:3600], ALU.mult)
                    ctx.fence(v, "f_dve", inst)
                    inst = v.tensor_tensor(ctx.W[0:64, 3600:4112],
                                           ctx.W[0:64, 3600:4112],
                                           ctx.W[0:64, 2576:3088], ALU.subtract)
                    cbe.inc(inst)
                    v.wait_ge(ytp.h, ytp0 + scn + 1)
                    for q4 in range(4):
                        buf = ctx.W[:, 5716 + (scn % 2) * 256 + q4 * 64:
                                    5716 + (scn % 2) * 256 + q4 * 64 + 64]
                        rm2.inc(v.tensor_copy(buf, ctx.pb[4][:, q4 * 128:
                                                             q4 * 128 + 64]))
                    if scn >= 1:
                        rstd_mults(scn - 1)
                rstd_mults(3)


def sub_l0_attn_a(ctx, x_in, x_out, b, w):
    if b == 0:
        emit_l0_attn_a_prep(ctx, w)
    emit_in_ln_tp(ctx, x_in, b)
    emit_l0_attn_a_mix(ctx, b, w)
    proj = dict(
        kchunks=8, kpart=128,
        w_src=lambda of: ap_off(w["wo"], of * 128,
                                [[D, 128], [128 * D, KC], [1, 128]]),
        src=lambda k, t0, n: ctx.b_fm(k, t0, n),
        bias=None,
    )
    emit_out_phase(ctx, x_in, x_out, b, proj=proj)


SUBLAYERS["l0_attn_a"] = sub_l0_attn_a





def emit_l1_attn_b_masks(ctx):
    nc = ctx.nc
    mk = ctx.sem("rk_mk")
    mke = ctx.sem("rk_mke")
    SC = 5506
    for li, lv in enumerate([1, 2, 4, 8, 16, 32, 64]):
        with nc.Block() as blk:
            @blk.gpsimd
            def _(g):
                g.iota(ctx.W[0:64, SC:SC + 128], [[1, 128]],
                       channel_multiplier=-2 * lv,
                       allow_small_or_imprecise_dtypes=True)
        with nc.Block() as blk:
            @blk.vector
            def _(v):
                it = ctx.W[0:64, SC:SC + 128]
                lb = ctx.W[0:64, SC + 128:SC + 256]
                ub = ctx.W[0:64, SC + 256:SC + 384]
                t1 = ctx.W[0:64, SC + 384:SC + 512]
                inst = v.tensor_scalar(t1, it, -0.5, None, ALU.is_ge)
                ctx.fence(v, "f_dve", inst)
                inst = v.tensor_scalar(lb, it, lv - 0.5, None, ALU.is_le)
                ctx.fence(v, "f_dve", inst)
                inst = v.tensor_tensor(lb, lb, t1, ALU.mult)
                ctx.fence(v, "f_dve", inst)
                inst = v.tensor_scalar(ub, it, lv - 0.5, None, ALU.is_ge)
                ctx.fence(v, "f_dve", inst)
                inst = v.tensor_scalar(t1, it, 2 * lv - 0.5, None, ALU.is_le)
                ctx.fence(v, "f_dve", inst)
                mk.inc(v.tensor_tensor(ub, ub, t1, ALU.mult))
        with nc.Block() as blk:
            @blk.tensor
            def _(t):
                mke.inc(t.matmul(ctx.pb[6][:, 0:128],
                                 ctx.W[0:64, SC + 128:SC + 256],
                                 ctx.W[0:64, SC + 256:SC + 384],
                                 start=True, stop=True))
            @blk.vector
            def _(v):
                v.wait_ge(mke.h, mke.n)
                v.tensor_copy(ctx.W[:, (li + 1) * 128:(li + 2) * 128],
                              ctx.pb[6][:, 0:128])
                if li == 0:
                    v.tensor_copy(ctx.W[:, 0:128], ctx.ident[:, :])


def emit_l1_attn_b_mix(ctx, b, w):
    nc = ctx.nc
    LV = [1, 2, 4, 8, 16, 32, 64]
    NCH = 16
    ld = ctx.sem("rk_ld")
    dtp = ctx.sem("rk_dtp")
    wtp = ctx.sem("rk_wtp")
    wte = ctx.sem("rk_wte")
    pm = ctx.sem("rk_pm")
    pv = ctx.sem("rk_pv")
    sig = ctx.sem("rk_sig")
    ktp = ctx.sem("rk_ktp")
    kte = ctx.sem("rk_kte")
    tbl = ctx.sem("rk_tbl")
    fac = ctx.sem("rk_fac")
    lm = ctx.sem("rk_lm")
    la = ctx.sem("rk_la")
    ym = ctx.sem("rk_ym")
    yev = ctx.sem("rk_yev")
    smm = ctx.sem("rk_sm")
    sev = ctx.sem("rk_sev")

    emit_l1_attn_b_masks(ctx)

    with nc.Block() as blk:
        @blk.sync
        def _(sync):
            ld.inc(sync.dma_start(out=ctx.W[0:16, 6018:6082],
                                  in_=w["time_decay"]), 16)
            ld.inc(sync.dma_start(out=ctx.W[0:16, 6082:6146],
                                  in_=w["time_first"]), 16)
            sync.wait_ge(ld.h, ld.n)
    with nc.Block() as blk:
        @blk.tensor
        def _(t):
            t.transpose(ctx.pb[6][0:64, 0:16], ctx.W[0:16, 6018:6082],
                        ctx.ident[0:16, 0:16])
            dtp.inc(t.transpose(ctx.pb[6][0:64, 16:32], ctx.W[0:16, 6082:6146],
                                ctx.ident[0:16, 0:16]))
        @blk.vector
        def _(v):
            v.wait_ge(dtp.h, dtp.n)
            v.tensor_copy(ctx.S[0:64, 224:240], ctx.pb[6][0:64, 0:16])
            v.tensor_copy(ctx.S[0:64, 240:256], ctx.pb[6][0:64, 16:32])

    for h in range(16):
        with nc.Block() as blk:
            @blk.sync
            def _(sync):
                wtp0 = wtp.n
                for wi in range(3):
                    nm = ["Wk", "Wv", "Wr"][wi]
                    if wi >= 1:
                        sync.wait_ge(wtp.h, wtp0 + wi * 2)
                    ld.inc(sync.dma_start(
                        out=ctx.C[0:64, 6144:7168],
                        in_=ap_off(w[nm], h * 64 * D, [[D, 64], [1, D]])), 16)

            @blk.tensor
            def _(t):
                ld0 = ld.n - 3 * 16
                wte0 = wte.n
                for wi in range(3):
                    t.wait_ge(ld.h, ld0 + 16 * (wi + 1))
                    if wi >= 1:
                        t.wait_ge(wte.h, wte0 + wi * 2)
                    for k in range(KC):
                        if k == 4:
                            t.wait_ge(wte.h, wte0 + wi * 2 + 1)
                        inst = t.transpose(
                            ctx.pb[6 + k % 2][:, (k // 2 % 2) * 128:
                                              (k // 2 % 2) * 128 + 64],
                            ctx.C[0:64, 6144 + k * 128:6144 + (k + 1) * 128],
                            ctx.ident[0:64, 0:64])
                        if k % 4 == 3:
                            wtp.inc(inst)

            @blk.vector
            def _(v):
                wtp0 = wtp.n - 6
                for wi in range(3):
                    for g2 in range(2):
                        v.wait_ge(wtp.h, wtp0 + wi * 2 + g2 + 1)
                        for k in range(g2 * 4, g2 * 4 + 4):
                            inst = v.tensor_copy(
                                ctx.Wb[:, 4096 + wi * 512 + k * 64:
                                       4096 + wi * 512 + (k + 1) * 64],
                                ctx.pb[6 + k % 2][:, (k // 2 % 2) * 128:
                                                  (k // 2 % 2) * 128 + 64])
                        wte.inc(inst)

        with nc.Block() as blk:
            @blk.tensor
            def _(t):
                pv0, kte0, tbl0 = pv.n, kte.n, tbl.n
                fac0, la0, yev0, sev0 = fac.n, la.n, yev.n, sev.n
                for g in range(3):
                    for ncn in range(4):
                        gi = g * 4 + ncn
                        if gi >= 2:
                            t.wait_ge(pv.h, pv0 + gi - 1)
                        for k in range(KC):
                            inst = t.matmul(
                                ctx.pb[gi % 2][0:64, :],
                                ctx.Wb[:, 4096 + g * 512 + k * 64:
                                       4096 + g * 512 + (k + 1) * 64],
                                ctx.a_fm(k, ncn * 512, 512),
                                start=(k == 0), stop=(k == KC - 1))
                        pm.inc(inst)
                t.wait_ge(pv.h, pv0 + 12)
                for g in range(2):
                    for tc in range(NCH):
                        g2 = (g * NCH + tc) // 4
                        if g2 >= 2:
                            t.wait_ge(kte.h, kte0 + g2 - 1)
                        inst = t.transpose(
                            ctx.pb[6 + g2 % 2][:, (tc % 4) * 128:(tc % 4) * 128 + 64],
                            ctx.C[0:64, g * 2048 + tc * 128:g * 2048 + (tc + 1) * 128],
                            ctx.ident[0:64, 0:64])
                        if tc % 4 == 3:
                            ktp.inc(inst)
                t.wait_ge(tbl.h, tbl0 + 1)
                t.wait_ge(kte.h, kte0 + 8)
                ktp.inc(t.transpose(ctx.pb[6][:, 448:512],
                                    ctx.W[0:64, 2946:3074],
                                    ctx.ident[0:64, 0:64]))
                t.wait_ge(kte.h, kte0 + 9)
                for c in range(NCH):
                    t.wait_ge(fac.h, fac0 + c + 1)
                    if c >= 1:
                        t.wait_ge(la.h, la0 + c)
                    for li in range(8):
                        bank = ctx.pb[li // 4]
                        qof = (li % 4) * 128
                        if li == 0:
                            lhs = ctx.W[0:64, 3394:3522]
                            rhs = ctx.C[0:64, 4096 + c * 128:4096 + (c + 1) * 128]
                        else:
                            lhs = ctx.W[0:64, 3522 + (li - 1) * 128:
                                        3522 + li * 128]
                            rhs = ctx.W[0:64, 4418 + (li - 1) * 128:
                                        4418 + li * 128]
                        inst = t.matmul(bank[:, qof:qof + 128], lhs, rhs,
                                        start=(li % 4 == 0), stop=True,
                                        skip_group_check=True)
                    lm.inc(inst)
                    t.wait_ge(la.h, la0 + c + 1)
                    if c >= 1:
                        t.wait_ge(yev.h, yev0 + c)
                        t.wait_ge(sev.h, sev0 + c)
                    t.matmul(ctx.pb[2][0:64, 0:128],
                             ctx.C[:, 7168 + c * 64:7168 + (c + 1) * 64],
                             ctx.W[:, 3138:3266], start=True, stop=False,
                             skip_group_check=True)
                    ym.inc(t.matmul(ctx.pb[2][0:64, 0:128],
                                    ctx.W[0:64, 3266 + (c % 2) * 64:
                                          3330 + (c % 2) * 64],
                                    ctx.W[0:64, 5378:5506],
                                    start=False, stop=True,
                                    skip_group_check=True))
                    if c >= 1:
                        t.wait_ge(sev.h, sev0 + c)
                    smm.inc(t.matmul(ctx.pb[5][0:64, 0:64],
                                     ctx.W[:, 5314:5378],
                                     ctx.C[:, 7168 + c * 64:7168 + (c + 1) * 64],
                                     start=True, stop=True))

            @blk.scalar
            def _(s):
                pm0 = pm.n - 12
                for ncn in range(4):
                    gi = 8 + ncn
                    s.wait_ge(pm.h, pm0 + gi + 1)
                    sig.inc(s.activation(
                        ctx.C[0:64, 4096 + ncn * 512:4096 + (ncn + 1) * 512],
                        ctx.pb[gi % 2][0:64, :], AF.Exp, scale=-1.0))

            @blk.vector
            def _(v):
                pm0 = pm.n - 12
                sg0 = sig.n - 4
                ktp0 = ktp.n - 9
                lm0 = lm.n - NCH
                ym0 = ym.n - NCH
                sm0 = smm.n - NCH
                for gi in range(8):
                    g, ncn = divmod(gi, 4)
                    v.wait_ge(pm.h, pm0 + gi + 1)
                    pv.inc(v.tensor_copy(
                        ctx.C[0:64, g * 2048 + ncn * 512:g * 2048 + (ncn + 1) * 512],
                        ctx.pb[gi % 2][0:64, :]))
                for ncn in range(4):
                    v.wait_ge(sig.h, sg0 + ncn + 1)
                    dst = ctx.C[0:64, 4096 + ncn * 512:4096 + (ncn + 1) * 512]
                    inst = v.tensor_scalar(dst, dst, 1.0, None, ALU.add)
                    ctx.fence(v, "f_dve", inst)
                    pv.inc(v.reciprocal(dst, dst))
                dcol = ctx.S[0:64, 224 + h:225 + h]
                tcol = ctx.S[0:64, 240 + h:241 + h]
                dp = lambda a, bb: ctx.W[0:64, 2816 + a:2816 + bb]
                rp = lambda a, bb: ctx.W[0:64, 2945 + a:2945 + bb]
                v.memset(dp(0, 1), 1.0)
                inst = v.tensor_scalar(dp(1, 2), dcol, 1.0, None, ALU.mult)
                ctx.fence(v, "f_dve", inst)
                p = 1
                while p < 128:
                    q = min(p, 128 - p)
                    inst = v.tensor_scalar(dp(p + 1, p + 1 + q), dp(1, 1 + q),
                                           dp(p, p + 1), None, ALU.mult)
                    ctx.fence(v, "f_dve", inst)
                    p *= 2
                inst = v.tensor_scalar(rp(128, 129), tcol, 1.0, None,
                                       ALU.mult)
                ctx.fence(v, "f_dve", inst)
                inst = v.tensor_scalar(rp(127, 128), rp(128, 129),
                                       dp(1, 2), None, ALU.mult)
                ctx.fence(v, "f_dve", inst)
                p = 2
                while p <= 64:
                    inst = v.tensor_scalar(rp(129 - 2 * p, 129 - p),
                                           rp(129 - p, 129), dp(p, p + 1),
                                           None, ALU.mult)
                    if p < 64:
                        ctx.fence(v, "f_dve", inst)
                    p *= 2
                tbl.inc(inst)
                v.wait_ge(tbl.h, tbl.n)
                for li, lv in enumerate(LV):
                    pr = ctx.W[0:64, 1024 + li * 128:1024 + (li + 1) * 128]
                    pk = ctx.W[0:64, 1920 + li * 128:1920 + (li + 1) * 128]
                    v.memset(pr, 0.0)
                    inst = v.memset(pk, 0.0)
                    ctx.fence(v, "f_dve", inst)
                    nb = 128 // (2 * lv)
                    prv = bass.AP(pr.tensor, pr.offset + lv,
                                  [[8192, 64], [2 * lv, nb], [1, lv]])
                    dpv = bass.AP(ctx.W, 2816, [[8192, 64], [0, nb], [1, lv]])
                    v.tensor_scalar(prv, dpv, 1.0, None, ALU.mult)
                    pkv = bass.AP(pk.tensor, pk.offset,
                                  [[8192, 64], [2 * lv, nb], [1, lv]])
                    rpv = bass.AP(ctx.W, 2945 + 128 - lv,
                                  [[8192, 64], [0, nb], [1, lv]])
                    inst = v.tensor_scalar(pkv, rpv, 1.0, None, ALU.mult)
                ctx.fence(v, "f_dve", inst)
                for g2 in range(8):
                    v.wait_ge(ktp.h, ktp0 + g2 + 1)
                    bank = ctx.pb[6 + g2 % 2]
                    base = 6144 if g2 < 4 else 7168
                    gg = g2 % 4
                    srcv = bass.AP(bank, 0, [[512, 128], [128, 4], [1, 64]])
                    dstv = bass.AP(ctx.C, base + gg * 4 * 64,
                                   [[8192, 128], [64, 4], [1, 64]])
                    kte.inc(v.tensor_copy(dstv, srcv))
                v.wait_ge(ktp.h, ktp0 + 9)
                kte.inc(v.tensor_copy(ctx.W[:, 3074:3138], ctx.pb[6][:, 448:512]))
                v.memset(ctx.W[0:64, 3266:3330], 0.0)
                inst = v.memset(ctx.W[0:64, 3330:3394], 0.0)
                ctx.fence(v, "f_dve", inst)
                for c in range(NCH):
                    kch = ctx.C[0:64, c * 128:(c + 1) * 128]
                    rch = ctx.C[0:64, 4096 + c * 128:4096 + (c + 1) * 128]
                    ktk = ctx.C[:, 6144 + c * 64:6144 + (c + 1) * 64]
                    v.tensor_scalar(ctx.W[0:64, 3394:3522], kch,
                                    rp(128, 129), None, ALU.mult)
                    kbc = bass.AP(ctx.C, c * 128, [[8192, 64], [0, 7], [1, 128]])
                    v.tensor_tensor(ctx.W[0:64, 3522:4418], kbc,
                                    ctx.W[0:64, 1920:2816], ALU.mult)
                    rbc = bass.AP(ctx.C, 4096 + c * 128,
                                  [[8192, 64], [0, 7], [1, 128]])
                    v.tensor_tensor(ctx.W[0:64, 4418:5314], rbc,
                                    ctx.W[0:64, 1024:1920], ALU.mult)
                    v.tensor_tensor(ctx.W[:, 5314:5378], ktk,
                                    ctx.W[:, 3074:3138], ALU.mult)
                    inst = v.tensor_tensor(ctx.W[0:64, 5378:5506], rch,
                                           dp(1, 129), ALU.mult)
                    fac.inc(inst)
                    v.wait_ge(lm.h, lm0 + c + 1)
                    v.tensor_tensor(ctx.S[:, 512:1024], ctx.pb[0][:, 0:512],
                                    ctx.W[:, 0:512], ALU.mult)
                    inst = v.tensor_tensor(ctx.S[:, 1024:1536],
                                           ctx.pb[1][:, 0:512],
                                           ctx.W[:, 512:1024], ALU.mult)
                    ctx.fence(v, "f_dve", inst)
                    inst = v.tensor_tensor(ctx.S[:, 512:1024],
                                           ctx.S[:, 512:1024],
                                           ctx.S[:, 1024:1536], ALU.add)
                    ctx.fence(v, "f_dve", inst)
                    inst = v.tensor_tensor(ctx.S[:, 512:768],
                                           ctx.S[:, 512:768],
                                           ctx.S[:, 768:1024], ALU.add)
                    ctx.fence(v, "f_dve", inst)
                    inst = v.tensor_tensor(ctx.W[:, 3138:3266],
                                           ctx.S[:, 512:640],
                                           ctx.S[:, 640:768], ALU.add)
                    la.inc(inst)
                    v.wait_ge(ym.h, ym0 + c + 1)
                    yev.inc(v.tensor_copy(
                        ctx.B[(h % 2) * 64:(h % 2) * 64 + 64,
                              (h // 2) * T + c * 128:(h // 2) * T + (c + 1) * 128],
                        ctx.pb[2][0:64, 0:128]))
                    v.wait_ge(smm.h, sm0 + c + 1)
                    inst = v.tensor_scalar(ctx.S[0:64, 1536:1600],
                                           ctx.W[0:64, 3266 + (c % 2) * 64:
                                                 3330 + (c % 2) * 64],
                                           dp(128, 129), None, ALU.mult)
                    ctx.fence(v, "f_dve", inst)
                    inst = v.tensor_tensor(ctx.W[0:64, 3266 + ((c + 1) % 2) * 64:
                                                 3330 + ((c + 1) % 2) * 64],
                                           ctx.pb[5][0:64, 0:64],
                                           ctx.S[0:64, 1536:1600], ALU.add)
                    sev.inc(inst)
                    v.wait_ge(sev.h, sev.n)


def sub_l1_attn_b(ctx, x_in, x_out, b, w):
    emit_in_ln_tp(ctx, x_in, b)
    emit_l1_attn_b_mix(ctx, b, w)
    proj = dict(
        kchunks=8, kpart=128, transpose=True,
        w_rows=lambda of: ap_off(w["Wo"], of * 128 * D, [[D, 128], [1, D]]),
        src=lambda k, t0, n: ctx.b_fm(k, t0, n),
        bias=None,
    )
    emit_out_phase(ctx, x_in, x_out, b, proj=proj)


SUBLAYERS["l1_attn_b"] = sub_l1_attn_b





CHAIN = [
    ("l0_attn_a", "L0_attn_a", ["w_qkv", "wo", "lambda_q1", "lambda_q2",
                                "lambda_k1", "lambda_k2", "norm_w"]),
    ("l0_ffn", "L0_ffn", ["gate_w", "gate_b", "sw1", "sb1", "sw2", "sb2",
                          "rw1", "rb1", "rw2", "rb2"]),
    ("l0_attn_b", "L0_attn_b", ["w_qkv", "w_proj", "b_proj"]),
    ("l0_ffn", "L0_ffn", ["gate_w", "gate_b", "sw1", "sb1", "sw2", "sb2",
                          "rw1", "rb1", "rw2", "rb2"]),
    ("l1_attn_a", "L1_attn_a", ["Wq", "bq", "Wk", "bk", "Wv", "bv", "Wo",
                                "bo", "q_phase", "k_phase", "temperature"]),
    ("l1_ffn", "L1_ffn", ["Win", "bin_", "W1", "b1", "W2", "b2", "Wout",
                          "bout"]),
    ("l1_attn_b", "L1_attn_b", ["Wk", "Wv", "Wr", "Wo", "time_decay",
                                "time_first"]),
    ("l1_ffn", "L1_ffn", ["Win", "bin_", "W1", "b1", "W2", "b2", "Wout",
                          "bout"]),
]



def build_fused(wshapes_by_stage):
    nc = bass.Bass("TRN2", target_bir_lowering=False)
    x_in = nc.dram_tensor("x", [2, T, D], F32, kind="ExternalInput").ap()
    y_out = nc.dram_tensor("y", [2, T, D], F32, kind="ExternalOutput").ap()
    xa = nc.dram_tensor("xa", [2, T, D], F32, kind="Internal").ap()
    xb = nc.dram_tensor("xb", [2, T, D], F32, kind="Internal").ap()
    wsc = nc.dram_tensor("wsc", [192 * 128, 2048], BF, kind="Internal").ap()
    waps = []
    for si, (which, prefix, wnames) in enumerate(CHAIN):
        waps.append({nm: nc.dram_tensor(f"s{si}_{nm}",
                                        list(wshapes_by_stage[si][nm]), F32,
                                        kind="ExternalInput").ap()
                     for nm in wnames})
    with contextlib.ExitStack() as es:
        ctx = Ctx(nc, es)
        ctx.wsc = wsc
        ctx.wcast_w = waps[1]
        ctx.wcast_cursor = 0
        emit_prelude(ctx)
        cur = x_in
        ping = [xa, xb]
        for si, (which, prefix, wnames) in enumerate(CHAIN):
            nxt = y_out if si == len(CHAIN) - 1 else ping[si % 2]
            for b in range(2):
                SUBLAYERS[which](ctx, cur, nxt, b, waps[si])
            cur = nxt
    return nc


_fused_cache = {}


def kernel(**inputs):
    x = np.ascontiguousarray(inputs["L0_attn_a_x"], dtype=np.float32)
    shards = [x[2 * c:2 * c + 2] for c in range(NCORES)]
    wstage = []
    flat = {}
    for si, (which, prefix, wnames) in enumerate(CHAIN):
        w = {nm: np.ascontiguousarray(inputs[prefix + "_" + nm],
                                      dtype=np.float32) for nm in wnames}
        wstage.append({nm: tuple(w[nm].shape) for nm in wnames})
        for nm in wnames:
            flat[f"s{si}_{nm}"] = w[nm]
    key = "fused"
    if key not in _fused_cache:
        _fused_cache[key] = build_fused(wstage)
    nc = _fused_cache[key]
    in_maps = [{"x": shards[c], **flat} for c in range(NCORES)]
    res = run_bass_kernel_spmd(nc, in_maps, core_ids=list(range(NCORES)))
    return np.concatenate([np.asarray(res.results[c]["y"])
                           for c in range(NCORES)], axis=0)
